# version 15
# baseline (speedup 1.0000x reference)
import numpy as np
import concourse.bass as bass
import concourse.mybir as mybir
from concourse.bass_utils import run_bass_kernel_spmd
from contextlib import ExitStack

F32 = mybir.dt.float32
BF16 = mybir.dt.bfloat16
U8 = mybir.dt.uint8
AF = mybir.ActivationFunctionType
ALU = mybir.AluOpType
AX = mybir.AxisListType

S = 2048
DM = 1024
ENGINES = ("tensor", "vector", "scalar", "gpsimd", "sync")


DMA_WIN = 8


class Prog:
    def __init__(self, nc):
        self.nc = nc
        self.ops = []
        self.res = {}
        self.fence_deps = set()
        self.dmas = {}

    def op(self, engine, fn, reads=(), writes=(), dma=False, tag=""):
        deps = set(self.fence_deps)
        for r in reads:
            e = self.res.get(r)
            if e and e["w"] is not None:
                deps.add(e["w"])
            if e and r.startswith("PS"):
                for ri in e["r"]:
                    if self.ops[ri]["engine"] != engine:
                        deps.add(ri)
        for w in writes:
            e = self.res.get(w)
            if e:
                if e["w"] is not None:
                    deps.add(e["w"])
                deps.update(e["r"])
        i = len(self.ops)
        if dma:
            ql = self.dmas.setdefault(engine, [])
            if len(ql) >= DMA_WIN:
                deps.add(ql[-DMA_WIN])
            ql.append(i)
        pruned = set()
        for d in deps:
            dop = self.ops[d]
            if (engine == "tensor" and dop["engine"] == "tensor"
                    and not dma and not dop["dma"]):
                continue
            pruned.add(d)
        self.ops.append(dict(engine=engine, fn=fn, dma=dma, deps=pruned, tag=tag))
        for r in reads:
            self.res.setdefault(r, {"w": None, "r": []})["r"].append(i)
        for w in writes:
            self.res[w] = {"w": i, "r": []}
        return i

    def fence(self):
        tails = set()
        for e in self.res.values():
            if e["w"] is not None:
                tails.add(e["w"])
            tails.update(e["r"])
        self.fence_deps = tails

    def finalize(self, stack, last_engine="sync"):
        nc = self.nc
        n = len(self.ops)
        needed = set()
        for o in self.ops:
            needed.update(o["deps"])
        cnt = {}
        tok = [None] * n
        qidx = {}
        for i, o in enumerate(self.ops):
            if o["dma"]:
                j = qidx.get(o["engine"], 0)
                qidx[o["engine"]] = j + 1
                k = ("q", o["engine"], j % DMA_WIN)
                cnt[k] = cnt.get(k, 0) + 16
                tok[i] = (k, cnt[k])
            elif i in needed:
                k = ("e", o["engine"])
                cnt[k] = cnt.get(k, 0) + 1
                tok[i] = (k, cnt[k])
        issue_clock = [None] * n
        stream_clock = {e: {} for e in ENGINES}
        waits = [None] * n
        for i, o in enumerate(self.ops):
            sc = stream_clock[o["engine"]]
            need = {}
            for d in sorted(o["deps"]):
                k, v = tok[d]
                if sc.get(k, 0) < v and need.get(k, 0) < v:
                    need[k] = v
            waits[i] = need
            for d in sorted(o["deps"]):
                k, v = tok[d]
                if sc.get(k, 0) < v:
                    sc[k] = v
                ic = issue_clock[d]
                if ic:
                    for kk, vv in ic.items():
                        if sc.get(kk, 0) < vv:
                            sc[kk] = vv
            issue_clock[i] = dict(sc)
        sems = {k: stack.enter_context(nc.semaphore("_".join(str(x) for x in k)))
                for k in cnt}
        block = stack.enter_context(nc.Block())
        self.n_ops = n

        def emit_for(engine):
            def run(eng):
                for i, o in enumerate(self.ops):
                    if o["engine"] != engine:
                        continue
                    for k, v in sorted(waits[i].items()):
                        eng.wait_ge(sems[k], v)
                    instr = o["fn"](eng)
                    if tok[i] is not None:
                        k, v = tok[i]
                        instr.then_inc(sems[k], 16 if o["dma"] else 1)
                if engine == last_engine:
                    for k, v in sorted(cnt.items()):
                        eng.wait_ge(sems[k], v)
            return run

        for e in ENGINES:
            getattr(block, e)(emit_for(e))

    def mm(self, out, lhsT, rhs, start, stop, reads, writes, tag=""):
        def fn(eng, out=out, lhsT=lhsT, rhs=rhs, start=start, stop=stop):
            return eng.matmul(out, lhsT, rhs, start=start, stop=stop)
        return self.op("tensor", fn, reads, writes, tag=tag)

    def tt(self, out, a, b, alu, reads, writes, engine="vector", tag=""):
        def fn(eng, out=out, a=a, b=b, alu=alu):
            return eng.tensor_tensor(out, a, b, op=alu)
        return self.op(engine, fn, reads, writes, tag=tag)

    def ts(self, out, a, s1, op0, reads, writes, s2=None, op1=None, engine="vector", tag=""):
        def fn(eng, out=out, a=a, s1=s1, s2=s2, op0=op0, op1=op1):
            if s2 is None:
                return eng.tensor_scalar(out, a, s1, None, op0=op0)
            return eng.tensor_scalar(out, a, s1, s2, op0=op0, op1=op1)
        return self.op(engine, fn, reads, writes, tag=tag)

    def act(self, out, in_, func, reads, writes, bias=None, scale=None, accum=None, tag=""):
        def fn(eng, out=out, in_=in_, func=func, bias=bias, scale=scale, accum=accum):
            kw = {}
            if bias is not None:
                kw["bias"] = bias
            if scale is not None:
                kw["scale"] = scale
            if accum is not None:
                kw["accum_out"] = accum
            return eng.activation(out, in_, func, **kw)
        return self.op("scalar", fn, reads, writes, tag=tag)

    def copy(self, out, in_, reads, writes, engine="vector", tag=""):
        if engine == "scalar":
            def fn(eng, out=out, in_=in_):
                return eng.copy(out, in_)
        else:
            def fn(eng, out=out, in_=in_):
                return eng.tensor_copy(out, in_)
        return self.op(engine, fn, reads, writes, tag=tag)

    def red(self, out, in_, alu, reads, writes, absv=False, tag=""):
        def fn(eng, out=out, in_=in_, alu=alu, absv=absv):
            return eng.tensor_reduce(out, in_, axis=AX.X, op=alu,
                                     apply_absolute_value=absv or None)
        return self.op("vector", fn, reads, writes, tag=tag)

    def recip(self, out, in_, reads, writes, tag=""):
        def fn(eng, out=out, in_=in_):
            return eng.reciprocal(out, in_)
        return self.op("vector", fn, reads, writes, tag=tag)

    def memset(self, ap, val, writes, engine="vector", tag=""):
        def fn(eng, ap=ap, val=val):
            return eng.memset(ap, val)
        return self.op(engine, fn, (), writes, tag=tag)

    def dma(self, queue, out, in_, reads, writes, tag=""):
        def fn(eng, out=out, in_=in_):
            return eng.dma_start(out=out, in_=in_)
        return self.op(queue, fn, reads, writes, dma=True, tag=tag)


def make_trig():
    d = np.arange(128) % 64
    i = (d % 32).astype(np.float64)
    t = np.arange(S, dtype=np.float64)
    inv_freq = 1.0 / (10000.0 ** ((2 * i) / 64.0))
    freqs = inv_freq[:, None] * t[None, :]
    base = ((2 * i) + 0.4 * 64) / (1.4 * 64)
    power = (t - S // 2) / 256.0
    scale = base[:, None] ** power[None, :]
    sign = np.where(d < 32, -1.0, 1.0)[:, None]
    return (np.cos(freqs) * scale, np.sin(freqs) * scale * sign,
            np.cos(freqs) / scale, np.sin(freqs) / scale * sign)


def prepack(inputs):
    f32 = np.float32
    x = np.asarray(inputs["L0_attn_x"], f32)
    Wq = np.asarray(inputs["L0_attn_Wq"], f32)
    Wk = np.asarray(inputs["L0_attn_Wk"], f32)
    Wv = np.asarray(inputs["L0_attn_Wv"], f32)
    Wo = np.asarray(inputs["L0_attn_Wo"], f32)
    bo = np.asarray(inputs["L0_attn_bo"], f32)
    rel = np.asarray(inputs["L0_attn_rel_emb"], f32)
    sh = {}
    sh["wqT"] = np.ascontiguousarray(
        Wq.reshape(8, 128, 4, 2, 128).transpose(2, 1, 0, 3, 4))
    sh["wkT"] = np.ascontiguousarray(
        Wk.reshape(8, 128, 4, 2, 128).transpose(2, 1, 0, 3, 4))
    sh["wvP"] = np.ascontiguousarray(
        Wv.reshape(8, 128, 4, 256).transpose(2, 1, 0, 3))
    sh["woP"] = np.ascontiguousarray(Wo.reshape(16, 64, 8, 128).transpose(1, 0, 2, 3))
    sh["boH"] = bo.reshape(2, 512)
    sh["relH"] = rel.reshape(1, 17)
    p = np.arange(128)[:, None]
    c = np.arange(144)[None, :]
    sh["GdH"] = rel[np.clip(c - p, 0, 16)].astype(f32)
    sh["U01dH"] = (c[:, 0:128] >= p).astype(f32)
    pp = np.arange(64)[:, None]
    cc = np.arange(16)[None, :]
    sh["GpH"] = rel[np.clip(cc + 64 - pp, 0, 16)].astype(f32)
    sh["r16H"] = np.broadcast_to(rel[16:17], (128, 1)).astype(f32).copy()
    c2 = np.arange(128)[None, :]
    sh["UcH"] = (c2 >= p).astype(f32)
    sh["UwH"] = (c2 <= p).astype(f32)
    hs = np.zeros((128, 2, 4), f32)
    for pr in range(128):
        for r in range(2):
            hs[pr, r, 2 * r + pr // 64] = 1.0
    sh["HSH"] = hs
    sel4 = np.zeros((4, 128), f32)
    sel4[0, 0:64] = 1; sel4[1, 0:64] = 1; sel4[2, 64:128] = 1; sel4[3, 64:128] = 1
    sh["SEL4H"] = sel4
    z4 = np.zeros((4, 128), f32)
    sh["SEL8AH"] = np.vstack([sel4, z4])
    sh["SEL8BH"] = np.vstack([z4, sel4])
    csq, ssq, csk, ssk = make_trig()
    sh["CSqH"] = csq.astype(f32); sh["SSqH"] = ssq.astype(f32)
    sh["CSkH"] = csk.astype(f32); sh["SSkH"] = ssk.astype(f32)

    wqkv = np.asarray(inputs["L0_ffn_wqkv"], f32)
    wol = np.asarray(inputs["L0_ffn_wo"], f32)
    w1 = np.asarray(inputs["L0_ffn_w1"], f32)
    w2 = np.asarray(inputs["L0_ffn_w2"], f32)
    b1 = np.asarray(inputs["L0_ffn_b1"], f32)
    b2 = np.asarray(inputs["L0_ffn_b2"], f32)
    lg = np.asarray(inputs["L0_ffn_ln_g"], f32)
    lb = np.asarray(inputs["L0_ffn_ln_b"], f32)
    qs = np.asarray(inputs["L0_ffn_q_scale"], f32)
    ks = np.asarray(inputs["L0_ffn_k_scale"], f32)
    perm = ((np.arange(128) % 64 + 32) % 64) + 64 * (np.arange(128) // 64)
    for l in (0, 1):
        wq3 = wqkv[l].reshape(8, 128, 3, 2, 256)
        sh[f"wqkvP{l}"] = np.ascontiguousarray(wq3.transpose(2, 3, 1, 0, 4))
        sh[f"wolP{l}"] = np.ascontiguousarray(wol[l].reshape(8, 64, 8, 128).transpose(1, 0, 2, 3))
        w1p = np.zeros((22, 128, 8, 2, 128), f32)
        wa_ = w1[l][:, 0:2730]
        wg_ = w1[l][:, 2730:5460]
        for k in range(22):
            awk = min(128, 2730 - 128 * k)
            w1p[k, :, :, 0, 0:awk] = wa_[:, 128 * k:128 * k + awk].reshape(8, 128, awk).transpose(1, 0, 2)
            w1p[k, :, :, 1, 0:awk] = wg_[:, 128 * k:128 * k + awk].reshape(8, 128, awk).transpose(1, 0, 2)
        sh[f"w1P{l}"] = np.ascontiguousarray(w1p)
        w2p = np.zeros((22 * 128, 1024), f32)
        w2p[0:2730] = w2[l]
        sh[f"w2P{l}"] = np.ascontiguousarray(
            w2p.reshape(22, 128, 8, 128).transpose(2, 1, 0, 3))
        b1a = np.zeros((2, 1408), f32)
        b1a.reshape(-1)[0:2730] = b1[l, 0:2730]
        sh[f"b1aH{l}"] = b1a
        b1g = np.zeros((128, 22), f32)
        for k in range(22):
            seg = b1[l, 2730 + 128 * k: 2730 + min(128 * (k + 1), 2730)]
            b1g[0:len(seg), k] = seg
        sh[f"b1gH{l}"] = b1g
        sh[f"b2H{l}"] = b2[l].reshape(2, 512)
        sh[f"lgH{l}"] = np.ascontiguousarray(lg[l].reshape(8, 128).T)
        sh[f"lbH{l}"] = np.ascontiguousarray(lb[l].reshape(8, 128).T)
        sh[f"qsH{l}"] = np.tile(qs[l], 2).reshape(128, 1).copy()
        sh[f"qspH{l}"] = np.tile(qs[l], 2)[perm].reshape(128, 1).copy()
        sh[f"ksH{l}"] = np.tile(ks[l], 2).reshape(128, 1).copy()
        sh[f"kspH{l}"] = np.tile(ks[l], 2)[perm].reshape(128, 1).copy()
    cores = []
    for b in range(8):
        d = dict(sh)
        d["xT"] = np.ascontiguousarray(x[b].T)
        cores.append(d)
    return cores


class G:
    pass


def psr(i, half):
    return f"PS{i}{'ab'[half]}"


def build_program(cfg):
    nc = bass.Bass("TRN2", target_bir_lowering=False)
    g = G()
    g.cfg = cfg

    def dram(name, shape, dtype=F32, kind="ExternalInput"):
        setattr(g, name, nc.dram_tensor(name, shape, dtype, kind=kind).ap())

    dram("xT", [DM, S])
    dram("wqT", [4, 128, 8, 2, 128]); dram("wkT", [4, 128, 8, 2, 128])
    dram("wvP", [4, 128, 8, 256]); dram("woP", [64, 16, 8, 128])
    dram("boH", [2, 512]); dram("relH", [1, 17])
    dram("GdH", [128, 144]); dram("U01dH", [128, 128]); dram("GpH", [64, 16])
    dram("r16H", [128, 1]); dram("UcH", [128, 128]); dram("UwH", [128, 128])
    dram("HSH", [128, 2, 4]); dram("SEL4H", [4, 128])
    dram("SEL8AH", [8, 128]); dram("SEL8BH", [8, 128])
    dram("CSqH", [128, S]); dram("SSqH", [128, S])
    dram("CSkH", [128, S]); dram("SSkH", [128, S])
    for l in (0, 1):
        dram(f"wqkvP{l}", [3, 2, 128, 8, 256]); dram(f"wolP{l}", [64, 8, 8, 128])
        dram(f"w1P{l}", [22, 128, 8, 2, 128]); dram(f"w2P{l}", [8, 128, 22, 128])
        dram(f"b1aH{l}", [2, 1408]); dram(f"b1gH{l}", [128, 22]); dram(f"b2H{l}", [2, 512])
        dram(f"lgH{l}", [128, 8]); dram(f"lbH{l}", [128, 8])
        dram(f"qsH{l}", [128, 1]); dram(f"qspH{l}", [128, 1])
        dram(f"ksH{l}", [128, 1]); dram(f"kspH{l}", [128, 1])
    dram("x1spill", [128, 8 * S], kind="Internal")
    dram("outT", [DM, S], kind="ExternalOutput")
    g.dump_stage = cfg.get("dump_stage")
    if g.dump_stage:
        dram("dumpT", [128, 8 * (S + 1)], kind="ExternalOutput")
        dram("dumpHF", [128, 8 * (S + 1)], BF16, kind="ExternalOutput")
    if cfg.get("dump_qk"):
        dram("dumpQ", [128, 2, S], BF16, kind="ExternalOutput")
        dram("dumpK", [128, 2, S], BF16, kind="ExternalOutput")
        dram("dumpV", [128, 16, 260], BF16, kind="ExternalOutput")

    st = ExitStack()
    en = st.enter_context

    def sb(name, shape, dtype):
        setattr(g, name, en(nc.sbuf_tensor(name, shape, dtype)))

    sb("XT", [128, 8, S + 1], F32)
    sb("HF", [128, 8, S + 1], BF16)
    sb("SQT", [128, S], BF16)
    sb("RS", [128, 4, 512], F32)
    sb("SMALL", [128, 16, 32], F32)
    sb("SMB", [128, 16, 128], BF16)
    sb("TBL", [128, 208], F32)
    sb("WBQ", [128, 6144], BF16)
    sb("WB3", [64, 4, 8, 128], BF16)
    sb("POOL", [128, 57472], U8)
    g.ps = [en(nc.psum_tensor(f"PS{i}", [128, 1024], F32)) for i in range(4)]
    g.bank = lambda i, half: g.ps[i][:, 512 * half:512 * half + 512]
    g.RSF = g.RS.ap().rearrange("p a b -> p (a b)")

    p = Prog(nc)
    g.p = p
    build_main(nc, p, g, cfg)
    p.finalize(st)
    st.close()
    return nc


WR = {"A": (0, 1536), "B": (1536, 3072), "C": (3072, 4608), "D": (4608, 6144)}


def wbq(g, lo, hi):
    return g.WBQ[:, lo:hi]


def wres(lo, hi):
    out = []
    for nm, (a, b) in WR.items():
        if lo < b and hi > a:
            out.append("WBQ" + nm)
    return out


def pool_attn(g, local):
    v = G()
    P = g.POOL
    def bf(off, n):
        return P[:, off:off + 2 * n].bitcast(BF16)
    v.QB = bf(0, 4096).rearrange("p (m t) -> p m t", m=2)
    v.KB = bf(8192, 4096).rearrange("p (m t) -> p m t", m=2)
    if local:
        v.PERM = bf(16384, 2048)
        v.VB = bf(20480, 4160).rearrange("p (k j) -> p k j", k=16)
        v.CTX = bf(28800, 2048).rearrange("p (h t) -> p h t", h=4)[0:64]
        v.AT = bf(32896, 2048)
        v.AT2 = bf(36992, 2048)
        v.CS1 = P[:, 41088:41088 + 8192].bitcast(F32)
        v.CS2 = P[:, 49280:49280 + 8192].bitcast(F32)
        v.RHL = v.PERM
        v.RHL_res = "PERM"
    else:
        v.VB = bf(16384, 4160).rearrange("p (k j) -> p k j", k=16)
        v.CTX = bf(24704, 2048).rearrange("p (h t) -> p h t", h=4)[0:64]
        v.AT = bf(28800, 2048)
        v.AT2 = bf(32896, 2048)
        v.RHL = bf(36992, 2048)
        v.RHL_res = "DRHL"
    return v


def pool_ffn(g):
    v = G()
    v.MC = g.POOL[:, 0:45056].bitcast(BF16).rearrange("p (k t) -> p k t", k=22)
    v.ST = g.POOL[:, 45056:45056 + 8192].bitcast(BF16)
    return v


def ld_tables(p, g):
    p.memset(g.SMB[:, 0, :], 1.0, writes=["SMB0"], tag="ones")
    p.memset(g.SMB[:, 10:14, :], 1.0, writes=["SMBw"], tag="ones512")
    p.dma("sync", g.TBL[:, 0:144], g.GdH, (), ["TBL0"], tag="Gd")
    p.dma("sync", g.TBL[0:64, 144:160], g.GpH, (), ["TBL1"], tag="Gp")
    p.dma("sync", g.TBL[0:1, 160:177], g.relH, (), ["TBL2"], tag="rel")
    p.dma("sync", g.TBL[:, 177:178], g.r16H, (), ["TBL3"], tag="r16")
    p.dma("gpsimd", g.SMB[:, 1, :], g.UcH, (), ["SMB1"], tag="Uc")
    p.dma("gpsimd", g.SMB[:, 2, :], g.UwH, (), ["SMB2"], tag="Uw")
    p.dma("gpsimd", g.SMB[:, 3, :], g.U01dH, (), ["SMB3"], tag="U01d")
    p.dma("gpsimd", g.SMB[:, 4, 0:8].rearrange("p (a b) -> p a b", a=2), g.HSH,
          (), ["SMB4"], tag="HS")
    p.ts(g.SMALL[:, 0, 0:1], g.TBL[:, 177:178], 0.125, ALU.mult,
         reads=["TBL3"], writes=["SMa0"], tag="r16s")
    p.red(g.SMALL[0:1, 1, 0:1], g.TBL[0:1, 160:177], ALU.max, reads=["TBL2"],
          writes=["SMa1"], absv=True, tag="relmax")
    p.copy(g.SMB[0:1, 6, 0:1], g.SMALL[0:1, 1, 0:1], reads=["SMa1"], writes=["SMB6"],
           tag="relmaxb")
    p.mm(g.bank(3, 0)[0:4, 0:1], g.SMB[0:1, 0, 0:4], g.SMB[0:1, 6, 0:1], True, True,
         reads=["SMB0", "SMB6"], writes=[psr(3, 0)], tag="relmm")
    p.copy(g.SMALL[0:4, 1, 1:2], g.bank(3, 0)[0:4, 0:1], reads=[psr(3, 0)],
           writes=["SMa1b"], tag="relmax4")


def stage_bias_hilo(p, g, srcH, dst_bf, dst_res, nseg, tag):
    rsb = g.RS.ap().rearrange("p a b -> p (a b)").bitcast(BF16)
    rsf = g.RSF
    RSALL = ["RS0", "RS1", "RS2", "RS3"]
    for s_ in range((nseg + 511) // 512):
        a, b = 512 * s_, min(512 * s_ + 512, nseg)
        w = b - a
        p.dma("sync", rsf[0:2, 0:w], srcH[0:2, a:b], RSALL, RSALL, tag=tag + "ld")
        p.copy(rsb[0:2, 1024:1024 + w], rsf[0:2, 0:w], reads=RSALL, writes=RSALL,
               tag=tag + "hi")
        p.copy(g.bank(3, 0)[0:2, 0:w], rsb[0:2, 1024:1024 + w], reads=RSALL,
               writes=[psr(3, 0)], tag=tag + "hf")
        p.tt(rsf[0:2, 0:w], rsf[0:2, 0:w], g.bank(3, 0)[0:2, 0:w], ALU.subtract,
             reads=RSALL + [psr(3, 0)], writes=RSALL, tag=tag + "lo")
        p.copy(rsb[0:2, 1536:1536 + w], rsf[0:2, 0:w], reads=RSALL, writes=RSALL,
               tag=tag + "lb")
        p.dma("sync", dst_bf[0:1, 0:2 * nseg].rearrange("p (r c) -> p r c", r=2)[:, :, a:b],
              rsb[0:2, 1024:1024 + w], RSALL, [dst_res], tag=tag + "chi")
        p.dma("sync", dst_bf[1:2, 0:2 * nseg].rearrange("p (r c) -> p r c", r=2)[:, :, a:b],
              rsb[0:2, 1536:1536 + w], RSALL, [dst_res], tag=tag + "clo")
    return dst_bf[0:2, 0:2 * nseg]


def bias_bounds(p, g, QB, KB, mode, tag):
    for which, buf, res in (("q", QB, "QB"), ("k", KB, "KB")):
        for m in range(2):
            p.tt(g.SQT[:, :], buf[:, m, :], buf[:, m, :], ALU.mult,
                 reads=[res], writes=["SQT"], tag=tag + "sq")
            for t4 in range(4):
                i, half = 2 + t4 // 2, t4 % 2
                p.mm(g.bank(i, half)[0:4, :],
                     g.SMB[:, 4, 0:8].rearrange("p (a b) -> p a b", a=2)[:, m, :],
                     g.SQT[:, 512 * t4:512 * t4 + 512], m == 0, m == 1,
                     reads=["SQT", "SMB4"], writes=[psr(i, half)], tag=tag + "n2")
        for t4 in range(4):
            i, half = 2 + t4 // 2, t4 % 2
            col = t4 if which == "q" else 4 + t4
            p.red(g.SMALL[0:4, 2, col:col + 1], g.bank(i, half)[0:4, :], ALU.max,
                  reads=[psr(i, half)], writes=[f"n2m{col}"], tag=tag + "n2m")
    p.red(g.SMALL[0:4, 2, 8:9], g.SMALL[0:4, 2, 4:8], ALU.max,
          reads=[f"n2m{c}" for c in (4, 5, 6, 7)], writes=["n2kg"], tag=tag + "kg")
    p.ts(g.SMALL[0:4, 3, 0:4], g.SMALL[0:4, 2, 0:4], g.SMALL[0:4, 2, 8:9], ALU.mult,
         reads=[f"n2m{c}" for c in (0, 1, 2, 3)] + ["n2kg"], writes=["bb"],
         tag=tag + "bb")
    p.act(g.SMALL[0:4, 3, 4:8], g.SMALL[0:4, 3, 0:4], AF.Ln, reads=["bb"],
          writes=["bb1"], tag=tag + "ln")
    p.act(g.SMALL[0:4, 3, 8:12], g.SMALL[0:4, 3, 4:8], AF.Exp, scale=0.5,
          reads=["bb1"], writes=["bb2"], tag=tag + "ex")
    if mode == "dense":
        p.ts(g.SMALL[0:4, 3, 12:16], g.SMALL[0:4, 3, 8:12], g.SMALL[0:4, 1, 1:2],
             ALU.mult, reads=["bb2", "SMa1b"], writes=["negb"], s2=-0.125,
             op1=ALU.mult, tag=tag + "nb")
    else:
        p.ts(g.SMALL[0:4, 3, 12:16], g.SMALL[0:4, 3, 8:12], -8.0, ALU.mult,
             reads=["bb2"], writes=["negb"], tag=tag + "nb")
    p.copy(g.SMB[0:4, 7, 0:4], g.SMALL[0:4, 3, 12:16], reads=["negb"],
           writes=["SMB7"], tag=tag + "nbb")
    p.dma("sync", g.SMB[0:1, 8, 0:16].rearrange("p (a b) -> p a b", a=4),
          g.SMB[0:4, 7, 0:4], ["SMB7"], ["SMB8"], tag=tag + "nbrow")
    p.mm(g.bank(3, 0)[:, 0:16], g.SMB[0:1, 0, :], g.SMB[0:1, 8, 0:16], True, True,
         reads=["SMB0", "SMB8"], writes=[psr(3, 0)], tag=tag + "bm")
    p.copy(g.SMALL[:, 4, 0:16], g.bank(3, 0)[:, 0:16], reads=[psr(3, 0)],
           writes=["BIAS"], tag=tag + "bev")


def attn_group(p, g, v, lay):
    nm = lay["tag"]
    dense = lay["dense"]
    nq0t = g.cfg.get("nq0t", 4)
    nh = g.cfg.get("nheads", 4)
    do_wo = g.cfg.get("do_wo", True)
    CTXB = [(1, 0), (1, 1), (3, 0)]
    queue = []

    def deadline(item):
        kind, gq, _ = item
        if kind == "norm":
            return gq + 2
        return (gq + 1) * nh + 2

    def do_flush(cur):
        while queue:
            if cur is not None and cur < min(deadline(it) for it in queue):
                break
            queue.pop(0)[2]()

    for q0t in range(nq0t):
        q0 = 512 * q0t
        chunks = lay["chunks"](q0t)
        nch = len(chunks)
        for hh in range(nh):
            m, base = hh // 2, 64 * (hh % 2)
            biasap = lay["bias_ap"](g, hh, q0t)
            gidx = q0t * nh + hh
            ci_bank = CTXB[gidx % 3]
            ctx_ps = g.bank(*ci_bank)[0:65, :]

            def emit_S(ci):
                kc, cmin, cmax, kind, tcol = chunks[ci]
                it = 0 if (ci // 2) % 2 == 0 else 2
                Sps = g.ps[it][:, 512 * (ci % 2):512 * (ci % 2) + 512]
                lhs = v.KB[base:base + 64, m, 128 * kc:128 * kc + 128]
                rhs = v.QB[base:base + 64, m, q0 + cmin:q0 + cmax]
                p.mm(Sps[:, cmin:cmax], lhs, rhs, True, True,
                     reads=["QB", "KB"], writes=[psr(it, ci % 2)], tag=nm + "S")

            def fixes(ci, aseg, ares, it):
                kc, cmin, cmax, kind, tcol = chunks[ci]
                Sps = g.ps[it][:, 512 * (ci % 2):512 * (ci % 2) + 512]
                sres = psr(it, ci % 2)
                if kind == "diag":
                    w = min(144, cmax - tcol)
                    p.tt(g.RS[:, 0, 0:w], Sps[:, tcol:tcol + w], g.TBL[:, 0:w],
                         ALU.mult, reads=[sres, "TBL0"], writes=["RS0"],
                         tag=nm + "bG")
                    p.act(aseg[:, tcol:tcol + w], g.RS[:, 0, 0:w], AF.Exp,
                          bias=biasap, scale=0.125, reads=["RS0", lay["bias_res"]],
                          writes=[ares], tag=nm + "bE")
                    wu = min(w, 128)
                    p.tt(aseg[:, tcol:tcol + wu], aseg[:, tcol:tcol + wu],
                         g.SMB[:, 3, 0:wu], ALU.mult, reads=[ares, "SMB3"],
                         writes=[ares], tag=nm + "bU")
                elif kind == "corner":
                    p.tt(g.RS[64:128, 0, 0:16], Sps[64:128, 0:16],
                         g.TBL[0:64, 144:160], ALU.mult,
                         reads=[sres, "TBL1"], writes=["RS0"], tag=nm + "cG")
                    p.act(aseg[64:128, 0:16], g.RS[64:128, 0, 0:16], AF.Exp,
                          bias=biasap[64:128, :], scale=0.125,
                          reads=["RS0", lay["bias_res"]], writes=[ares],
                          tag=nm + "cE")
                elif kind == "tri_causal":
                    p.tt(aseg[:, tcol:tcol + 128], aseg[:, tcol:tcol + 128],
                         g.SMB[:, 1, :], ALU.mult, reads=[ares, "SMB1"],
                         writes=[ares], tag=nm + "tc")
                elif kind == "tri_window":
                    p.tt(aseg[:, tcol:tcol + 128], aseg[:, tcol:tcol + 128],
                         g.SMB[:, 2, :], ALU.mult, reads=[ares, "SMB2"],
                         writes=[ares], tag=nm + "tw")

            def emit_av(bb):
                a0, a1 = 2 * bb, 2 * bb + 1
                ab2 = 1024 * (bb % 2)
                for ci in ([a0] if a1 >= nch else [a0, a1]):
                    kc, cmin, cmax, kind, tcol = chunks[ci]
                    p.mm(ctx_ps[:, cmin:cmax], v.VB[:, kc, 65 * hh:65 * hh + 65],
                         v.AT[:, ab2 + 512 * (ci % 2) + cmin:ab2 + 512 * (ci % 2) + cmax],
                         ci == 0, ci == nch - 1,
                         reads=[f"ATs{2 * (bb % 2) + (ci % 2)}", "VB"],
                         writes=[psr(*ci_bank)], tag=nm + "AV")

            npair = (nch + 1) // 2
            emit_S(0)
            if nch > 1:
                emit_S(1)
            for b in range(npair):
                c0 = 2 * b
                c1 = 2 * b + 1 if 2 * b + 1 < nch else None
                it = 0 if b % 2 == 0 else 2
                if 2 * b + 2 < nch:
                    emit_S(2 * b + 2)
                    if 2 * b + 3 < nch:
                        emit_S(2 * b + 3)
                if b == 0:
                    do_flush(gidx)
                cmin0 = chunks[c0][1]
                hi = 1024 if c1 is not None else 512
                abase = 1024 * (b % 2)
                ares0, ares1 = f"ATs{2 * (b % 2)}", f"ATs{2 * (b % 2) + 1}"
                p.act(v.AT[:, abase + cmin0:abase + hi],
                      g.ps[it][:, cmin0:hi], AF.Exp,
                      bias=biasap, scale=lay["scale"](g),
                      reads=[psr(it, 0), psr(it, 1), lay["bias_res"]] + lay["scale_res"],
                      writes=[ares0, ares1], tag=nm + "exp")
                fixes(c0, v.AT[:, abase:abase + 512], ares0, it)
                if c1 is not None:
                    fixes(c1, v.AT[:, abase + 512:abase + 1024], ares1, it)
                if b > 0:
                    emit_av(b - 1)
            emit_av(npair - 1)
            if g.cfg.get("skip_norm"):
                p.copy(v.CTX[0:64, hh, :], ctx_ps[0:64, :], reads=[psr(*ci_bank)],
                       writes=[f"CTXn{hh}"], tag=nm + "cp")
                continue
            p.copy(g.RS[64:65, 1, 0:512], ctx_ps[64:65, :], reads=[psr(*ci_bank)],
                   writes=["sumrow"], tag=nm + "sum")
            r = gidx % 2
            cS = 1024 * r
            p.ts(g.RS[64:65, 2, 0:512], g.RS[64:65, 1, 0:512], 1e-30, ALU.max,
                 reads=["sumrow"], writes=["RS2"], tag=nm + "smx")
            p.act(g.RS[64:65, 2, 0:512], g.RS[64:65, 2, 0:512], AF.Ln,
                  reads=["RS2"], writes=["RS2"], tag=nm + "rln")
            p.act(g.RS[64:65, 3, 0:512], g.RS[64:65, 2, 0:512], AF.Exp, scale=-1.0,
                  reads=["RS2"], writes=["RS3"], tag=nm + "rcp")
            p.copy(g.SQT[64:65, cS:cS + 512], g.RS[64:65, 3, 0:512],
                   reads=["RS3"], writes=[f"SQTr{r}"], tag=nm + "rhi")
            p.copy(g.RS[64:65, 2, 0:512], g.SQT[64:65, cS:cS + 512],
                   reads=[f"SQTr{r}"], writes=["RS2"], tag=nm + "rhf")
            p.tt(g.RS[64:65, 3, 0:512], g.RS[64:65, 3, 0:512],
                 g.RS[64:65, 2, 0:512], ALU.subtract,
                 reads=["RS3", "RS2"], writes=["RS3"], tag=nm + "rlo")
            p.copy(g.SQT[64:65, cS + 512:cS + 1024], g.RS[64:65, 3, 0:512],
                   reads=["RS3"], writes=[f"SQTr{r}"], tag=nm + "rlb")

            def make_norm(h2=hh, r=r, cS=cS, q0t=q0t):
                cb = CTXB[(q0t * nh + h2) % 3]

                def run():
                    p.mm(g.bank(3, 1)[0:64, :], g.SMB[64:65, 0, 0:64],
                         g.SQT[64:65, cS:cS + 512], True, False,
                         reads=[f"SQTr{r}", "SMB0"], writes=[psr(3, 1)],
                         tag=nm + "rm")
                    p.mm(g.bank(3, 1)[0:64, :], g.SMB[64:65, 0, 0:64],
                         g.SQT[64:65, cS + 512:cS + 1024], False, True,
                         reads=[f"SQTr{r}", "SMB0"], writes=[psr(3, 1)],
                         tag=nm + "rm")
                    p.copy(g.RS[0:64, 1, 0:512], g.bank(3, 1)[0:64, :],
                           reads=[psr(3, 1)], writes=["rme"], tag=nm + "rme")
                    p.tt(v.CTX[0:64, h2, :], g.bank(*cb)[0:64, :],
                         g.RS[0:64, 1, 0:512], ALU.mult,
                         reads=[psr(*cb), "rme"], writes=[f"CTXn{h2}"],
                         tag=nm + "cn")
                return run
            queue.append(("norm", gidx, make_norm()))
        if not do_wo:
            continue

        def make_wo(q0=q0, nm=nm, q0t=q0t):
            s_late = ((q0t + 1) * nh + 2) % 3
            banks = [(3, 1), CTXB[s_late]]

            def run():
                for d in range(8):
                    tb = banks[d % 2]
                    yb = g.bank(*tb)
                    yres = psr(*tb)
                    last_h = not (dense and lay["bo"])
                    for hh2 in range(nh):
                        p.mm(yb, lay["wo_ap"](d, hh2), v.CTX[0:64, hh2, :], hh2 == 0,
                             (hh2 == nh - 1) and last_h,
                             reads=[lay["wo_res"], f"CTXn{hh2}"],
                             writes=[yres], tag=nm + "wo")
                    if dense and lay["bo"]:
                        p.mm(yb, lay["bohl"][0:2, 128 * d:128 * d + 128],
                             g.SMB[0:2, 10:14, :].rearrange("p a b -> p (a b)"),
                             False, True,
                             reads=[lay["bohl_res"], "SMBw"], writes=[yres],
                             tag=nm + "bo")
                    p.tt(g.XT[:, d, 1 + q0:1 + q0 + 512], yb,
                         g.XT[:, d, 1 + q0:1 + q0 + 512], ALU.add,
                         reads=[yres, f"XT{d}"], writes=[f"XT{d}"], tag=nm + "res")
            return run

        queue.append(("wo", q0t, make_wo()))
    do_flush(None)


def dense_chunks(q0t):
    nch = 4 * q0t + 4
    plains, fixy = [], []
    for kc in range(nch):
        j = kc - (nch - 4)
        if j >= 0:
            fixy.append((kc, 128 * j, 512, "diag", 128 * j))
        elif kc == nch - 5:
            fixy.append((kc, 0, 512, "corner", 0))
        else:
            plains.append((kc, 0, 512, "plain", 0))
    return plains[:1] + fixy + plains[1:]


def local_chunks(q0t):
    out = []
    for jj in [0, -4, -3, -2, -1, 1, 2, 3]:
        kc = 4 * q0t + jj
        if kc < 0:
            continue
        cmin = max(0, 128 * jj)
        cmax = min(512, 128 * jj + 640)
        if jj <= -1:
            out.append((kc, cmin, cmax, "tri_window", cmax - 128))
        else:
            out.append((kc, cmin, cmax, "tri_causal", cmin))
    return out


def dense_phase(p, g, bohl_ap):
    v = pool_attn(g, local=False)
    p.memset(g.ps[0][:, :], 0.0, writes=[psr(0, 0), psr(0, 1)], tag="ps0z")
    p.memset(g.ps[2][:, :], 0.0, writes=[psr(2, 0), psr(2, 1)], tag="ps2z")
    parts = g.cfg.get("dense_parts", {"proj", "bias", "attn"})
    for grp in range(g.cfg.get("dense_groups", 4)):
        wq = wbq(g, 0, 2048).rearrange("p (c m j) -> p c m j", c=8, m=2)
        wk = wbq(g, 2048, 4096).rearrange("p (c m j) -> p c m j", c=8, m=2)
        wv = wbq(g, 4096, 6144).rearrange("p (c j) -> p c j", c=8)
        p.dma("gpsimd", wq, g.wqT[grp], wres(0, 2048), wres(0, 2048), tag="ldwq")
        p.dma("gpsimd", wk, g.wkT[grp], wres(2048, 4096), wres(2048, 4096), tag="ldwk")
        p.dma("gpsimd", wv, g.wvP[grp], wres(4096, 6144), wres(4096, 6144), tag="ldwv")
        p.dma("gpsimd", g.WB3[:, :, :, :], g.woP[:, 4 * grp:4 * grp + 4, :, :],
              ["WB3"], ["WB3"], tag="ldwo")
        for m in range(2):
            for t4 in range(4):
                sl = slice(1 + 512 * t4, 1 + 512 * t4 + 512)
                bq, bk = g.bank(0, t4 % 2), g.bank(1, t4 % 2)
                for c in range(8):
                    p.mm(bq, wq[:, c, m, :], g.HF[:, c, sl], c == 0, c == 7,
                         reads=wres(0, 2048) + [f"HF{c}"], writes=[psr(0, t4 % 2)],
                         tag="qprj")
                    p.mm(bk, wk[:, c, m, :], g.HF[:, c, sl], c == 0, c == 7,
                         reads=wres(2048, 4096) + [f"HF{c}"], writes=[psr(1, t4 % 2)],
                         tag="kprj")
                p.copy(v.QB[:, m, 512 * t4:512 * t4 + 512], bq, reads=[psr(0, t4 % 2)],
                       writes=["QB"], engine="scalar", tag="qev")
                p.copy(v.KB[:, m, 512 * t4:512 * t4 + 512], bk, reads=[psr(1, t4 % 2)],
                       writes=["KB"], engine="scalar", tag="kev")
        p.memset(v.VB[:, :, :].rearrange("p k (h j) -> p k h j", h=4)[:, :, :, 64:65],
                 1.0, writes=["VB"], tag="vones")
        for tt in range(16):
            bv = g.bank(2, tt % 2)
            off = 1 + 128 * tt
            for c in range(8):
                p.mm(bv[:, 0:256], g.HF[:, c, off:off + 128], wv[:, c, :],
                     c == 0, c == 7, reads=wres(4096, 6144) + [f"HF{c}"],
                     writes=[psr(2, tt % 2)], tag="vprj")
            p.copy(v.VB[:, tt, :].rearrange("p (h j) -> p h j", h=4)[:, :, 0:64],
                   bv[:, 0:256].rearrange("p (h j) -> p h j", h=4), reads=[psr(2, tt % 2)],
                   writes=["VB"], engine="scalar", tag="vev")
        if "bias" not in parts:
            continue
        bias_bounds(p, g, v.QB, v.KB, "dense", f"db{grp}")
        if "attn" not in parts:
            continue
        lay = dict(tag=f"dA{grp}", dense=True, scale=lambda g: g.SMALL[:, 0, 0:1],
                   scale_res=["SMa0"],
                   bias_ap=lambda g, hh, q0t: g.SMALL[:, 4, 4 * hh + q0t:4 * hh + q0t + 1],
                   bias_res="BIAS", chunks=dense_chunks,
                   wo_ap=lambda d, hh: g.WB3[0:64, hh, d, :], wo_res="WB3",
                   bo=(grp == 0), bohl=bohl_ap, bohl_res="MCst")
        attn_group(p, g, v, lay)


def ln_compute(p, g, shifted, stage_bf, stage_res, tag):
    RSALL = ["RS0", "RS1", "RS2", "RS3"]
    for d in range(8):
        p.copy(g.HF[:, d, 1:S + 1], g.XT[:, d, 1:S + 1], reads=[f"XT{d}"],
               writes=[f"HF{d}"], tag=tag + "hf")
    for d in range(8):
        off = 1 if (not shifted or d < 4) else 0
        rd = [f"HF{d}", "HFpad"] if shifted else [f"HF{d}"]
        p.tt(g.SQT[:, :], g.HF[:, d, off:off + S], g.HF[:, d, off:off + S],
             ALU.mult, reads=rd, writes=["SQT"], tag=tag + "sq")
        for t4 in range(4):
            p.mm(g.bank(t4 // 2, t4 % 2)[0:1, :], g.SMB[:, 0, 0:1],
                 g.HF[:, d, off + 512 * t4:off + 512 * t4 + 512], d == 0, d == 7,
                 reads=rd + ["SMB0"], writes=[psr(t4 // 2, t4 % 2)], tag=tag + "sum")
            p.mm(g.bank(2 + t4 // 2, t4 % 2)[0:1, :], g.SMB[:, 0, 0:1],
                 g.SQT[:, 512 * t4:512 * t4 + 512], d == 0, d == 7,
                 reads=["SQT", "SMB0"], writes=[psr(2 + t4 // 2, t4 % 2)],
                 tag=tag + "sqs")
    r0 = g.RSF[0:1, :]
    for h in range(2):
        for j in range(2):
            t4 = 2 * h + j
            p.copy(r0[:, 512 * j:512 * j + 512], g.bank(t4 // 2, t4 % 2)[0:1, :],
                   reads=[psr(t4 // 2, t4 % 2)], writes=RSALL, tag=tag + "se")
            p.copy(r0[:, 1024 + 512 * j:1024 + 512 * j + 512],
                   g.bank(2 + t4 // 2, t4 % 2)[0:1, :],
                   reads=[psr(2 + t4 // 2, t4 % 2)], writes=RSALL, tag=tag + "qe")
        mu = r0[:, 0:1024]
        ms = r0[:, 1024:2048]
        p.ts(mu, mu, 1.0 / DM, ALU.mult, reads=RSALL, writes=RSALL, tag=tag + "mu")
        p.ts(ms, ms, 1.0 / DM, ALU.mult, reads=RSALL, writes=RSALL, tag=tag + "ms")
        for j in range(2):
            tb = g.bank(0, 0)[0:1, :]
            p.tt(tb, mu[:, 512 * j:512 * j + 512], mu[:, 512 * j:512 * j + 512],
                 ALU.mult, reads=RSALL, writes=[psr(0, 0)], tag=tag + "m2")
            p.tt(ms[:, 512 * j:512 * j + 512], ms[:, 512 * j:512 * j + 512], tb,
                 ALU.subtract, reads=RSALL + [psr(0, 0)], writes=RSALL,
                 tag=tag + "var")
        p.ts(ms, ms, 1e-5, ALU.add, reads=RSALL, writes=RSALL, tag=tag + "eps")
        p.act(ms, ms, AF.Ln, reads=RSALL, writes=RSALL, tag=tag + "ln")
        p.act(ms, ms, AF.Exp, scale=-0.5, reads=RSALL, writes=RSALL, tag=tag + "rs")
        p.tt(mu, mu, ms, ALU.mult, reads=RSALL, writes=RSALL, tag=tag + "b0")
        p.ts(mu, mu, -1.0, ALU.mult, reads=RSALL, writes=RSALL, tag=tag + "b1")
        hc = slice(1024 * h, 1024 * h + 1024)
        p.copy(stage_bf[0:1, 0:1024], ms, reads=RSALL, writes=[stage_res],
               tag=tag + "ahi")
        p.dma("sync", g.SQT[32:33, hc], stage_bf[0:1, 0:1024], [stage_res],
              ["SQT"], tag=tag + "ahm")
        for j in range(2):
            tb = g.bank(0, 0)[0:1, :]
            p.copy(tb, stage_bf[0:1, 512 * j:512 * j + 512], reads=[stage_res],
                   writes=[psr(0, 0)], tag=tag + "ahf")
            p.tt(ms[:, 512 * j:512 * j + 512], ms[:, 512 * j:512 * j + 512], tb,
                 ALU.subtract, reads=RSALL + [psr(0, 0)], writes=RSALL,
                 tag=tag + "alo")
        p.copy(stage_bf[0:1, 0:1024], ms, reads=RSALL, writes=[stage_res],
               tag=tag + "alb")
        p.dma("sync", g.SQT[33:34, hc], stage_bf[0:1, 0:1024], [stage_res],
              ["SQT"], tag=tag + "alm")
        p.copy(g.SQT[0:1, hc], mu, reads=RSALL, writes=["SQT"], tag=tag + "bhi")
        for j in range(2):
            tb = g.bank(0, 0)[0:1, :]
            p.copy(tb, g.SQT[0:1, 1024 * h + 512 * j:1024 * h + 512 * j + 512],
                   reads=["SQT"], writes=[psr(0, 0)], tag=tag + "bhf")
            p.tt(mu[:, 512 * j:512 * j + 512], mu[:, 512 * j:512 * j + 512], tb,
                 ALU.subtract, reads=RSALL + [psr(0, 0)], writes=RSALL,
                 tag=tag + "blo")
        p.copy(stage_bf[0:1, 0:1024], mu, reads=RSALL, writes=[stage_res],
               tag=tag + "blb")
        p.dma("sync", g.SQT[1:2, hc], stage_bf[0:1, 0:1024], [stage_res],
              ["SQT"], tag=tag + "blm")
    for t4 in range(4):
        p.mm(g.bank(t4 // 2, t4 % 2), g.SMB[32:34, 0, :],
             g.SQT[32:34, 512 * t4:512 * t4 + 512], True, True,
             reads=["SQT", "SMB0"], writes=[psr(t4 // 2, t4 % 2)], tag=tag + "Am")
        p.mm(g.bank(2 + t4 // 2, t4 % 2), g.SMB[0:2, 0, :],
             g.SQT[0:2, 512 * t4:512 * t4 + 512], True, True,
             reads=["SQT", "SMB0"], writes=[psr(2 + t4 // 2, t4 % 2)], tag=tag + "Bm")


def AB_aps(g):
    Am = [g.bank(t4 // 2, t4 % 2) for t4 in range(4)]
    Bm = [g.bank(2 + t4 // 2, t4 % 2) for t4 in range(4)]
    Ar = [psr(t4 // 2, t4 % 2) for t4 in range(4)]
    Br = [psr(2 + t4 // 2, t4 % 2) for t4 in range(4)]
    return Am, Bm, Ar, Br


def outer_ln(p, g, spill, stage_bf, stage_res, tag):
    ln_compute(p, g, False, stage_bf[0:1, 0:2048], stage_res, tag)
    Am, Bm, Ar, Br = AB_aps(g)
    for d in range(8):
        for t4 in range(4):
            sl = slice(1 + 512 * t4, 1 + 512 * t4 + 512)
            p.tt(g.XT[:, d, sl], g.XT[:, d, sl], Am[t4], ALU.mult,
                 reads=[f"XT{d}", Ar[t4]], writes=[f"XT{d}"], tag=tag + "ap1")
            p.tt(g.XT[:, d, sl], g.XT[:, d, sl], Bm[t4], ALU.add,
                 reads=[f"XT{d}", Br[t4]], writes=[f"XT{d}"], tag=tag + "ap2")
        p.copy(g.HF[:, d, 1:S + 1], g.XT[:, d, 1:S + 1], reads=[f"XT{d}"],
               writes=[f"HF{d}"], tag=tag + "hf2")
        if spill:
            p.dma("sync", g.x1spill[:, S * d:S * d + S], g.XT[:, d, 1:S + 1],
                  [f"XT{d}"], [f"spill{d}"], tag=tag + "sp")


def local_qk_prep(p, g, v, l, grp, side, stage="all"):
    buf = v.QB if side == "q" else v.KB
    res = "QB" if side == "q" else "KB"
    if stage == "rot":
        return local_qk_rot(p, g, v, l, grp, side, buf, res)
    base_col = (0 if side == "q" else 512) + 256 * grp
    wr_lo, wr_hi = (0, 2048) if side == "q" else (2048, 4096)
    scol0 = getattr(g, f"{'qs' if side == 'q' else 'ks'}H{l}")
    p.dma("sync", g.SMALL[:, 5, 3:4], scol0, ["SMa5c"], ["SMa5c"], tag=f"{side}qs0")
    p.tt(g.SMALL[:, 5, 2:3], g.SMALL[:, 5, 3:4], g.SMALL[:, 5, 3:4], ALU.mult,
         reads=["SMa5c"], writes=["SMa5b"], tag=f"{side}qs2")
    p.ts(g.SMB[:, 15, 0:8].rearrange("p (a b) -> p a b", a=2),
         g.SMB[:, 4, 0:8].rearrange("p (a b) -> p a b", a=2),
         g.SMALL[:, 5, 2:3], ALU.mult, reads=["SMB4", "SMa5b"], writes=["SMB15"],
         tag=f"{side}hsq")
    wsl = wbq(g, wr_lo, wr_hi).rearrange("p (c j) -> p c j", c=8)
    p.dma("gpsimd", wsl, getattr(g, f"wqkvP{l}")[0 if side == "q" else 1, grp],
          wres(wr_lo, wr_hi), wres(wr_lo, wr_hi), tag=f"ldw{side}")
    for m in range(2):
        for t4 in range(4):
            b = g.bank(0, t4 % 2)
            for c in range(8):
                off = 1 if c < 4 else 0
                p.mm(b, wsl[:, c, 128 * m:128 * m + 128],
                     g.HF[:, c, off + 512 * t4:off + 512 * t4 + 512], c == 0, c == 7,
                     reads=wres(wr_lo, wr_hi) + [f"HF{c}", "HFpad"],
                     writes=[psr(0, t4 % 2)], tag=f"{side}prj")
            p.copy(buf[:, m, 512 * t4:512 * t4 + 512], b, reads=[psr(0, t4 % 2)],
                   writes=[res], engine="scalar", tag=f"{side}ev")
    for m in range(2):
        p.tt(g.SQT[:, :], buf[:, m, :], buf[:, m, :], ALU.mult, reads=[res],
             writes=["SQT"], tag=f"{side}sq")
        for t4 in range(4):
            i, half = 2 + t4 // 2, t4 % 2
            p.mm(g.bank(i, half)[0:4, :],
                 g.SMB[:, 4, 0:8].rearrange("p (a b) -> p a b", a=2)[:, m, :],
                 g.SQT[:, 512 * t4:512 * t4 + 512], m == 0, m == 1,
                 reads=["SQT", "SMB4"], writes=[psr(i, half)], tag=f"{side}n2")
            p.mm(g.bank(t4 // 2, t4 % 2)[0:4, :],
                 g.SMB[:, 15, 0:8].rearrange("p (a b) -> p a b", a=2)[:, m, :],
                 g.SQT[:, 512 * t4:512 * t4 + 512], m == 0, m == 1,
                 reads=["SQT", "SMB15"], writes=[psr(t4 // 2, t4 % 2)],
                 tag=f"{side}n2w")
    rsv = g.RSF[0:4, :]
    for t4 in range(4):
        i, half = 2 + t4 // 2, t4 % 2
        p.ts(g.RS[0:4, t4, 0:512], g.bank(i, half)[0:4, :], 1e-24, ALU.max,
             reads=[psr(i, half)], writes=[f"RS{t4}"], tag=f"{side}mx")
    p.act(rsv, rsv, AF.Ln, reads=["RS0", "RS1", "RS2", "RS3"],
          writes=["RS0", "RS1", "RS2", "RS3"], tag=f"{side}ln")
    p.act(rsv, rsv, AF.Exp, scale=-0.5, reads=["RS0", "RS1", "RS2", "RS3"],
          writes=["RS0", "RS1", "RS2", "RS3"], tag=f"{side}ex")
    p.tt(v.PERM[0:4, :], rsv, rsv, ALU.mult, reads=["RS0", "RS1", "RS2", "RS3"],
         writes=["PERM"], tag=f"{side}rn2")
    c0 = 0 if side == "q" else 8
    for t4 in range(4):
        bk = g.bank(t4 // 2, t4 % 2)
        p.tt(bk[0:4, :], bk[0:4, :], v.PERM[0:4, 512 * t4:512 * t4 + 512], ALU.mult,
             reads=[psr(t4 // 2, t4 % 2), "PERM"], writes=[psr(t4 // 2, t4 % 2)],
             tag=f"{side}n2t")
        p.red(g.SMALL[0:4, 8, c0 + t4:c0 + t4 + 1], bk[0:4, :], ALU.max,
              reads=[psr(t4 // 2, t4 % 2)], writes=[f"n2t{side}{t4}"],
              tag=f"{side}n2m")
    p.red(g.SMALL[0:4, 8, c0 + 4:c0 + 5], g.SMALL[0:4, 8, c0:c0 + 4], ALU.max,
          reads=[f"n2t{side}{t}" for t in range(4)], writes=["n2gq" if side == "q" else "n2gk"],
          tag=f"{side}n2g")
    p.copy(g.SQT[0:4, :], rsv, reads=["RS0", "RS1", "RS2", "RS3"], writes=["SQT"],
           tag=f"{side}hi")
    for t4 in range(4):
        p.copy(g.bank(2 + t4 // 2, t4 % 2)[0:4, :], g.SQT[0:4, 512 * t4:512 * t4 + 512],
               reads=["SQT"], writes=[psr(2 + t4 // 2, t4 % 2)], tag=f"{side}hf")
        p.tt(g.RS[0:4, t4, 0:512], g.RS[0:4, t4, 0:512],
             g.bank(2 + t4 // 2, t4 % 2)[0:4, :], ALU.subtract,
             reads=[f"RS{t4}", psr(2 + t4 // 2, t4 % 2)], writes=[f"RS{t4}"],
             tag=f"{side}lo")
    p.copy(v.PERM[0:4, :], rsv, reads=["RS0", "RS1", "RS2", "RS3"], writes=["PERM"],
           tag=f"{side}lb")
    for m in range(2):
        rb = 32 * m if side == "q" else 64 + 4 * m
        p.dma("sync", v.AT2[rb + 0:rb + 1, :], g.SQT[2 * m:2 * m + 1, :], ["SQT"],
              ["AT1"], tag=f"{side}p1")
        p.dma("sync", v.AT2[rb + 1:rb + 2, :], v.PERM[2 * m:2 * m + 1, :], ["PERM"],
              ["AT1"], tag=f"{side}p2")
        p.dma("sync", v.AT2[rb + 2:rb + 3, :], g.SQT[2 * m + 1:2 * m + 2, :], ["SQT"],
              ["AT1"], tag=f"{side}p3")
        p.dma("sync", v.AT2[rb + 3:rb + 4, :], v.PERM[2 * m + 1:2 * m + 2, :], ["PERM"],
              ["AT1"], tag=f"{side}p4")


def local_qk_rot(p, g, v, l, grp, side, buf, res):
    csH = g.CSqH if side == "q" else g.CSkH
    ssH = g.SSqH if side == "q" else g.SSkH
    scol = getattr(g, f"{'qs' if side == 'q' else 'ks'}H{l}")
    spcol = getattr(g, f"{'qsp' if side == 'q' else 'ksp'}H{l}")
    p.dma("sync", v.CS1, csH, ["CS1"], ["CS1"], tag=f"{side}cs")
    p.dma("sync", v.CS2, ssH, ["CS2"], ["CS2"], tag=f"{side}ss")
    p.dma("sync", g.SMALL[:, 5, 0:1], scol, ["SMa5"], ["SMa5"], tag=f"{side}qs")
    p.dma("sync", g.SMALL[:, 5, 1:2], spcol, ["SMa5"], ["SMa5"], tag=f"{side}qsp")
    p.ts(v.CS1, v.CS1, g.SMALL[:, 5, 0:1], ALU.mult, reads=["CS1", "SMa5"],
         writes=["CS1"], tag=f"{side}cf")
    p.ts(v.CS2, v.CS2, g.SMALL[:, 5, 1:2], ALU.mult, reads=["CS2", "SMa5"],
         writes=["CS2"], tag=f"{side}sf")
    for m in range(2):
        rb = 32 * m
        for blk, src in ((0, 32), (32, 0), (64, 96), (96, 64)):
            p.dma("sync", v.PERM[blk:blk + 32, :], buf[src:src + 32, m, :],
                  [res], ["PERM"], tag=f"{side}pm")
        p.tt(g.RSF, buf[:, m, :], v.CS1, ALU.mult, reads=[res, "CS1"],
             writes=["RS0", "RS1", "RS2", "RS3"], tag=f"{side}u")
        for half in range(2):
            p.tt(g.ps[half][:, :], v.PERM[:, 1024 * half:1024 * half + 1024],
                 v.CS2[:, 1024 * half:1024 * half + 1024], ALU.mult,
                 reads=["PERM", "CS2"], writes=[psr(half, 0), psr(half, 1)],
                 tag=f"{side}u2")
            p.tt(g.RSF[:, 1024 * half:1024 * half + 1024],
                 g.RSF[:, 1024 * half:1024 * half + 1024],
                 g.ps[half][:, :], ALU.add,
                 reads=["RS0", "RS1", "RS2", "RS3", psr(half, 0), psr(half, 1)],
                 writes=["RS0", "RS1", "RS2", "RS3"], tag=f"{side}s")
        for t4 in range(4):
            if side == "q":
                lhs = g.SMB[0:4, 9, :] if rb == 0 else g.SMB[32:36, 9, :]
                rhs = v.AT2[rb:rb + 4, 512 * t4:512 * t4 + 512]
                sres = "SMB9"
            else:
                lhs = g.SMB[64:72, 9, :] if rb == 0 else g.SMB[64:72, 14, :]
                rhs = v.AT2[64:72, 512 * t4:512 * t4 + 512]
                sres = "SMB9k"
            p.mm(g.bank(3, 1), lhs, rhs,
                 True, True, reads=["AT1", sres], writes=[psr(3, 1)],
                 tag=f"{side}rm")
            p.tt(buf[:, m, 512 * t4:512 * t4 + 512],
                 g.RSF[:, 512 * t4:512 * t4 + 512], g.bank(3, 1), ALU.mult,
                 reads=["RS0", "RS1", "RS2", "RS3", psr(3, 1)], writes=[res],
                 tag=f"{side}qh")


def local_bias(p, g, tag):
    p.ts(g.SMALL[0:4, 8, 13:14], g.SMALL[0:4, 8, 4:5], g.SMALL[0:4, 8, 12:13],
         ALU.mult, reads=["n2gq", "n2gk"], writes=["lbb"], tag=tag + "p")
    p.act(g.SMALL[0:4, 8, 14:15], g.SMALL[0:4, 8, 13:14], AF.Ln, reads=["lbb"],
          writes=["lbb1"], tag=tag + "ln")
    p.act(g.SMALL[0:4, 8, 15:16], g.SMALL[0:4, 8, 14:15], AF.Exp, scale=0.5,
          reads=["lbb1"], writes=["lbb2"], tag=tag + "ex")
    p.ts(g.SMALL[0:4, 8, 16:17], g.SMALL[0:4, 8, 15:16], -16.0, ALU.mult,
         reads=["lbb2"], writes=["lnegb"], tag=tag + "nb")
    p.copy(g.SMB[0:4, 7, 0:1], g.SMALL[0:4, 8, 16:17], reads=["lnegb"],
           writes=["SMB7"], tag=tag + "nbb")
    p.dma("sync", g.SMB[0:1, 8, 0:4].rearrange("p (a b) -> p a b", a=4),
          g.SMB[0:4, 7, 0:1], ["SMB7"], ["SMB8"], tag=tag + "nbr")
    p.mm(g.bank(3, 0)[:, 0:4], g.SMB[0:1, 0, :], g.SMB[0:1, 8, 0:4], True, True,
         reads=["SMB0", "SMB8"], writes=[psr(3, 0)], tag=tag + "bm")
    p.copy(g.SMALL[:, 9, 0:4], g.bank(3, 0)[:, 0:4], reads=[psr(3, 0)],
           writes=["BIASL"], tag=tag + "bev")


def local_phase(p, g, l):
    v = pool_attn(g, local=True)
    p.memset(g.ps[0][:, :], 0.0, writes=[psr(0, 0), psr(0, 1)], tag="ps0z")
    p.memset(g.ps[2][:, :], 0.0, writes=[psr(2, 0), psr(2, 1)], tag="ps2z")
    if l > 0:
        for d in range(8):
            p.copy(g.HF[:, d, 1:S + 1], g.XT[:, d, 1:S + 1], reads=[f"XT{d}"],
                   writes=[f"HF{d}"], tag=f"lhf{l}")
    p.dma("gpsimd", g.SMB[0:4, 9, :], g.SEL4H, ["SMB9"], ["SMB9"], tag="sel0")
    p.dma("gpsimd", g.SMB[32:36, 9, :], g.SEL4H, ["SMB9"], ["SMB9"], tag="sel1")
    p.dma("gpsimd", g.SMB[64:72, 9, :], g.SEL8AH, ["SMB9k"], ["SMB9k"], tag="sel2")
    p.dma("gpsimd", g.SMB[64:72, 14, :], g.SEL8BH, ["SMB9k"], ["SMB9k"], tag="sel3")
    for grp in range(2):
        local_qk_prep(p, g, v, l, grp, "q", "front")
        local_qk_prep(p, g, v, l, grp, "k", "front")
        local_qk_prep(p, g, v, l, grp, "q", "rot")
        local_qk_prep(p, g, v, l, grp, "k", "rot")
        wv = wbq(g, 4096, 6144).rearrange("p (c j) -> p c j", c=8)
        p.dma("gpsimd", wv, getattr(g, f"wqkvP{l}")[2, grp],
              wres(4096, 6144), wres(4096, 6144), tag="ldwvl")
        p.memset(v.VB[:, :, :].rearrange("p k (h j) -> p k h j", h=4)[:, :, :, 64:65],
                 1.0, writes=["VB"], tag="vones")
        for tt in range(16):
            bv = g.bank(2, tt % 2)
            for c in range(8):
                off = 1 if c < 4 else 0
                p.mm(bv[:, 0:256], g.HF[:, c, off + 128 * tt:off + 128 * tt + 128],
                     wv[:, c, :], c == 0, c == 7,
                     reads=wres(4096, 6144) + [f"HF{c}", "HFpad"],
                     writes=[psr(2, tt % 2)], tag="vprjl")
            p.copy(v.VB[:, tt, :].rearrange("p (h j) -> p h j", h=4)[:, :, 0:64],
                   bv[:, 0:256].rearrange("p (h j) -> p h j", h=4),
                   reads=[psr(2, tt % 2)], writes=["VB"], engine="scalar", tag="vevl")
        p.dma("gpsimd", g.WB3[:, :, :, :], getattr(g, f"wolP{l}")[:, 4 * grp:4 * grp + 4, :, :],
              ["WB3"], ["WB3"], tag="ldwol")
        if g.cfg.get("dump_qk") == (l, grp):
            p.dma("sync", g.dumpQ, v.QB[:, :, :], ["QB"], [], tag="dq")
            p.dma("sync", g.dumpK, v.KB[:, :, :], ["KB"], [], tag="dk")
            p.dma("sync", g.dumpV, v.VB[:, :, :], ["VB"], [], tag="dv")
        local_bias(p, g, f"lb{l}{grp}")
        lay = dict(tag=f"lA{l}{grp}", dense=False, scale=lambda g: 8.0,
                   scale_res=[],
                   bias_ap=lambda g, hh, q0t: g.SMALL[:, 9, hh:hh + 1],
                   bias_res="BIASL", chunks=local_chunks,
                   wo_ap=lambda d, hh: g.WB3[0:64, hh, d, :], wo_res="WB3",
                   bo=False)
        attn_group(p, g, v, lay)


def ffn_phase(p, g, l):
    v = pool_ffn(g)
    p.dma("sync", g.SMALL[:, 6, 0:8], getattr(g, f"lgH{l}"), ["SMa6"], ["SMa6"],
          tag="lg")
    p.dma("sync", g.SMALL[:, 7, 0:8], getattr(g, f"lbH{l}"), ["SMa7"], ["SMa7"],
          tag="lb")
    p.dma("sync", g.SMALL[:, 8, 0:22], getattr(g, f"b1gH{l}"), ["SMa8"], ["SMa8"],
          tag="b1g")
    ln_compute(p, g, True, v.ST[0:1, 0:2048], "MCst", f"il{l}")
    Am, Bm, Ar, Br = AB_aps(g)
    for d in range(8):
        off = 1 if d < 4 else 0
        for t4 in range(4):
            slx = slice(off + 512 * t4, off + 512 * t4 + 512)
            slh = slice(1 + 512 * t4, 1 + 512 * t4 + 512)
            p.tt(g.RSF[:, 0:512], g.XT[:, d, slx], Am[t4], ALU.mult,
                 reads=[f"XT{d}", Ar[t4]], writes=["RS0"], tag="iap1")
            p.tt(g.RSF[:, 0:512], g.RSF[:, 0:512], Bm[t4], ALU.add,
                 reads=["RS0", Br[t4]], writes=["RS0"], tag="iap2")
            p.ts(g.HF[:, d, slh], g.RSF[:, 0:512], g.SMALL[:, 6, d:d + 1], ALU.mult,
                 reads=["RS0", "SMa6", "SMa7"], writes=[f"HF{d}"],
                 s2=g.SMALL[:, 7, d:d + 1], op1=ALU.add, tag="iap3")
    b1hl = stage_bias_hilo(p, g, getattr(g, f"b1aH{l}"), v.ST[:, 0:2816], "MCst",
                           1408, f"b1a{l}")
    b2hl = stage_bias_hilo(p, g, getattr(g, f"b2H{l}"), v.ST[:, 2816:3840], "MCst",
                           512, f"b2{l}")
    for th in range(2):
        tok = slice(1 + 1024 * th, 1 + 1024 * th + 1024)
        for k in range(22):
            aw = min(128, 2730 - 128 * k)
            lo = 0 if k % 2 == 0 else 3072
            wag = wbq(g, lo, lo + 2048).rearrange("p (c x j) -> p c x j", c=8, x=2)
            wa = wag[:, :, 0, :]
            wg = wag[:, :, 1, :]
            p.dma("gpsimd", wag, getattr(g, f"w1P{l}")[k],
                  wres(lo, lo + 2048), wres(lo, lo + 2048), tag="ldw1")
            ia, ig = (0, 2) if k % 2 == 0 else (1, 3)
            for c in range(8):
                for half in range(2):
                    hsl = slice(1 + 1024 * th + 512 * half, 1 + 1024 * th + 512 * half + 512)
                    p.mm(g.bank(ia, half)[0:aw, :], wa[:, c, 0:aw], g.HF[:, c, hsl],
                         c == 0, False, reads=wres(lo, lo + 1024) + [f"HF{c}"],
                         writes=[psr(ia, half)], tag="w1a")
                    p.mm(g.bank(ig, half)[0:aw, :], wg[:, c, 0:aw], g.HF[:, c, hsl],
                         c == 0, c == 7, reads=wres(lo + 1024, lo + 2048) + [f"HF{c}"],
                         writes=[psr(ig, half)], tag="w1g")
            r = 0 if k < 11 else 1
            for half in range(2):
                p.mm(g.bank(ia, half)[0:aw, :],
                     b1hl[0:2, 1408 * r + 128 * (k - 11 * r):1408 * r + 128 * (k - 11 * r) + aw],
                     g.SMB[0:2, 10:14, :].rearrange("p a b -> p (a b)"), False, True,
                     reads=["MCst", "SMBw"], writes=[psr(ia, half)], tag="b1mm")
            rseg = slice(0, 1024) if k % 2 == 0 else slice(1024, 2048)
            rres = ["RS0", "RS1"] if k % 2 == 0 else ["RS2", "RS3"]
            p.act(g.RSF[:, rseg][0:aw, :], g.ps[ig][0:aw, :], AF.Gelu,
                  bias=g.SMALL[0:aw, 8, k:k + 1], reads=[psr(ig, 0), psr(ig, 1), "SMa8"],
                  writes=rres, tag="gelu")
            p.tt(v.MC[0:aw, k, :], g.ps[ia][0:aw, :], g.RSF[0:aw, rseg], ALU.mult,
                 reads=[psr(ia, 0), psr(ia, 1)] + rres, writes=["MC"],
                 tag="mtt")
        for d in range(8):
            lo = 0 if d % 2 == 0 else 3072
            wd = wbq(g, lo, lo + 2816).rearrange("p (k j) -> p k j", k=22)
            p.dma("gpsimd", wd, getattr(g, f"w2P{l}")[d],
                  wres(lo, lo + 2816), wres(lo, lo + 2816), tag="ldw2")
            ybanks = (g.ps[2] if d % 2 == 0 else g.ps[3])
            yres = [psr(2, 0), psr(2, 1)] if d % 2 == 0 else [psr(3, 0), psr(3, 1)]
            for k in range(22):
                kw = min(128, 2730 - 128 * k)
                for half in range(2):
                    p.mm(ybanks[:, 512 * half:512 * half + 512], wd[0:kw, k, :],
                         v.MC[0:kw, k, 512 * half:512 * half + 512], k == 0, False,
                         reads=wres(lo, lo + 2816) + ["MC"], writes=yres, tag="w2")
            for half in range(2):
                p.mm(ybanks[:, 512 * half:512 * half + 512],
                     b2hl[0:2, 128 * d:128 * d + 128], g.SMB[0:2, 10:14, :].rearrange("p a b -> p (a b)"),
                     False, True, reads=["MCst", "SMBw"], writes=yres, tag="b2mm")
            p.tt(g.XT[:, d, tok], ybanks[:, :], g.XT[:, d, tok], ALU.add,
                 reads=yres + [f"XT{d}"], writes=[f"XT{d}"], tag="w2res")


def final_phase(p, g):
    v = pool_ffn(g)
    for d in range(8):
        p.dma("sync", g.RSF, g.x1spill[:, S * d:S * d + S],
              ["RS0", "RS1", "RS2", "RS3"], ["RS0", "RS1", "RS2", "RS3"],
              tag="unspill")
        p.tt(g.XT[:, d, 1:S + 1], g.XT[:, d, 1:S + 1], g.RSF, ALU.add,
             reads=[f"XT{d}", "RS0", "RS1", "RS2", "RS3"], writes=[f"XT{d}"],
             tag="addx1")
    ln_compute(p, g, False, v.ST[0:1, 0:2048], "MCst", "fl")
    Am, Bm, Ar, Br = AB_aps(g)
    for d in range(8):
        for t4 in range(4):
            sl = slice(1 + 512 * t4, 1 + 512 * t4 + 512)
            p.tt(g.XT[:, d, sl], g.XT[:, d, sl], Am[t4], ALU.mult,
                 reads=[f"XT{d}", Ar[t4]], writes=[f"XT{d}"], tag="fap1")
            p.tt(g.XT[:, d, sl], g.XT[:, d, sl], Bm[t4], ALU.add,
                 reads=[f"XT{d}", Br[t4]], writes=[f"XT{d}"], tag="fap2")
        p.dma("sync", g.outT[128 * d:128 * d + 128, :], g.XT[:, d, 1:S + 1],
              [f"XT{d}"], [f"out{d}"], tag="stout")


def dump_xt(p, g):
    p.dma("sync", g.dumpT, g.XT[:, :, :].rearrange("p a b -> p (a b)"),
          [f"XT{d}" for d in range(8)] + ["XTpad"], [], tag="dumpXT")
    p.dma("sync", g.dumpHF, g.HF[:, :, :].rearrange("p a b -> p (a b)"),
          [f"HF{d}" for d in range(8)] + ["HFpad"], [], tag="dumpHF")


def build_main(nc, p, g, cfg):
    stage = cfg.get("stages", "all")
    ds = cfg.get("dump_stage")
    p.memset(g.XT[:, :, 0:1], 0.0, writes=["XTpad"], tag="xtpad")
    p.memset(g.HF[:, :, 0:1], 0.0, writes=["HFpad"], tag="hfpad")
    p.dma("sync", g.XT[:, :, 1:S + 1], g.xT.rearrange("(a p) t -> p a t", p=128),
          (), [f"XT{d}" for d in range(8)], tag="ldx")
    ld_tables(p, g)
    for d in range(8):
        p.copy(g.HF[:, d, 1:S + 1], g.XT[:, d, 1:S + 1], reads=[f"XT{d}"],
               writes=[f"HF{d}"], tag="hf0")
    vff = pool_ffn(g)
    bo_bf = vff.ST[:, 0:1024]
    bohl = stage_bias_hilo(p, g, g.boH, bo_bf, "MCst", 512, "bo")
    want = lambda s: stage == "all" or s in stage
    if want("dense"):
        dense_phase(p, g, bohl)
        if ds == "x2":
            return dump_xt(p, g)
        p.fence()
        outer_ln(p, g, spill=True, stage_bf=vff.ST, stage_res="MCst", tag="ln1")
        if ds == "x1":
            return dump_xt(p, g)
    for l in (0, 1):
        if want(f"loc{l}"):
            p.fence()
            local_phase(p, g, l)
            if ds == f"xa{l}":
                return dump_xt(p, g)
        if want(f"ffn{l}"):
            p.fence()
            ffn_phase(p, g, l)
            if ds == f"xb{l}":
                return dump_xt(p, g)
    if want("final"):
        p.fence()
        final_phase(p, g)
    else:
        for d in range(8):
            p.dma("sync", g.outT[128 * d:128 * d + 128, :], g.XT[:, d, 1:S + 1],
                  [f"XT{d}"], [f"out{d}"], tag="stout0")


def kernel(**inputs):
    import os
    cfg = {}
    abl = os.environ.get("ABL", "")
    if abl:
        import json as _json
        cfg = _json.loads(abl)
        if "stages" in cfg:
            cfg["stages"] = set(cfg["stages"])
    cores = prepack(inputs)
    nc = build_program(cfg)
    res = run_bass_kernel_spmd(nc, cores, core_ids=list(range(8)))
    return np.stack([np.asarray(r["outT"], np.float32).T for r in res.results], axis=0)


# version 16
# speedup vs baseline: 1.0099x; 1.0099x over previous
import numpy as np
import concourse.bass as bass
import concourse.mybir as mybir
from concourse.bass_utils import run_bass_kernel_spmd
from contextlib import ExitStack

F32 = mybir.dt.float32
BF16 = mybir.dt.bfloat16
U8 = mybir.dt.uint8
AF = mybir.ActivationFunctionType
ALU = mybir.AluOpType
AX = mybir.AxisListType

S = 2048
DM = 1024
ENGINES = ("tensor", "vector", "scalar", "gpsimd", "sync")


DMA_WIN = 8


class Prog:
    def __init__(self, nc):
        self.nc = nc
        self.ops = []
        self.res = {}
        self.fence_deps = set()
        self.dmas = {}

    def op(self, engine, fn, reads=(), writes=(), dma=False, tag=""):
        deps = set(self.fence_deps)
        for r in reads:
            e = self.res.get(r)
            if e and e["w"] is not None:
                deps.add(e["w"])
            if e and r.startswith("PS"):
                for ri in e["r"]:
                    if self.ops[ri]["engine"] != engine:
                        deps.add(ri)
        for w in writes:
            e = self.res.get(w)
            if e:
                if e["w"] is not None:
                    deps.add(e["w"])
                deps.update(e["r"])
        i = len(self.ops)
        if dma:
            ql = self.dmas.setdefault(engine, [])
            if len(ql) >= DMA_WIN:
                deps.add(ql[-DMA_WIN])
            ql.append(i)
        pruned = set()
        for d in deps:
            dop = self.ops[d]
            if (engine == "tensor" and dop["engine"] == "tensor"
                    and not dma and not dop["dma"]):
                continue
            pruned.add(d)
        self.ops.append(dict(engine=engine, fn=fn, dma=dma, deps=pruned, tag=tag))
        for r in reads:
            self.res.setdefault(r, {"w": None, "r": []})["r"].append(i)
        for w in writes:
            self.res[w] = {"w": i, "r": []}
        return i

    def fence(self):
        tails = set()
        for e in self.res.values():
            if e["w"] is not None:
                tails.add(e["w"])
            tails.update(e["r"])
        self.fence_deps = tails

    def finalize(self, stack, last_engine="sync"):
        nc = self.nc
        n = len(self.ops)
        needed = set()
        for o in self.ops:
            needed.update(o["deps"])
        cnt = {}
        tok = [None] * n
        qidx = {}
        for i, o in enumerate(self.ops):
            if o["dma"]:
                j = qidx.get(o["engine"], 0)
                qidx[o["engine"]] = j + 1
                k = ("q", o["engine"], j % DMA_WIN)
                cnt[k] = cnt.get(k, 0) + 16
                tok[i] = (k, cnt[k])
            elif i in needed:
                k = ("e", o["engine"])
                cnt[k] = cnt.get(k, 0) + 1
                tok[i] = (k, cnt[k])
        issue_clock = [None] * n
        stream_clock = {e: {} for e in ENGINES}
        waits = [None] * n
        for i, o in enumerate(self.ops):
            sc = stream_clock[o["engine"]]
            need = {}
            for d in sorted(o["deps"]):
                k, v = tok[d]
                if sc.get(k, 0) < v and need.get(k, 0) < v:
                    need[k] = v
            waits[i] = need
            for d in sorted(o["deps"]):
                k, v = tok[d]
                if sc.get(k, 0) < v:
                    sc[k] = v
                ic = issue_clock[d]
                if ic:
                    for kk, vv in ic.items():
                        if sc.get(kk, 0) < vv:
                            sc[kk] = vv
            issue_clock[i] = dict(sc)
        sems = {k: stack.enter_context(nc.semaphore("_".join(str(x) for x in k)))
                for k in cnt}
        block = stack.enter_context(nc.Block())
        self.n_ops = n

        def emit_for(engine):
            def run(eng):
                for i, o in enumerate(self.ops):
                    if o["engine"] != engine:
                        continue
                    for k, v in sorted(waits[i].items()):
                        eng.wait_ge(sems[k], v)
                    instr = o["fn"](eng)
                    if tok[i] is not None:
                        k, v = tok[i]
                        instr.then_inc(sems[k], 16 if o["dma"] else 1)
                if engine == last_engine:
                    for k, v in sorted(cnt.items()):
                        eng.wait_ge(sems[k], v)
            return run

        for e in ENGINES:
            getattr(block, e)(emit_for(e))

    def mm(self, out, lhsT, rhs, start, stop, reads, writes, tag=""):
        def fn(eng, out=out, lhsT=lhsT, rhs=rhs, start=start, stop=stop):
            return eng.matmul(out, lhsT, rhs, start=start, stop=stop)
        return self.op("tensor", fn, reads, writes, tag=tag)

    def tt(self, out, a, b, alu, reads, writes, engine="vector", tag=""):
        def fn(eng, out=out, a=a, b=b, alu=alu):
            return eng.tensor_tensor(out, a, b, op=alu)
        return self.op(engine, fn, reads, writes, tag=tag)

    def ts(self, out, a, s1, op0, reads, writes, s2=None, op1=None, engine="vector", tag=""):
        def fn(eng, out=out, a=a, s1=s1, s2=s2, op0=op0, op1=op1):
            if s2 is None:
                return eng.tensor_scalar(out, a, s1, None, op0=op0)
            return eng.tensor_scalar(out, a, s1, s2, op0=op0, op1=op1)
        return self.op(engine, fn, reads, writes, tag=tag)

    def act(self, out, in_, func, reads, writes, bias=None, scale=None, accum=None, tag=""):
        def fn(eng, out=out, in_=in_, func=func, bias=bias, scale=scale, accum=accum):
            kw = {}
            if bias is not None:
                kw["bias"] = bias
            if scale is not None:
                kw["scale"] = scale
            if accum is not None:
                kw["accum_out"] = accum
            return eng.activation(out, in_, func, **kw)
        return self.op("scalar", fn, reads, writes, tag=tag)

    def copy(self, out, in_, reads, writes, engine="vector", tag=""):
        if engine == "scalar":
            def fn(eng, out=out, in_=in_):
                return eng.copy(out, in_)
        else:
            def fn(eng, out=out, in_=in_):
                return eng.tensor_copy(out, in_)
        return self.op(engine, fn, reads, writes, tag=tag)

    def red(self, out, in_, alu, reads, writes, absv=False, tag=""):
        def fn(eng, out=out, in_=in_, alu=alu, absv=absv):
            return eng.tensor_reduce(out, in_, axis=AX.X, op=alu,
                                     apply_absolute_value=absv or None)
        return self.op("vector", fn, reads, writes, tag=tag)

    def recip(self, out, in_, reads, writes, tag=""):
        def fn(eng, out=out, in_=in_):
            return eng.reciprocal(out, in_)
        return self.op("vector", fn, reads, writes, tag=tag)

    def memset(self, ap, val, writes, engine="vector", tag=""):
        def fn(eng, ap=ap, val=val):
            return eng.memset(ap, val)
        return self.op(engine, fn, (), writes, tag=tag)

    def dma(self, queue, out, in_, reads, writes, tag=""):
        def fn(eng, out=out, in_=in_):
            return eng.dma_start(out=out, in_=in_)
        return self.op(queue, fn, reads, writes, dma=True, tag=tag)


def make_trig():
    d = np.arange(128) % 64
    i = (d % 32).astype(np.float64)
    t = np.arange(S, dtype=np.float64)
    inv_freq = 1.0 / (10000.0 ** ((2 * i) / 64.0))
    freqs = inv_freq[:, None] * t[None, :]
    base = ((2 * i) + 0.4 * 64) / (1.4 * 64)
    power = (t - S // 2) / 256.0
    scale = base[:, None] ** power[None, :]
    sign = np.where(d < 32, -1.0, 1.0)[:, None]
    return (np.cos(freqs) * scale, np.sin(freqs) * scale * sign,
            np.cos(freqs) / scale, np.sin(freqs) / scale * sign)


def prepack(inputs):
    f32 = np.float32
    x = np.asarray(inputs["L0_attn_x"], f32)
    Wq = np.asarray(inputs["L0_attn_Wq"], f32)
    Wk = np.asarray(inputs["L0_attn_Wk"], f32)
    Wv = np.asarray(inputs["L0_attn_Wv"], f32)
    Wo = np.asarray(inputs["L0_attn_Wo"], f32)
    bo = np.asarray(inputs["L0_attn_bo"], f32)
    rel = np.asarray(inputs["L0_attn_rel_emb"], f32)
    sh = {}
    sh["wqT"] = np.ascontiguousarray(
        Wq.reshape(8, 128, 4, 2, 128).transpose(2, 1, 0, 3, 4))
    sh["wkT"] = np.ascontiguousarray(
        Wk.reshape(8, 128, 4, 2, 128).transpose(2, 1, 0, 3, 4))
    sh["wvP"] = np.ascontiguousarray(
        Wv.reshape(8, 128, 4, 256).transpose(2, 1, 0, 3))
    sh["woP"] = np.ascontiguousarray(Wo.reshape(16, 64, 8, 128).transpose(1, 0, 2, 3))
    sh["boH"] = bo.reshape(2, 512)
    sh["relH"] = rel.reshape(1, 17)
    p = np.arange(128)[:, None]
    c = np.arange(144)[None, :]
    sh["GdH"] = rel[np.clip(c - p, 0, 16)].astype(f32)
    sh["U01dH"] = (c[:, 0:128] >= p).astype(f32)
    pp = np.arange(64)[:, None]
    cc = np.arange(16)[None, :]
    sh["GpH"] = rel[np.clip(cc + 64 - pp, 0, 16)].astype(f32)
    sh["r16H"] = np.broadcast_to(rel[16:17], (128, 1)).astype(f32).copy()
    c2 = np.arange(128)[None, :]
    sh["UcH"] = (c2 >= p).astype(f32)
    sh["UwH"] = (c2 <= p).astype(f32)
    hs = np.zeros((128, 2, 4), f32)
    for pr in range(128):
        for r in range(2):
            hs[pr, r, 2 * r + pr // 64] = 1.0
    sh["HSH"] = hs
    sel4 = np.zeros((4, 128), f32)
    sel4[0, 0:64] = 1; sel4[1, 0:64] = 1; sel4[2, 64:128] = 1; sel4[3, 64:128] = 1
    sh["SEL4H"] = sel4
    z4 = np.zeros((4, 128), f32)
    sh["SEL8AH"] = np.vstack([sel4, z4])
    sh["SEL8BH"] = np.vstack([z4, sel4])
    csq, ssq, csk, ssk = make_trig()
    sh["CSqH"] = csq.astype(f32); sh["SSqH"] = ssq.astype(f32)
    sh["CSkH"] = csk.astype(f32); sh["SSkH"] = ssk.astype(f32)

    wqkv = np.asarray(inputs["L0_ffn_wqkv"], f32)
    wol = np.asarray(inputs["L0_ffn_wo"], f32)
    w1 = np.asarray(inputs["L0_ffn_w1"], f32)
    w2 = np.asarray(inputs["L0_ffn_w2"], f32)
    b1 = np.asarray(inputs["L0_ffn_b1"], f32)
    b2 = np.asarray(inputs["L0_ffn_b2"], f32)
    lg = np.asarray(inputs["L0_ffn_ln_g"], f32)
    lb = np.asarray(inputs["L0_ffn_ln_b"], f32)
    qs = np.asarray(inputs["L0_ffn_q_scale"], f32)
    ks = np.asarray(inputs["L0_ffn_k_scale"], f32)
    perm = ((np.arange(128) % 64 + 32) % 64) + 64 * (np.arange(128) // 64)
    for l in (0, 1):
        wq3 = wqkv[l].reshape(8, 128, 3, 2, 256)
        sh[f"wqkvP{l}"] = np.ascontiguousarray(wq3.transpose(2, 3, 1, 0, 4))
        sh[f"wolP{l}"] = np.ascontiguousarray(wol[l].reshape(8, 64, 8, 128).transpose(1, 0, 2, 3))
        w1p = np.zeros((22, 128, 8, 2, 128), f32)
        wa_ = w1[l][:, 0:2730]
        wg_ = w1[l][:, 2730:5460]
        for k in range(22):
            awk = min(128, 2730 - 128 * k)
            w1p[k, :, :, 0, 0:awk] = wa_[:, 128 * k:128 * k + awk].reshape(8, 128, awk).transpose(1, 0, 2)
            w1p[k, :, :, 1, 0:awk] = wg_[:, 128 * k:128 * k + awk].reshape(8, 128, awk).transpose(1, 0, 2)
        sh[f"w1P{l}"] = np.ascontiguousarray(w1p)
        w2p = np.zeros((22 * 128, 1024), f32)
        w2p[0:2730] = w2[l]
        sh[f"w2P{l}"] = np.ascontiguousarray(
            w2p.reshape(22, 128, 8, 128).transpose(2, 1, 0, 3))
        b1a = np.zeros((2, 1408), f32)
        b1a.reshape(-1)[0:2730] = b1[l, 0:2730]
        sh[f"b1aH{l}"] = b1a
        b1g = np.zeros((128, 22), f32)
        for k in range(22):
            seg = b1[l, 2730 + 128 * k: 2730 + min(128 * (k + 1), 2730)]
            b1g[0:len(seg), k] = seg
        sh[f"b1gH{l}"] = b1g
        sh[f"b2H{l}"] = b2[l].reshape(2, 512)
        sh[f"lgH{l}"] = np.ascontiguousarray(lg[l].reshape(8, 128).T)
        sh[f"lbH{l}"] = np.ascontiguousarray(lb[l].reshape(8, 128).T)
        sh[f"qsH{l}"] = np.tile(qs[l], 2).reshape(128, 1).copy()
        sh[f"qspH{l}"] = np.tile(qs[l], 2)[perm].reshape(128, 1).copy()
        sh[f"ksH{l}"] = np.tile(ks[l], 2).reshape(128, 1).copy()
        sh[f"kspH{l}"] = np.tile(ks[l], 2)[perm].reshape(128, 1).copy()
    cores = []
    for b in range(8):
        d = dict(sh)
        d["xT"] = np.ascontiguousarray(x[b].T)
        cores.append(d)
    return cores


class G:
    pass


def psr(i, half):
    return f"PS{i}{'ab'[half]}"


def build_program(cfg):
    nc = bass.Bass("TRN2", target_bir_lowering=False)
    g = G()
    g.cfg = cfg

    def dram(name, shape, dtype=F32, kind="ExternalInput"):
        setattr(g, name, nc.dram_tensor(name, shape, dtype, kind=kind).ap())

    dram("xT", [DM, S])
    dram("wqT", [4, 128, 8, 2, 128]); dram("wkT", [4, 128, 8, 2, 128])
    dram("wvP", [4, 128, 8, 256]); dram("woP", [64, 16, 8, 128])
    dram("boH", [2, 512]); dram("relH", [1, 17])
    dram("GdH", [128, 144]); dram("U01dH", [128, 128]); dram("GpH", [64, 16])
    dram("r16H", [128, 1]); dram("UcH", [128, 128]); dram("UwH", [128, 128])
    dram("HSH", [128, 2, 4]); dram("SEL4H", [4, 128])
    dram("SEL8AH", [8, 128]); dram("SEL8BH", [8, 128])
    dram("CSqH", [128, S]); dram("SSqH", [128, S])
    dram("CSkH", [128, S]); dram("SSkH", [128, S])
    for l in (0, 1):
        dram(f"wqkvP{l}", [3, 2, 128, 8, 256]); dram(f"wolP{l}", [64, 8, 8, 128])
        dram(f"w1P{l}", [22, 128, 8, 2, 128]); dram(f"w2P{l}", [8, 128, 22, 128])
        dram(f"b1aH{l}", [2, 1408]); dram(f"b1gH{l}", [128, 22]); dram(f"b2H{l}", [2, 512])
        dram(f"lgH{l}", [128, 8]); dram(f"lbH{l}", [128, 8])
        dram(f"qsH{l}", [128, 1]); dram(f"qspH{l}", [128, 1])
        dram(f"ksH{l}", [128, 1]); dram(f"kspH{l}", [128, 1])
    dram("x1spill", [128, 8 * S], kind="Internal")
    dram("outT", [DM, S], kind="ExternalOutput")
    g.dump_stage = cfg.get("dump_stage")
    if g.dump_stage:
        dram("dumpT", [128, 8 * (S + 1)], kind="ExternalOutput")
        dram("dumpHF", [128, 8 * (S + 1)], BF16, kind="ExternalOutput")
    if cfg.get("dump_qk"):
        dram("dumpQ", [128, 2, S], BF16, kind="ExternalOutput")
        dram("dumpK", [128, 2, S], BF16, kind="ExternalOutput")
        dram("dumpV", [128, 16, 260], BF16, kind="ExternalOutput")

    st = ExitStack()
    en = st.enter_context

    def sb(name, shape, dtype):
        setattr(g, name, en(nc.sbuf_tensor(name, shape, dtype)))

    sb("XT", [128, 8, S + 1], F32)
    sb("HF", [128, 8, S + 1], BF16)
    sb("SQT", [128, S], BF16)
    sb("RS", [128, 4, 512], F32)
    sb("SMALL", [128, 16, 32], F32)
    sb("SMB", [128, 16, 128], BF16)
    sb("TBL", [128, 208], F32)
    sb("WBQ", [128, 6144], BF16)
    sb("WB3", [64, 4, 8, 128], BF16)
    sb("POOL", [128, 57472], U8)
    g.ps = [en(nc.psum_tensor(f"PS{i}", [128, 1024], F32)) for i in range(4)]
    g.bank = lambda i, half: g.ps[i][:, 512 * half:512 * half + 512]
    g.RSF = g.RS.ap().rearrange("p a b -> p (a b)")

    p = Prog(nc)
    g.p = p
    build_main(nc, p, g, cfg)
    p.finalize(st)
    st.close()
    return nc


WR = {"A": (0, 1536), "B": (1536, 3072), "C": (3072, 4608), "D": (4608, 6144)}


def wbq(g, lo, hi):
    return g.WBQ[:, lo:hi]


def wres(lo, hi):
    out = []
    for nm, (a, b) in WR.items():
        if lo < b and hi > a:
            out.append("WBQ" + nm)
    return out


def pool_attn(g, local):
    v = G()
    P = g.POOL
    def bf(off, n):
        return P[:, off:off + 2 * n].bitcast(BF16)
    v.QB = bf(0, 4096).rearrange("p (m t) -> p m t", m=2)
    v.KB = bf(8192, 4096).rearrange("p (m t) -> p m t", m=2)
    if local:
        v.PERM = bf(16384, 2048)
        v.VB = bf(20480, 4160).rearrange("p (k j) -> p k j", k=16)
        v.CTX = bf(28800, 2048).rearrange("p (h t) -> p h t", h=4)[0:64]
        v.AT = bf(32896, 2048)
        v.AT2 = bf(36992, 2048)
        v.CS1 = P[:, 41088:41088 + 8192].bitcast(F32)
        v.CS2 = P[:, 49280:49280 + 8192].bitcast(F32)
        v.RHL = v.PERM
        v.RHL_res = "PERM"
    else:
        v.VB = bf(16384, 4160).rearrange("p (k j) -> p k j", k=16)
        v.CTX = bf(24704, 2048).rearrange("p (h t) -> p h t", h=4)[0:64]
        v.AT = bf(28800, 2048)
        v.AT2 = bf(32896, 2048)
        v.RHL = bf(36992, 2048)
        v.RHL_res = "DRHL"
    return v


def pool_ffn(g):
    v = G()
    v.MC = g.POOL[:, 0:45056].bitcast(BF16).rearrange("p (k t) -> p k t", k=22)
    v.ST = g.POOL[:, 45056:45056 + 8192].bitcast(BF16)
    return v


def ld_tables(p, g):
    p.memset(g.SMB[:, 0, :], 1.0, writes=["SMB0"], tag="ones")
    p.memset(g.SMB[:, 10:14, :], 1.0, writes=["SMBw"], tag="ones512")
    p.dma("sync", g.TBL[:, 0:144], g.GdH, (), ["TBL0"], tag="Gd")
    p.dma("sync", g.TBL[0:64, 144:160], g.GpH, (), ["TBL1"], tag="Gp")
    p.dma("sync", g.TBL[0:1, 160:177], g.relH, (), ["TBL2"], tag="rel")
    p.dma("sync", g.TBL[:, 177:178], g.r16H, (), ["TBL3"], tag="r16")
    p.dma("gpsimd", g.SMB[:, 1, :], g.UcH, (), ["SMB1"], tag="Uc")
    p.dma("gpsimd", g.SMB[:, 2, :], g.UwH, (), ["SMB2"], tag="Uw")
    p.dma("gpsimd", g.SMB[:, 3, :], g.U01dH, (), ["SMB3"], tag="U01d")
    p.dma("gpsimd", g.SMB[:, 4, 0:8].rearrange("p (a b) -> p a b", a=2), g.HSH,
          (), ["SMB4"], tag="HS")
    p.ts(g.SMALL[:, 0, 0:1], g.TBL[:, 177:178], 0.125, ALU.mult,
         reads=["TBL3"], writes=["SMa0"], tag="r16s")
    p.red(g.SMALL[0:1, 1, 0:1], g.TBL[0:1, 160:177], ALU.max, reads=["TBL2"],
          writes=["SMa1"], absv=True, tag="relmax")
    p.copy(g.SMB[0:1, 6, 0:1], g.SMALL[0:1, 1, 0:1], reads=["SMa1"], writes=["SMB6"],
           tag="relmaxb")
    p.mm(g.bank(3, 0)[0:4, 0:1], g.SMB[0:1, 0, 0:4], g.SMB[0:1, 6, 0:1], True, True,
         reads=["SMB0", "SMB6"], writes=[psr(3, 0)], tag="relmm")
    p.copy(g.SMALL[0:4, 1, 1:2], g.bank(3, 0)[0:4, 0:1], reads=[psr(3, 0)],
           writes=["SMa1b"], tag="relmax4")


def stage_bias_hilo(p, g, srcH, dst_bf, dst_res, nseg, tag):
    rsb = g.RS.ap().rearrange("p a b -> p (a b)").bitcast(BF16)
    rsf = g.RSF
    RSALL = ["RS0", "RS1", "RS2", "RS3"]
    for s_ in range((nseg + 511) // 512):
        a, b = 512 * s_, min(512 * s_ + 512, nseg)
        w = b - a
        p.dma("sync", rsf[0:2, 0:w], srcH[0:2, a:b], RSALL, RSALL, tag=tag + "ld")
        p.copy(rsb[0:2, 1024:1024 + w], rsf[0:2, 0:w], reads=RSALL, writes=RSALL,
               tag=tag + "hi")
        p.copy(g.bank(3, 0)[0:2, 0:w], rsb[0:2, 1024:1024 + w], reads=RSALL,
               writes=[psr(3, 0)], tag=tag + "hf")
        p.tt(rsf[0:2, 0:w], rsf[0:2, 0:w], g.bank(3, 0)[0:2, 0:w], ALU.subtract,
             reads=RSALL + [psr(3, 0)], writes=RSALL, tag=tag + "lo")
        p.copy(rsb[0:2, 1536:1536 + w], rsf[0:2, 0:w], reads=RSALL, writes=RSALL,
               tag=tag + "lb")
        p.dma("sync", dst_bf[0:1, 0:2 * nseg].rearrange("p (r c) -> p r c", r=2)[:, :, a:b],
              rsb[0:2, 1024:1024 + w], RSALL, [dst_res], tag=tag + "chi")
        p.dma("sync", dst_bf[1:2, 0:2 * nseg].rearrange("p (r c) -> p r c", r=2)[:, :, a:b],
              rsb[0:2, 1536:1536 + w], RSALL, [dst_res], tag=tag + "clo")
    return dst_bf[0:2, 0:2 * nseg]


def bias_bounds(p, g, QB, KB, mode, tag):
    for which, buf, res in (("q", QB, "QB"), ("k", KB, "KB")):
        for m in range(2):
            p.tt(g.SQT[:, :], buf[:, m, :], buf[:, m, :], ALU.mult,
                 reads=[res], writes=["SQT"], tag=tag + "sq")
            for t4 in range(4):
                i, half = 2 + t4 // 2, t4 % 2
                p.mm(g.bank(i, half)[0:4, :],
                     g.SMB[:, 4, 0:8].rearrange("p (a b) -> p a b", a=2)[:, m, :],
                     g.SQT[:, 512 * t4:512 * t4 + 512], m == 0, m == 1,
                     reads=["SQT", "SMB4"], writes=[psr(i, half)], tag=tag + "n2")
        for t4 in range(4):
            i, half = 2 + t4 // 2, t4 % 2
            col = t4 if which == "q" else 4 + t4
            p.red(g.SMALL[0:4, 2, col:col + 1], g.bank(i, half)[0:4, :], ALU.max,
                  reads=[psr(i, half)], writes=[f"n2m{col}"], tag=tag + "n2m")
    p.red(g.SMALL[0:4, 2, 8:9], g.SMALL[0:4, 2, 4:8], ALU.max,
          reads=[f"n2m{c}" for c in (4, 5, 6, 7)], writes=["n2kg"], tag=tag + "kg")
    p.ts(g.SMALL[0:4, 3, 0:4], g.SMALL[0:4, 2, 0:4], g.SMALL[0:4, 2, 8:9], ALU.mult,
         reads=[f"n2m{c}" for c in (0, 1, 2, 3)] + ["n2kg"], writes=["bb"],
         tag=tag + "bb")
    p.act(g.SMALL[0:4, 3, 4:8], g.SMALL[0:4, 3, 0:4], AF.Ln, reads=["bb"],
          writes=["bb1"], tag=tag + "ln")
    p.act(g.SMALL[0:4, 3, 8:12], g.SMALL[0:4, 3, 4:8], AF.Exp, scale=0.5,
          reads=["bb1"], writes=["bb2"], tag=tag + "ex")
    if mode == "dense":
        p.ts(g.SMALL[0:4, 3, 12:16], g.SMALL[0:4, 3, 8:12], g.SMALL[0:4, 1, 1:2],
             ALU.mult, reads=["bb2", "SMa1b"], writes=["negb"], s2=-0.125,
             op1=ALU.mult, tag=tag + "nb")
    else:
        p.ts(g.SMALL[0:4, 3, 12:16], g.SMALL[0:4, 3, 8:12], -8.0, ALU.mult,
             reads=["bb2"], writes=["negb"], tag=tag + "nb")
    p.copy(g.SMB[0:4, 7, 0:4], g.SMALL[0:4, 3, 12:16], reads=["negb"],
           writes=["SMB7"], tag=tag + "nbb")
    p.dma("sync", g.SMB[0:1, 8, 0:16].rearrange("p (a b) -> p a b", a=4),
          g.SMB[0:4, 7, 0:4], ["SMB7"], ["SMB8"], tag=tag + "nbrow")
    p.mm(g.bank(3, 0)[:, 0:16], g.SMB[0:1, 0, :], g.SMB[0:1, 8, 0:16], True, True,
         reads=["SMB0", "SMB8"], writes=[psr(3, 0)], tag=tag + "bm")
    p.copy(g.SMALL[:, 4, 0:16], g.bank(3, 0)[:, 0:16], reads=[psr(3, 0)],
           writes=["BIAS"], tag=tag + "bev")


def attn_group(p, g, v, lay):
    nm = lay["tag"]
    dense = lay["dense"]
    nq0t = g.cfg.get("nq0t", 4)
    nh = g.cfg.get("nheads", 4)
    do_wo = g.cfg.get("do_wo", True)
    CTXB = [(1, 0), (1, 1), (3, 0)]
    queue = []

    def deadline(item):
        kind, gq, _ = item
        if kind == "norm":
            return gq + 2
        return (gq + 1) * nh + 2

    def do_flush(cur):
        while queue:
            if cur is not None and cur < min(deadline(it) for it in queue):
                break
            queue.pop(0)[2]()

    for q0t in range(nq0t):
        q0 = 512 * q0t
        chunks = lay["chunks"](q0t)
        nch = len(chunks)
        for hh in range(nh):
            m, base = hh // 2, 64 * (hh % 2)
            biasap = lay["bias_ap"](g, hh, q0t)
            gidx = q0t * nh + hh
            ci_bank = CTXB[gidx % 3]
            ctx_ps = g.bank(*ci_bank)[0:65, :]

            def emit_S(ci):
                kc, cmin, cmax, kind, tcol = chunks[ci]
                it = 0 if (ci // 2) % 2 == 0 else 2
                Sps = g.ps[it][:, 512 * (ci % 2):512 * (ci % 2) + 512]
                lhs = v.KB[base:base + 64, m, 128 * kc:128 * kc + 128]
                rhs = v.QB[base:base + 64, m, q0 + cmin:q0 + cmax]
                p.mm(Sps[:, cmin:cmax], lhs, rhs, True, True,
                     reads=["QB", "KB"], writes=[psr(it, ci % 2)], tag=nm + "S")

            def fixes(ci, aseg, ares, it):
                kc, cmin, cmax, kind, tcol = chunks[ci]
                Sps = g.ps[it][:, 512 * (ci % 2):512 * (ci % 2) + 512]
                sres = psr(it, ci % 2)
                if kind == "diag":
                    w = min(144, cmax - tcol)
                    p.tt(g.RS[:, 0, 0:w], Sps[:, tcol:tcol + w], g.TBL[:, 0:w],
                         ALU.mult, reads=[sres, "TBL0"], writes=["RS0"],
                         tag=nm + "bG")
                    p.act(aseg[:, tcol:tcol + w], g.RS[:, 0, 0:w], AF.Exp,
                          bias=biasap, scale=0.125, reads=["RS0", lay["bias_res"]],
                          writes=[ares], tag=nm + "bE")
                    wu = min(w, 128)
                    p.tt(aseg[:, tcol:tcol + wu], aseg[:, tcol:tcol + wu],
                         g.SMB[:, 3, 0:wu], ALU.mult, reads=[ares, "SMB3"],
                         writes=[ares], tag=nm + "bU")
                elif kind == "corner":
                    p.tt(g.RS[64:128, 0, 0:16], Sps[64:128, 0:16],
                         g.TBL[0:64, 144:160], ALU.mult,
                         reads=[sres, "TBL1"], writes=["RS0"], tag=nm + "cG")
                    p.act(aseg[64:128, 0:16], g.RS[64:128, 0, 0:16], AF.Exp,
                          bias=biasap[64:128, :], scale=0.125,
                          reads=["RS0", lay["bias_res"]], writes=[ares],
                          tag=nm + "cE")
                elif kind == "tri_causal":
                    p.tt(aseg[:, tcol:tcol + 128], aseg[:, tcol:tcol + 128],
                         g.SMB[:, 1, :], ALU.mult, reads=[ares, "SMB1"],
                         writes=[ares], tag=nm + "tc")
                elif kind == "tri_window":
                    p.tt(aseg[:, tcol:tcol + 128], aseg[:, tcol:tcol + 128],
                         g.SMB[:, 2, :], ALU.mult, reads=[ares, "SMB2"],
                         writes=[ares], tag=nm + "tw")

            def emit_av(bb):
                a0, a1 = 2 * bb, 2 * bb + 1
                ab2 = 1024 * (bb % 2)
                for ci in ([a0] if a1 >= nch else [a0, a1]):
                    kc, cmin, cmax, kind, tcol = chunks[ci]
                    p.mm(ctx_ps[:, cmin:cmax], v.VB[:, kc, 65 * hh:65 * hh + 65],
                         v.AT[:, ab2 + 512 * (ci % 2) + cmin:ab2 + 512 * (ci % 2) + cmax],
                         ci == 0, ci == nch - 1,
                         reads=[f"ATs{2 * (bb % 2) + (ci % 2)}", "VB"],
                         writes=[psr(*ci_bank)], tag=nm + "AV")

            npair = (nch + 1) // 2
            emit_S(0)
            if nch > 1:
                emit_S(1)
            for b in range(npair):
                c0 = 2 * b
                c1 = 2 * b + 1 if 2 * b + 1 < nch else None
                it = 0 if b % 2 == 0 else 2
                if 2 * b + 2 < nch:
                    emit_S(2 * b + 2)
                    if 2 * b + 3 < nch:
                        emit_S(2 * b + 3)
                if b == 0:
                    do_flush(gidx)
                cmin0 = chunks[c0][1]
                hi = 1024 if c1 is not None else 512
                abase = 1024 * (b % 2)
                ares0, ares1 = f"ATs{2 * (b % 2)}", f"ATs{2 * (b % 2) + 1}"
                p.act(v.AT[:, abase + cmin0:abase + hi],
                      g.ps[it][:, cmin0:hi], AF.Exp,
                      bias=biasap, scale=lay["scale"](g),
                      reads=[psr(it, 0), psr(it, 1), lay["bias_res"]] + lay["scale_res"],
                      writes=[ares0, ares1], tag=nm + "exp")
                fixes(c0, v.AT[:, abase:abase + 512], ares0, it)
                if c1 is not None:
                    fixes(c1, v.AT[:, abase + 512:abase + 1024], ares1, it)
                if b > 0:
                    emit_av(b - 1)
            emit_av(npair - 1)
            if g.cfg.get("skip_norm"):
                p.copy(v.CTX[0:64, hh, :], ctx_ps[0:64, :], reads=[psr(*ci_bank)],
                       writes=[f"CTXn{hh}"], tag=nm + "cp")
                continue
            p.copy(g.RS[64:65, 1, 0:512], ctx_ps[64:65, :], reads=[psr(*ci_bank)],
                   writes=["sumrow"], tag=nm + "sum")
            r = gidx % 2
            cS = 1024 * r
            p.ts(g.RS[64:65, 2, 0:512], g.RS[64:65, 1, 0:512], 1e-30, ALU.max,
                 reads=["sumrow"], writes=["RS2"], tag=nm + "smx")
            p.act(g.RS[64:65, 2, 0:512], g.RS[64:65, 2, 0:512], AF.Ln,
                  reads=["RS2"], writes=["RS2"], tag=nm + "rln")
            p.act(g.RS[64:65, 3, 0:512], g.RS[64:65, 2, 0:512], AF.Exp, scale=-1.0,
                  reads=["RS2"], writes=["RS3"], tag=nm + "rcp")
            p.copy(g.SQT[64:65, cS:cS + 512], g.RS[64:65, 3, 0:512],
                   reads=["RS3"], writes=[f"SQTr{r}"], tag=nm + "rhi")
            p.copy(g.RS[64:65, 2, 0:512], g.SQT[64:65, cS:cS + 512],
                   reads=[f"SQTr{r}"], writes=["RS2"], tag=nm + "rhf")
            p.tt(g.RS[64:65, 3, 0:512], g.RS[64:65, 3, 0:512],
                 g.RS[64:65, 2, 0:512], ALU.subtract,
                 reads=["RS3", "RS2"], writes=["RS3"], tag=nm + "rlo")
            p.copy(g.SQT[64:65, cS + 512:cS + 1024], g.RS[64:65, 3, 0:512],
                   reads=["RS3"], writes=[f"SQTr{r}"], tag=nm + "rlb")

            def make_norm(h2=hh, r=r, cS=cS, q0t=q0t):
                cb = CTXB[(q0t * nh + h2) % 3]

                def run():
                    p.mm(g.bank(3, 1)[0:64, :], g.SMB[64:65, 0, 0:64],
                         g.SQT[64:65, cS:cS + 512], True, False,
                         reads=[f"SQTr{r}", "SMB0"], writes=[psr(3, 1)],
                         tag=nm + "rm")
                    p.mm(g.bank(3, 1)[0:64, :], g.SMB[64:65, 0, 0:64],
                         g.SQT[64:65, cS + 512:cS + 1024], False, True,
                         reads=[f"SQTr{r}", "SMB0"], writes=[psr(3, 1)],
                         tag=nm + "rm")
                    p.copy(g.RS[0:64, 1, 0:512], g.bank(3, 1)[0:64, :],
                           reads=[psr(3, 1)], writes=["rme"], tag=nm + "rme")
                    p.tt(v.CTX[0:64, h2, :], g.bank(*cb)[0:64, :],
                         g.RS[0:64, 1, 0:512], ALU.mult,
                         reads=[psr(*cb), "rme"], writes=[f"CTXn{h2}"],
                         tag=nm + "cn")
                return run
            queue.append(("norm", gidx, make_norm()))
        if not do_wo:
            continue

        def make_wo(q0=q0, nm=nm, q0t=q0t):
            s_late = ((q0t + 1) * nh + 2) % 3
            banks = [(3, 1), CTXB[s_late]]

            def run():
                for d in range(8):
                    tb = banks[d % 2]
                    yb = g.bank(*tb)
                    yres = psr(*tb)
                    last_h = not (dense and lay["bo"])
                    for hh2 in range(nh):
                        p.mm(yb, lay["wo_ap"](d, hh2), v.CTX[0:64, hh2, :], hh2 == 0,
                             (hh2 == nh - 1) and last_h,
                             reads=[lay["wo_res"], f"CTXn{hh2}"],
                             writes=[yres], tag=nm + "wo")
                    if dense and lay["bo"]:
                        p.mm(yb, lay["bohl"][0:2, 128 * d:128 * d + 128],
                             g.SMB[0:2, 10:14, :].rearrange("p a b -> p (a b)"),
                             False, True,
                             reads=[lay["bohl_res"], "SMBw"], writes=[yres],
                             tag=nm + "bo")
                    p.tt(g.XT[:, d, 1 + q0:1 + q0 + 512], yb,
                         g.XT[:, d, 1 + q0:1 + q0 + 512], ALU.add,
                         reads=[yres, f"XT{d}"], writes=[f"XT{d}"], tag=nm + "res")
            return run

        queue.append(("wo", q0t, make_wo()))
    do_flush(None)


def dense_chunks(q0t):
    nch = 4 * q0t + 4
    plains, fixy = [], []
    for kc in range(nch):
        j = kc - (nch - 4)
        if j >= 0:
            fixy.append((kc, 128 * j, 512, "diag", 128 * j))
        elif kc == nch - 5:
            fixy.append((kc, 0, 512, "corner", 0))
        else:
            plains.append((kc, 0, 512, "plain", 0))
    return plains[:1] + fixy + plains[1:]


def local_chunks(q0t):
    out = []
    for jj in [0, -4, -3, -2, -1, 1, 2, 3]:
        kc = 4 * q0t + jj
        if kc < 0:
            continue
        cmin = max(0, 128 * jj)
        cmax = min(512, 128 * jj + 640)
        if jj <= -1:
            out.append((kc, cmin, cmax, "tri_window", cmax - 128))
        else:
            out.append((kc, cmin, cmax, "tri_causal", cmin))
    return out


def dense_phase(p, g, bohl_ap):
    v = pool_attn(g, local=False)
    p.memset(g.ps[0][:, :], 0.0, writes=[psr(0, 0), psr(0, 1)], tag="ps0z")
    p.memset(g.ps[2][:, :], 0.0, writes=[psr(2, 0), psr(2, 1)], tag="ps2z")
    parts = g.cfg.get("dense_parts", {"proj", "bias", "attn"})
    for grp in range(g.cfg.get("dense_groups", 4)):
        wq = wbq(g, 0, 2048).rearrange("p (c m j) -> p c m j", c=8, m=2)
        wk = wbq(g, 2048, 4096).rearrange("p (c m j) -> p c m j", c=8, m=2)
        wv = wbq(g, 4096, 6144).rearrange("p (c j) -> p c j", c=8)
        p.dma("gpsimd", wq, g.wqT[grp], wres(0, 2048), wres(0, 2048), tag="ldwq")
        p.dma("gpsimd", wk, g.wkT[grp], wres(2048, 4096), wres(2048, 4096), tag="ldwk")
        p.dma("gpsimd", wv, g.wvP[grp], wres(4096, 6144), wres(4096, 6144), tag="ldwv")
        p.dma("gpsimd", g.WB3[:, :, :, :], g.woP[:, 4 * grp:4 * grp + 4, :, :],
              ["WB3"], ["WB3"], tag="ldwo")
        for m in range(2):
            for t4 in range(4):
                sl = slice(1 + 512 * t4, 1 + 512 * t4 + 512)
                bq, bk = g.bank(0, t4 % 2), g.bank(1, t4 % 2)
                for c in range(8):
                    p.mm(bq, wq[:, c, m, :], g.HF[:, c, sl], c == 0, c == 7,
                         reads=wres(0, 2048) + [f"HF{c}"], writes=[psr(0, t4 % 2)],
                         tag="qprj")
                    p.mm(bk, wk[:, c, m, :], g.HF[:, c, sl], c == 0, c == 7,
                         reads=wres(2048, 4096) + [f"HF{c}"], writes=[psr(1, t4 % 2)],
                         tag="kprj")
                p.copy(v.QB[:, m, 512 * t4:512 * t4 + 512], bq, reads=[psr(0, t4 % 2)],
                       writes=["QB"], engine="scalar", tag="qev")
                p.copy(v.KB[:, m, 512 * t4:512 * t4 + 512], bk, reads=[psr(1, t4 % 2)],
                       writes=["KB"], engine="scalar", tag="kev")
        p.memset(v.VB[:, :, :].rearrange("p k (h j) -> p k h j", h=4)[:, :, :, 64:65],
                 1.0, writes=["VB"], tag="vones")
        for tt in range(16):
            bv = g.bank(2, tt % 2)
            off = 1 + 128 * tt
            for c in range(8):
                p.mm(bv[:, 0:256], g.HF[:, c, off:off + 128], wv[:, c, :],
                     c == 0, c == 7, reads=wres(4096, 6144) + [f"HF{c}"],
                     writes=[psr(2, tt % 2)], tag="vprj")
            p.copy(v.VB[:, tt, :].rearrange("p (h j) -> p h j", h=4)[:, :, 0:64],
                   bv[:, 0:256].rearrange("p (h j) -> p h j", h=4), reads=[psr(2, tt % 2)],
                   writes=["VB"], engine="scalar", tag="vev")
        if "bias" not in parts:
            continue
        bias_bounds(p, g, v.QB, v.KB, "dense", f"db{grp}")
        if "attn" not in parts:
            continue
        lay = dict(tag=f"dA{grp}", dense=True, scale=lambda g: g.SMALL[:, 0, 0:1],
                   scale_res=["SMa0"],
                   bias_ap=lambda g, hh, q0t: g.SMALL[:, 4, 4 * hh + q0t:4 * hh + q0t + 1],
                   bias_res="BIAS", chunks=dense_chunks,
                   wo_ap=lambda d, hh: g.WB3[0:64, hh, d, :], wo_res="WB3",
                   bo=(grp == 0), bohl=bohl_ap, bohl_res="MCst")
        attn_group(p, g, v, lay)


def ln_compute(p, g, shifted, stage_bf, stage_res, tag):
    RSALL = ["RS0", "RS1", "RS2", "RS3"]
    for d in range(8):
        p.copy(g.HF[:, d, 1:S + 1], g.XT[:, d, 1:S + 1], reads=[f"XT{d}"],
               writes=[f"HF{d}"], tag=tag + "hf")
    for d in range(8):
        off = 1 if (not shifted or d < 4) else 0
        rd = [f"HF{d}", "HFpad"] if shifted else [f"HF{d}"]
        p.tt(g.SQT[:, :], g.HF[:, d, off:off + S], g.HF[:, d, off:off + S],
             ALU.mult, reads=rd, writes=["SQT"], tag=tag + "sq")
        for t4 in range(4):
            p.mm(g.bank(t4 // 2, t4 % 2)[0:1, :], g.SMB[:, 0, 0:1],
                 g.HF[:, d, off + 512 * t4:off + 512 * t4 + 512], d == 0, d == 7,
                 reads=rd + ["SMB0"], writes=[psr(t4 // 2, t4 % 2)], tag=tag + "sum")
            p.mm(g.bank(2 + t4 // 2, t4 % 2)[0:1, :], g.SMB[:, 0, 0:1],
                 g.SQT[:, 512 * t4:512 * t4 + 512], d == 0, d == 7,
                 reads=["SQT", "SMB0"], writes=[psr(2 + t4 // 2, t4 % 2)],
                 tag=tag + "sqs")
    r0 = g.RSF[0:1, :]
    for h in range(2):
        for j in range(2):
            t4 = 2 * h + j
            p.copy(r0[:, 512 * j:512 * j + 512], g.bank(t4 // 2, t4 % 2)[0:1, :],
                   reads=[psr(t4 // 2, t4 % 2)], writes=RSALL, tag=tag + "se")
            p.copy(r0[:, 1024 + 512 * j:1024 + 512 * j + 512],
                   g.bank(2 + t4 // 2, t4 % 2)[0:1, :],
                   reads=[psr(2 + t4 // 2, t4 % 2)], writes=RSALL, tag=tag + "qe")
        mu = r0[:, 0:1024]
        ms = r0[:, 1024:2048]
        p.ts(mu, mu, 1.0 / DM, ALU.mult, reads=RSALL, writes=RSALL, tag=tag + "mu")
        p.ts(ms, ms, 1.0 / DM, ALU.mult, reads=RSALL, writes=RSALL, tag=tag + "ms")
        for j in range(2):
            tb = g.bank(0, 0)[0:1, :]
            p.tt(tb, mu[:, 512 * j:512 * j + 512], mu[:, 512 * j:512 * j + 512],
                 ALU.mult, reads=RSALL, writes=[psr(0, 0)], tag=tag + "m2")
            p.tt(ms[:, 512 * j:512 * j + 512], ms[:, 512 * j:512 * j + 512], tb,
                 ALU.subtract, reads=RSALL + [psr(0, 0)], writes=RSALL,
                 tag=tag + "var")
        p.ts(ms, ms, 1e-5, ALU.add, reads=RSALL, writes=RSALL, tag=tag + "eps")
        p.act(ms, ms, AF.Ln, reads=RSALL, writes=RSALL, tag=tag + "ln")
        p.act(ms, ms, AF.Exp, scale=-0.5, reads=RSALL, writes=RSALL, tag=tag + "rs")
        p.tt(mu, mu, ms, ALU.mult, reads=RSALL, writes=RSALL, tag=tag + "b0")
        p.ts(mu, mu, -1.0, ALU.mult, reads=RSALL, writes=RSALL, tag=tag + "b1")
        hc = slice(1024 * h, 1024 * h + 1024)
        p.copy(stage_bf[0:1, 0:1024], ms, reads=RSALL, writes=[stage_res],
               tag=tag + "ahi")
        p.dma("sync", g.SQT[32:33, hc], stage_bf[0:1, 0:1024], [stage_res],
              ["SQT"], tag=tag + "ahm")
        for j in range(2):
            tb = g.bank(0, 0)[0:1, :]
            p.copy(tb, stage_bf[0:1, 512 * j:512 * j + 512], reads=[stage_res],
                   writes=[psr(0, 0)], tag=tag + "ahf")
            p.tt(ms[:, 512 * j:512 * j + 512], ms[:, 512 * j:512 * j + 512], tb,
                 ALU.subtract, reads=RSALL + [psr(0, 0)], writes=RSALL,
                 tag=tag + "alo")
        p.copy(stage_bf[0:1, 0:1024], ms, reads=RSALL, writes=[stage_res],
               tag=tag + "alb")
        p.dma("sync", g.SQT[33:34, hc], stage_bf[0:1, 0:1024], [stage_res],
              ["SQT"], tag=tag + "alm")
        p.copy(g.SQT[0:1, hc], mu, reads=RSALL, writes=["SQT"], tag=tag + "bhi")
        for j in range(2):
            tb = g.bank(0, 0)[0:1, :]
            p.copy(tb, g.SQT[0:1, 1024 * h + 512 * j:1024 * h + 512 * j + 512],
                   reads=["SQT"], writes=[psr(0, 0)], tag=tag + "bhf")
            p.tt(mu[:, 512 * j:512 * j + 512], mu[:, 512 * j:512 * j + 512], tb,
                 ALU.subtract, reads=RSALL + [psr(0, 0)], writes=RSALL,
                 tag=tag + "blo")
        p.copy(stage_bf[0:1, 0:1024], mu, reads=RSALL, writes=[stage_res],
               tag=tag + "blb")
        p.dma("sync", g.SQT[1:2, hc], stage_bf[0:1, 0:1024], [stage_res],
              ["SQT"], tag=tag + "blm")
    for t4 in range(4):
        p.mm(g.bank(t4 // 2, t4 % 2), g.SMB[32:34, 0, :],
             g.SQT[32:34, 512 * t4:512 * t4 + 512], True, True,
             reads=["SQT", "SMB0"], writes=[psr(t4 // 2, t4 % 2)], tag=tag + "Am")
        p.mm(g.bank(2 + t4 // 2, t4 % 2), g.SMB[0:2, 0, :],
             g.SQT[0:2, 512 * t4:512 * t4 + 512], True, True,
             reads=["SQT", "SMB0"], writes=[psr(2 + t4 // 2, t4 % 2)], tag=tag + "Bm")


def AB_aps(g):
    Am = [g.bank(t4 // 2, t4 % 2) for t4 in range(4)]
    Bm = [g.bank(2 + t4 // 2, t4 % 2) for t4 in range(4)]
    Ar = [psr(t4 // 2, t4 % 2) for t4 in range(4)]
    Br = [psr(2 + t4 // 2, t4 % 2) for t4 in range(4)]
    return Am, Bm, Ar, Br


def outer_ln(p, g, spill, stage_bf, stage_res, tag):
    ln_compute(p, g, False, stage_bf[0:1, 0:2048], stage_res, tag)
    Am, Bm, Ar, Br = AB_aps(g)
    for d in range(8):
        for t4 in range(4):
            sl = slice(1 + 512 * t4, 1 + 512 * t4 + 512)
            p.tt(g.XT[:, d, sl], g.XT[:, d, sl], Am[t4], ALU.mult,
                 reads=[f"XT{d}", Ar[t4]], writes=[f"XT{d}"], tag=tag + "ap1")
            p.tt(g.XT[:, d, sl], g.XT[:, d, sl], Bm[t4], ALU.add,
                 reads=[f"XT{d}", Br[t4]], writes=[f"XT{d}"], tag=tag + "ap2")
        p.copy(g.HF[:, d, 1:S + 1], g.XT[:, d, 1:S + 1], reads=[f"XT{d}"],
               writes=[f"HF{d}"], tag=tag + "hf2")
        if spill:
            p.dma("sync", g.x1spill[:, S * d:S * d + S], g.XT[:, d, 1:S + 1],
                  [f"XT{d}"], [f"spill{d}"], tag=tag + "sp")


def local_qk_prep(p, g, v, l, grp, side, stage="all"):
    buf = v.QB if side == "q" else v.KB
    res = "QB" if side == "q" else "KB"
    if stage == "rot":
        return local_qk_rot(p, g, v, l, grp, side, buf, res)
    base_col = (0 if side == "q" else 512) + 256 * grp
    wr_lo, wr_hi = (0, 2048) if side == "q" else (2048, 4096)
    scol0 = getattr(g, f"{'qs' if side == 'q' else 'ks'}H{l}")
    p.dma("sync", g.SMALL[:, 5, 3:4], scol0, ["SMa5c"], ["SMa5c"], tag=f"{side}qs0")
    p.tt(g.SMALL[:, 5, 2:3], g.SMALL[:, 5, 3:4], g.SMALL[:, 5, 3:4], ALU.mult,
         reads=["SMa5c"], writes=["SMa5b"], tag=f"{side}qs2")
    p.ts(g.SMB[:, 15, 0:8].rearrange("p (a b) -> p a b", a=2),
         g.SMB[:, 4, 0:8].rearrange("p (a b) -> p a b", a=2),
         g.SMALL[:, 5, 2:3], ALU.mult, reads=["SMB4", "SMa5b"], writes=["SMB15"],
         tag=f"{side}hsq")
    wsl = wbq(g, wr_lo, wr_hi).rearrange("p (c j) -> p c j", c=8)
    p.dma("gpsimd", wsl, getattr(g, f"wqkvP{l}")[0 if side == "q" else 1, grp],
          wres(wr_lo, wr_hi), wres(wr_lo, wr_hi), tag=f"ldw{side}")
    for m in range(2):
        for t4 in range(4):
            b = g.bank(0, t4 % 2)
            for c in range(8):
                off = 1 if c < 4 else 0
                p.mm(b, wsl[:, c, 128 * m:128 * m + 128],
                     g.HF[:, c, off + 512 * t4:off + 512 * t4 + 512], c == 0, c == 7,
                     reads=wres(wr_lo, wr_hi) + [f"HF{c}", "HFpad"],
                     writes=[psr(0, t4 % 2)], tag=f"{side}prj")
            p.copy(buf[:, m, 512 * t4:512 * t4 + 512], b, reads=[psr(0, t4 % 2)],
                   writes=[res], engine="scalar", tag=f"{side}ev")
    for m in range(2):
        p.tt(g.SQT[:, :], buf[:, m, :], buf[:, m, :], ALU.mult, reads=[res],
             writes=["SQT"], tag=f"{side}sq")
        for t4 in range(4):
            i, half = 2 + t4 // 2, t4 % 2
            p.mm(g.bank(i, half)[0:4, :],
                 g.SMB[:, 4, 0:8].rearrange("p (a b) -> p a b", a=2)[:, m, :],
                 g.SQT[:, 512 * t4:512 * t4 + 512], m == 0, m == 1,
                 reads=["SQT", "SMB4"], writes=[psr(i, half)], tag=f"{side}n2")
            p.mm(g.bank(t4 // 2, t4 % 2)[0:4, :],
                 g.SMB[:, 15, 0:8].rearrange("p (a b) -> p a b", a=2)[:, m, :],
                 g.SQT[:, 512 * t4:512 * t4 + 512], m == 0, m == 1,
                 reads=["SQT", "SMB15"], writes=[psr(t4 // 2, t4 % 2)],
                 tag=f"{side}n2w")
    rsv = g.RSF[0:4, :]
    for t4 in range(4):
        i, half = 2 + t4 // 2, t4 % 2
        p.ts(g.RS[0:4, t4, 0:512], g.bank(i, half)[0:4, :], 1e-24, ALU.max,
             reads=[psr(i, half)], writes=[f"RS{t4}"], tag=f"{side}mx")
    p.act(rsv, rsv, AF.Ln, reads=["RS0", "RS1", "RS2", "RS3"],
          writes=["RS0", "RS1", "RS2", "RS3"], tag=f"{side}ln")
    p.act(rsv, rsv, AF.Exp, scale=-0.5, reads=["RS0", "RS1", "RS2", "RS3"],
          writes=["RS0", "RS1", "RS2", "RS3"], tag=f"{side}ex")
    p.tt(v.PERM[0:4, :], rsv, rsv, ALU.mult, reads=["RS0", "RS1", "RS2", "RS3"],
         writes=["PERM"], tag=f"{side}rn2")
    c0 = 0 if side == "q" else 8
    for t4 in range(4):
        bk = g.bank(t4 // 2, t4 % 2)
        p.tt(bk[0:4, :], bk[0:4, :], v.PERM[0:4, 512 * t4:512 * t4 + 512], ALU.mult,
             reads=[psr(t4 // 2, t4 % 2), "PERM"], writes=[psr(t4 // 2, t4 % 2)],
             tag=f"{side}n2t")
        p.red(g.SMALL[0:4, 8, c0 + t4:c0 + t4 + 1], bk[0:4, :], ALU.max,
              reads=[psr(t4 // 2, t4 % 2)], writes=[f"n2t{side}{t4}"],
              tag=f"{side}n2m")
    p.red(g.SMALL[0:4, 8, c0 + 4:c0 + 5], g.SMALL[0:4, 8, c0:c0 + 4], ALU.max,
          reads=[f"n2t{side}{t}" for t in range(4)], writes=["n2gq" if side == "q" else "n2gk"],
          tag=f"{side}n2g")
    p.copy(g.SQT[0:4, :], rsv, reads=["RS0", "RS1", "RS2", "RS3"], writes=["SQT"],
           tag=f"{side}hi")
    for t4 in range(4):
        p.copy(g.bank(2 + t4 // 2, t4 % 2)[0:4, :], g.SQT[0:4, 512 * t4:512 * t4 + 512],
               reads=["SQT"], writes=[psr(2 + t4 // 2, t4 % 2)], tag=f"{side}hf")
        p.tt(g.RS[0:4, t4, 0:512], g.RS[0:4, t4, 0:512],
             g.bank(2 + t4 // 2, t4 % 2)[0:4, :], ALU.subtract,
             reads=[f"RS{t4}", psr(2 + t4 // 2, t4 % 2)], writes=[f"RS{t4}"],
             tag=f"{side}lo")
    p.copy(v.PERM[0:4, :], rsv, reads=["RS0", "RS1", "RS2", "RS3"], writes=["PERM"],
           tag=f"{side}lb")
    for m in range(2):
        rb = 32 * m if side == "q" else 64 + 4 * m
        p.dma("sync", v.AT2[rb + 0:rb + 1, :], g.SQT[2 * m:2 * m + 1, :], ["SQT"],
              [f"AT1{side}"], tag=f"{side}p1")
        p.dma("sync", v.AT2[rb + 1:rb + 2, :], v.PERM[2 * m:2 * m + 1, :], ["PERM"],
              [f"AT1{side}"], tag=f"{side}p2")
        p.dma("sync", v.AT2[rb + 2:rb + 3, :], g.SQT[2 * m + 1:2 * m + 2, :], ["SQT"],
              [f"AT1{side}"], tag=f"{side}p3")
        p.dma("sync", v.AT2[rb + 3:rb + 4, :], v.PERM[2 * m + 1:2 * m + 2, :], ["PERM"],
              [f"AT1{side}"], tag=f"{side}p4")


def local_qk_rot(p, g, v, l, grp, side, buf, res):
    csH = g.CSqH if side == "q" else g.CSkH
    ssH = g.SSqH if side == "q" else g.SSkH
    scol = getattr(g, f"{'qs' if side == 'q' else 'ks'}H{l}")
    spcol = getattr(g, f"{'qsp' if side == 'q' else 'ksp'}H{l}")
    p.dma("sync", v.CS1, csH, ["CS1"], ["CS1"], tag=f"{side}cs")
    p.dma("sync", v.CS2, ssH, ["CS2"], ["CS2"], tag=f"{side}ss")
    p.dma("sync", g.SMALL[:, 5, 0:1], scol, ["SMa5"], ["SMa5"], tag=f"{side}qs")
    p.dma("sync", g.SMALL[:, 5, 1:2], spcol, ["SMa5"], ["SMa5"], tag=f"{side}qsp")
    p.ts(v.CS1, v.CS1, g.SMALL[:, 5, 0:1], ALU.mult, reads=["CS1", "SMa5"],
         writes=["CS1"], tag=f"{side}cf")
    p.ts(v.CS2, v.CS2, g.SMALL[:, 5, 1:2], ALU.mult, reads=["CS2", "SMa5"],
         writes=["CS2"], tag=f"{side}sf")
    for m in range(2):
        rb = 32 * m
        for blk, src in ((0, 32), (32, 0), (64, 96), (96, 64)):
            p.dma("sync", v.PERM[blk:blk + 32, :], buf[src:src + 32, m, :],
                  [res], ["PERM"], tag=f"{side}pm")
        p.tt(g.RSF, buf[:, m, :], v.CS1, ALU.mult, reads=[res, "CS1"],
             writes=["RS0", "RS1", "RS2", "RS3"], tag=f"{side}u")
        for half in range(2):
            p.tt(g.ps[half][:, :], v.PERM[:, 1024 * half:1024 * half + 1024],
                 v.CS2[:, 1024 * half:1024 * half + 1024], ALU.mult,
                 reads=["PERM", "CS2"], writes=[psr(half, 0), psr(half, 1)],
                 tag=f"{side}u2")
            p.tt(g.RSF[:, 1024 * half:1024 * half + 1024],
                 g.RSF[:, 1024 * half:1024 * half + 1024],
                 g.ps[half][:, :], ALU.add,
                 reads=["RS0", "RS1", "RS2", "RS3", psr(half, 0), psr(half, 1)],
                 writes=["RS0", "RS1", "RS2", "RS3"], tag=f"{side}s")
        for t4 in range(4):
            if side == "q":
                lhs = g.SMB[0:4, 9, :] if rb == 0 else g.SMB[32:36, 9, :]
                rhs = v.AT2[rb:rb + 4, 512 * t4:512 * t4 + 512]
                sres = "SMB9"
            else:
                lhs = g.SMB[64:72, 9, :] if rb == 0 else g.SMB[64:72, 14, :]
                rhs = v.AT2[64:72, 512 * t4:512 * t4 + 512]
                sres = "SMB9k"
            p.mm(g.bank(3, 1), lhs, rhs,
                 True, True, reads=[f"AT1{side}", sres], writes=[psr(3, 1)],
                 tag=f"{side}rm")
            p.tt(buf[:, m, 512 * t4:512 * t4 + 512],
                 g.RSF[:, 512 * t4:512 * t4 + 512], g.bank(3, 1), ALU.mult,
                 reads=["RS0", "RS1", "RS2", "RS3", psr(3, 1)], writes=[res],
                 tag=f"{side}qh")


def local_bias(p, g, tag):
    p.ts(g.SMALL[0:4, 8, 13:14], g.SMALL[0:4, 8, 4:5], g.SMALL[0:4, 8, 12:13],
         ALU.mult, reads=["n2gq", "n2gk"], writes=["lbb"], tag=tag + "p")
    p.act(g.SMALL[0:4, 8, 14:15], g.SMALL[0:4, 8, 13:14], AF.Ln, reads=["lbb"],
          writes=["lbb1"], tag=tag + "ln")
    p.act(g.SMALL[0:4, 8, 15:16], g.SMALL[0:4, 8, 14:15], AF.Exp, scale=0.5,
          reads=["lbb1"], writes=["lbb2"], tag=tag + "ex")
    p.ts(g.SMALL[0:4, 8, 16:17], g.SMALL[0:4, 8, 15:16], -16.0, ALU.mult,
         reads=["lbb2"], writes=["lnegb"], tag=tag + "nb")
    p.copy(g.SMB[0:4, 7, 0:1], g.SMALL[0:4, 8, 16:17], reads=["lnegb"],
           writes=["SMB7"], tag=tag + "nbb")
    p.dma("sync", g.SMB[0:1, 8, 0:4].rearrange("p (a b) -> p a b", a=4),
          g.SMB[0:4, 7, 0:1], ["SMB7"], ["SMB8"], tag=tag + "nbr")
    p.mm(g.bank(3, 0)[:, 0:4], g.SMB[0:1, 0, :], g.SMB[0:1, 8, 0:4], True, True,
         reads=["SMB0", "SMB8"], writes=[psr(3, 0)], tag=tag + "bm")
    p.copy(g.SMALL[:, 9, 0:4], g.bank(3, 0)[:, 0:4], reads=[psr(3, 0)],
           writes=["BIASL"], tag=tag + "bev")


def local_phase(p, g, l):
    v = pool_attn(g, local=True)
    p.memset(g.ps[0][:, :], 0.0, writes=[psr(0, 0), psr(0, 1)], tag="ps0z")
    p.memset(g.ps[2][:, :], 0.0, writes=[psr(2, 0), psr(2, 1)], tag="ps2z")
    if l > 0:
        for d in range(8):
            p.copy(g.HF[:, d, 1:S + 1], g.XT[:, d, 1:S + 1], reads=[f"XT{d}"],
                   writes=[f"HF{d}"], tag=f"lhf{l}")
    p.dma("gpsimd", g.SMB[0:4, 9, :], g.SEL4H, ["SMB9"], ["SMB9"], tag="sel0")
    p.dma("gpsimd", g.SMB[32:36, 9, :], g.SEL4H, ["SMB9"], ["SMB9"], tag="sel1")
    p.dma("gpsimd", g.SMB[64:72, 9, :], g.SEL8AH, ["SMB9k"], ["SMB9k"], tag="sel2")
    p.dma("gpsimd", g.SMB[64:72, 14, :], g.SEL8BH, ["SMB9k"], ["SMB9k"], tag="sel3")
    for grp in range(2):
        local_qk_prep(p, g, v, l, grp, "q", "front")
        local_qk_prep(p, g, v, l, grp, "k", "front")
        local_qk_prep(p, g, v, l, grp, "q", "rot")
        local_qk_prep(p, g, v, l, grp, "k", "rot")
        wv = wbq(g, 4096, 6144).rearrange("p (c j) -> p c j", c=8)
        p.dma("gpsimd", wv, getattr(g, f"wqkvP{l}")[2, grp],
              wres(4096, 6144), wres(4096, 6144), tag="ldwvl")
        p.memset(v.VB[:, :, :].rearrange("p k (h j) -> p k h j", h=4)[:, :, :, 64:65],
                 1.0, writes=["VB"], tag="vones")
        for tt in range(16):
            bv = g.bank(2, tt % 2)
            for c in range(8):
                off = 1 if c < 4 else 0
                p.mm(bv[:, 0:256], g.HF[:, c, off + 128 * tt:off + 128 * tt + 128],
                     wv[:, c, :], c == 0, c == 7,
                     reads=wres(4096, 6144) + [f"HF{c}", "HFpad"],
                     writes=[psr(2, tt % 2)], tag="vprjl")
            p.copy(v.VB[:, tt, :].rearrange("p (h j) -> p h j", h=4)[:, :, 0:64],
                   bv[:, 0:256].rearrange("p (h j) -> p h j", h=4),
                   reads=[psr(2, tt % 2)], writes=["VB"], engine="scalar", tag="vevl")
        p.dma("gpsimd", g.WB3[:, :, :, :], getattr(g, f"wolP{l}")[:, 4 * grp:4 * grp + 4, :, :],
              ["WB3"], ["WB3"], tag="ldwol")
        if g.cfg.get("dump_qk") == (l, grp):
            p.dma("sync", g.dumpQ, v.QB[:, :, :], ["QB"], [], tag="dq")
            p.dma("sync", g.dumpK, v.KB[:, :, :], ["KB"], [], tag="dk")
            p.dma("sync", g.dumpV, v.VB[:, :, :], ["VB"], [], tag="dv")
        local_bias(p, g, f"lb{l}{grp}")
        lay = dict(tag=f"lA{l}{grp}", dense=False, scale=lambda g: 8.0,
                   scale_res=[],
                   bias_ap=lambda g, hh, q0t: g.SMALL[:, 9, hh:hh + 1],
                   bias_res="BIASL", chunks=local_chunks,
                   wo_ap=lambda d, hh: g.WB3[0:64, hh, d, :], wo_res="WB3",
                   bo=False)
        attn_group(p, g, v, lay)


def ffn_phase(p, g, l):
    v = pool_ffn(g)
    p.dma("sync", g.SMALL[:, 6, 0:8], getattr(g, f"lgH{l}"), ["SMa6"], ["SMa6"],
          tag="lg")
    p.dma("sync", g.SMALL[:, 7, 0:8], getattr(g, f"lbH{l}"), ["SMa7"], ["SMa7"],
          tag="lb")
    p.dma("sync", g.SMALL[:, 8, 0:22], getattr(g, f"b1gH{l}"), ["SMa8"], ["SMa8"],
          tag="b1g")
    ln_compute(p, g, True, v.ST[0:1, 0:2048], "MCst", f"il{l}")
    Am, Bm, Ar, Br = AB_aps(g)
    for d in range(8):
        off = 1 if d < 4 else 0
        for t4 in range(4):
            slx = slice(off + 512 * t4, off + 512 * t4 + 512)
            slh = slice(1 + 512 * t4, 1 + 512 * t4 + 512)
            p.tt(g.RSF[:, 0:512], g.XT[:, d, slx], Am[t4], ALU.mult,
                 reads=[f"XT{d}", Ar[t4]], writes=["RS0"], tag="iap1")
            p.tt(g.RSF[:, 0:512], g.RSF[:, 0:512], Bm[t4], ALU.add,
                 reads=["RS0", Br[t4]], writes=["RS0"], tag="iap2")
            p.ts(g.HF[:, d, slh], g.RSF[:, 0:512], g.SMALL[:, 6, d:d + 1], ALU.mult,
                 reads=["RS0", "SMa6", "SMa7"], writes=[f"HF{d}"],
                 s2=g.SMALL[:, 7, d:d + 1], op1=ALU.add, tag="iap3")
    b1hl = stage_bias_hilo(p, g, getattr(g, f"b1aH{l}"), v.ST[:, 0:2816], "MCst",
                           1408, f"b1a{l}")
    b2hl = stage_bias_hilo(p, g, getattr(g, f"b2H{l}"), v.ST[:, 2816:3840], "MCst",
                           512, f"b2{l}")
    for th in range(2):
        tok = slice(1 + 1024 * th, 1 + 1024 * th + 1024)
        for k in range(22):
            aw = min(128, 2730 - 128 * k)
            lo = 0 if k % 2 == 0 else 3072
            wag = wbq(g, lo, lo + 2048).rearrange("p (c x j) -> p c x j", c=8, x=2)
            wa = wag[:, :, 0, :]
            wg = wag[:, :, 1, :]
            p.dma("gpsimd", wag, getattr(g, f"w1P{l}")[k],
                  wres(lo, lo + 2048), wres(lo, lo + 2048), tag="ldw1")
            ia, ig = (0, 2) if k % 2 == 0 else (1, 3)
            for c in range(8):
                for half in range(2):
                    hsl = slice(1 + 1024 * th + 512 * half, 1 + 1024 * th + 512 * half + 512)
                    p.mm(g.bank(ia, half)[0:aw, :], wa[:, c, 0:aw], g.HF[:, c, hsl],
                         c == 0, False, reads=wres(lo, lo + 1024) + [f"HF{c}"],
                         writes=[psr(ia, half)], tag="w1a")
                    p.mm(g.bank(ig, half)[0:aw, :], wg[:, c, 0:aw], g.HF[:, c, hsl],
                         c == 0, c == 7, reads=wres(lo + 1024, lo + 2048) + [f"HF{c}"],
                         writes=[psr(ig, half)], tag="w1g")
            r = 0 if k < 11 else 1
            for half in range(2):
                p.mm(g.bank(ia, half)[0:aw, :],
                     b1hl[0:2, 1408 * r + 128 * (k - 11 * r):1408 * r + 128 * (k - 11 * r) + aw],
                     g.SMB[0:2, 10:14, :].rearrange("p a b -> p (a b)"), False, True,
                     reads=["MCst", "SMBw"], writes=[psr(ia, half)], tag="b1mm")
            rseg = slice(0, 1024) if k % 2 == 0 else slice(1024, 2048)
            rres = ["RS0", "RS1"] if k % 2 == 0 else ["RS2", "RS3"]
            p.act(g.RSF[:, rseg][0:aw, :], g.ps[ig][0:aw, :], AF.Gelu,
                  bias=g.SMALL[0:aw, 8, k:k + 1], reads=[psr(ig, 0), psr(ig, 1), "SMa8"],
                  writes=rres, tag="gelu")
            p.tt(v.MC[0:aw, k, :], g.ps[ia][0:aw, :], g.RSF[0:aw, rseg], ALU.mult,
                 reads=[psr(ia, 0), psr(ia, 1)] + rres, writes=["MC"],
                 tag="mtt")
        for d in range(8):
            lo = 0 if d % 2 == 0 else 3072
            wd = wbq(g, lo, lo + 2816).rearrange("p (k j) -> p k j", k=22)
            p.dma("gpsimd", wd, getattr(g, f"w2P{l}")[d],
                  wres(lo, lo + 2816), wres(lo, lo + 2816), tag="ldw2")
            ybanks = (g.ps[2] if d % 2 == 0 else g.ps[3])
            yres = [psr(2, 0), psr(2, 1)] if d % 2 == 0 else [psr(3, 0), psr(3, 1)]
            for k in range(22):
                kw = min(128, 2730 - 128 * k)
                for half in range(2):
                    p.mm(ybanks[:, 512 * half:512 * half + 512], wd[0:kw, k, :],
                         v.MC[0:kw, k, 512 * half:512 * half + 512], k == 0, False,
                         reads=wres(lo, lo + 2816) + ["MC"], writes=yres, tag="w2")
            for half in range(2):
                p.mm(ybanks[:, 512 * half:512 * half + 512],
                     b2hl[0:2, 128 * d:128 * d + 128], g.SMB[0:2, 10:14, :].rearrange("p a b -> p (a b)"),
                     False, True, reads=["MCst", "SMBw"], writes=yres, tag="b2mm")
            p.tt(g.XT[:, d, tok], ybanks[:, :], g.XT[:, d, tok], ALU.add,
                 reads=yres + [f"XT{d}"], writes=[f"XT{d}"], tag="w2res")


def final_phase(p, g):
    v = pool_ffn(g)
    for d in range(8):
        p.dma("sync", g.RSF, g.x1spill[:, S * d:S * d + S],
              ["RS0", "RS1", "RS2", "RS3"], ["RS0", "RS1", "RS2", "RS3"],
              tag="unspill")
        p.tt(g.XT[:, d, 1:S + 1], g.XT[:, d, 1:S + 1], g.RSF, ALU.add,
             reads=[f"XT{d}", "RS0", "RS1", "RS2", "RS3"], writes=[f"XT{d}"],
             tag="addx1")
    ln_compute(p, g, False, v.ST[0:1, 0:2048], "MCst", "fl")
    Am, Bm, Ar, Br = AB_aps(g)
    for d in range(8):
        for t4 in range(4):
            sl = slice(1 + 512 * t4, 1 + 512 * t4 + 512)
            p.tt(g.XT[:, d, sl], g.XT[:, d, sl], Am[t4], ALU.mult,
                 reads=[f"XT{d}", Ar[t4]], writes=[f"XT{d}"], tag="fap1")
            p.tt(g.XT[:, d, sl], g.XT[:, d, sl], Bm[t4], ALU.add,
                 reads=[f"XT{d}", Br[t4]], writes=[f"XT{d}"], tag="fap2")
        p.dma("sync", g.outT[128 * d:128 * d + 128, :], g.XT[:, d, 1:S + 1],
              [f"XT{d}"], [f"out{d}"], tag="stout")


def dump_xt(p, g):
    p.dma("sync", g.dumpT, g.XT[:, :, :].rearrange("p a b -> p (a b)"),
          [f"XT{d}" for d in range(8)] + ["XTpad"], [], tag="dumpXT")
    p.dma("sync", g.dumpHF, g.HF[:, :, :].rearrange("p a b -> p (a b)"),
          [f"HF{d}" for d in range(8)] + ["HFpad"], [], tag="dumpHF")


def build_main(nc, p, g, cfg):
    stage = cfg.get("stages", "all")
    ds = cfg.get("dump_stage")
    p.memset(g.XT[:, :, 0:1], 0.0, writes=["XTpad"], tag="xtpad")
    p.memset(g.HF[:, :, 0:1], 0.0, writes=["HFpad"], tag="hfpad")
    p.dma("sync", g.XT[:, :, 1:S + 1], g.xT.rearrange("(a p) t -> p a t", p=128),
          (), [f"XT{d}" for d in range(8)], tag="ldx")
    ld_tables(p, g)
    for d in range(8):
        p.copy(g.HF[:, d, 1:S + 1], g.XT[:, d, 1:S + 1], reads=[f"XT{d}"],
               writes=[f"HF{d}"], tag="hf0")
    vff = pool_ffn(g)
    bo_bf = vff.ST[:, 0:1024]
    bohl = stage_bias_hilo(p, g, g.boH, bo_bf, "MCst", 512, "bo")
    want = lambda s: stage == "all" or s in stage
    if want("dense"):
        dense_phase(p, g, bohl)
        if ds == "x2":
            return dump_xt(p, g)
        p.fence()
        outer_ln(p, g, spill=True, stage_bf=vff.ST, stage_res="MCst", tag="ln1")
        if ds == "x1":
            return dump_xt(p, g)
    for l in (0, 1):
        if want(f"loc{l}"):
            p.fence()
            local_phase(p, g, l)
            if ds == f"xa{l}":
                return dump_xt(p, g)
        if want(f"ffn{l}"):
            p.fence()
            ffn_phase(p, g, l)
            if ds == f"xb{l}":
                return dump_xt(p, g)
    if want("final"):
        p.fence()
        final_phase(p, g)
    else:
        for d in range(8):
            p.dma("sync", g.outT[128 * d:128 * d + 128, :], g.XT[:, d, 1:S + 1],
                  [f"XT{d}"], [f"out{d}"], tag="stout0")


def kernel(**inputs):
    import os
    cfg = {}
    abl = os.environ.get("ABL", "")
    if abl:
        import json as _json
        cfg = _json.loads(abl)
        if "stages" in cfg:
            cfg["stages"] = set(cfg["stages"])
    cores = prepack(inputs)
    nc = build_program(cfg)
    res = run_bass_kernel_spmd(nc, cores, core_ids=list(range(8)))
    return np.stack([np.asarray(r["outT"], np.float32).T for r in res.results], axis=0)


# version 17
# speedup vs baseline: 1.0145x; 1.0046x over previous
import numpy as np
import concourse.bass as bass
import concourse.mybir as mybir
from concourse.bass_utils import run_bass_kernel_spmd
from contextlib import ExitStack

F32 = mybir.dt.float32
BF16 = mybir.dt.bfloat16
U8 = mybir.dt.uint8
AF = mybir.ActivationFunctionType
ALU = mybir.AluOpType
AX = mybir.AxisListType

S = 2048
DM = 1024
ENGINES = ("tensor", "vector", "scalar", "gpsimd", "sync")


DMA_WIN = 8


class Prog:
    def __init__(self, nc):
        self.nc = nc
        self.ops = []
        self.res = {}
        self.fence_deps = set()
        self.dmas = {}

    def op(self, engine, fn, reads=(), writes=(), dma=False, tag=""):
        deps = set(self.fence_deps)
        for r in reads:
            e = self.res.get(r)
            if e and e["w"] is not None:
                deps.add(e["w"])
            if e and r.startswith("PS"):
                for ri in e["r"]:
                    if self.ops[ri]["engine"] != engine:
                        deps.add(ri)
        for w in writes:
            e = self.res.get(w)
            if e:
                if e["w"] is not None:
                    deps.add(e["w"])
                deps.update(e["r"])
        i = len(self.ops)
        if dma:
            ql = self.dmas.setdefault(engine, [])
            if len(ql) >= DMA_WIN:
                deps.add(ql[-DMA_WIN])
            ql.append(i)
        pruned = set()
        for d in deps:
            dop = self.ops[d]
            if (engine == "tensor" and dop["engine"] == "tensor"
                    and not dma and not dop["dma"]):
                continue
            pruned.add(d)
        self.ops.append(dict(engine=engine, fn=fn, dma=dma, deps=pruned, tag=tag))
        for r in reads:
            self.res.setdefault(r, {"w": None, "r": []})["r"].append(i)
        for w in writes:
            self.res[w] = {"w": i, "r": []}
        return i

    def fence(self):
        tails = set()
        for e in self.res.values():
            if e["w"] is not None:
                tails.add(e["w"])
            tails.update(e["r"])
        self.fence_deps = tails

    def finalize(self, stack, last_engine="sync"):
        nc = self.nc
        n = len(self.ops)
        needed = set()
        for o in self.ops:
            needed.update(o["deps"])
        cnt = {}
        tok = [None] * n
        qidx = {}
        for i, o in enumerate(self.ops):
            if o["dma"]:
                j = qidx.get(o["engine"], 0)
                qidx[o["engine"]] = j + 1
                k = ("q", o["engine"], j % DMA_WIN)
                cnt[k] = cnt.get(k, 0) + 16
                tok[i] = (k, cnt[k])
            elif i in needed:
                k = ("e", o["engine"])
                cnt[k] = cnt.get(k, 0) + 1
                tok[i] = (k, cnt[k])
        issue_clock = [None] * n
        stream_clock = {e: {} for e in ENGINES}
        waits = [None] * n
        for i, o in enumerate(self.ops):
            sc = stream_clock[o["engine"]]
            need = {}
            for d in sorted(o["deps"]):
                k, v = tok[d]
                if sc.get(k, 0) < v and need.get(k, 0) < v:
                    need[k] = v
            waits[i] = need
            for d in sorted(o["deps"]):
                k, v = tok[d]
                if sc.get(k, 0) < v:
                    sc[k] = v
                ic = issue_clock[d]
                if ic:
                    for kk, vv in ic.items():
                        if sc.get(kk, 0) < vv:
                            sc[kk] = vv
            issue_clock[i] = dict(sc)
        sems = {k: stack.enter_context(nc.semaphore("_".join(str(x) for x in k)))
                for k in cnt}
        block = stack.enter_context(nc.Block())
        self.n_ops = n

        def emit_for(engine):
            def run(eng):
                for i, o in enumerate(self.ops):
                    if o["engine"] != engine:
                        continue
                    for k, v in sorted(waits[i].items()):
                        eng.wait_ge(sems[k], v)
                    instr = o["fn"](eng)
                    if tok[i] is not None:
                        k, v = tok[i]
                        instr.then_inc(sems[k], 16 if o["dma"] else 1)
                if engine == last_engine:
                    for k, v in sorted(cnt.items()):
                        eng.wait_ge(sems[k], v)
            return run

        for e in ENGINES:
            getattr(block, e)(emit_for(e))

    def mm(self, out, lhsT, rhs, start, stop, reads, writes, tag=""):
        def fn(eng, out=out, lhsT=lhsT, rhs=rhs, start=start, stop=stop):
            return eng.matmul(out, lhsT, rhs, start=start, stop=stop)
        return self.op("tensor", fn, reads, writes, tag=tag)

    def tt(self, out, a, b, alu, reads, writes, engine="vector", tag=""):
        def fn(eng, out=out, a=a, b=b, alu=alu):
            return eng.tensor_tensor(out, a, b, op=alu)
        return self.op(engine, fn, reads, writes, tag=tag)

    def ts(self, out, a, s1, op0, reads, writes, s2=None, op1=None, engine="vector", tag=""):
        def fn(eng, out=out, a=a, s1=s1, s2=s2, op0=op0, op1=op1):
            if s2 is None:
                return eng.tensor_scalar(out, a, s1, None, op0=op0)
            return eng.tensor_scalar(out, a, s1, s2, op0=op0, op1=op1)
        return self.op(engine, fn, reads, writes, tag=tag)

    def act(self, out, in_, func, reads, writes, bias=None, scale=None, accum=None, tag=""):
        def fn(eng, out=out, in_=in_, func=func, bias=bias, scale=scale, accum=accum):
            kw = {}
            if bias is not None:
                kw["bias"] = bias
            if scale is not None:
                kw["scale"] = scale
            if accum is not None:
                kw["accum_out"] = accum
            return eng.activation(out, in_, func, **kw)
        return self.op("scalar", fn, reads, writes, tag=tag)

    def copy(self, out, in_, reads, writes, engine="vector", tag=""):
        if engine == "scalar":
            def fn(eng, out=out, in_=in_):
                return eng.copy(out, in_)
        else:
            def fn(eng, out=out, in_=in_):
                return eng.tensor_copy(out, in_)
        return self.op(engine, fn, reads, writes, tag=tag)

    def red(self, out, in_, alu, reads, writes, absv=False, tag=""):
        def fn(eng, out=out, in_=in_, alu=alu, absv=absv):
            return eng.tensor_reduce(out, in_, axis=AX.X, op=alu,
                                     apply_absolute_value=absv or None)
        return self.op("vector", fn, reads, writes, tag=tag)

    def recip(self, out, in_, reads, writes, tag=""):
        def fn(eng, out=out, in_=in_):
            return eng.reciprocal(out, in_)
        return self.op("vector", fn, reads, writes, tag=tag)

    def memset(self, ap, val, writes, engine="vector", tag=""):
        def fn(eng, ap=ap, val=val):
            return eng.memset(ap, val)
        return self.op(engine, fn, (), writes, tag=tag)

    def dma(self, queue, out, in_, reads, writes, tag=""):
        def fn(eng, out=out, in_=in_):
            return eng.dma_start(out=out, in_=in_)
        return self.op(queue, fn, reads, writes, dma=True, tag=tag)


def make_trig():
    d = np.arange(128) % 64
    i = (d % 32).astype(np.float64)
    t = np.arange(S, dtype=np.float64)
    inv_freq = 1.0 / (10000.0 ** ((2 * i) / 64.0))
    freqs = inv_freq[:, None] * t[None, :]
    base = ((2 * i) + 0.4 * 64) / (1.4 * 64)
    power = (t - S // 2) / 256.0
    scale = base[:, None] ** power[None, :]
    sign = np.where(d < 32, -1.0, 1.0)[:, None]
    return (np.cos(freqs) * scale, np.sin(freqs) * scale * sign,
            np.cos(freqs) / scale, np.sin(freqs) / scale * sign)


def prepack(inputs):
    f32 = np.float32
    x = np.asarray(inputs["L0_attn_x"], f32)
    Wq = np.asarray(inputs["L0_attn_Wq"], f32)
    Wk = np.asarray(inputs["L0_attn_Wk"], f32)
    Wv = np.asarray(inputs["L0_attn_Wv"], f32)
    Wo = np.asarray(inputs["L0_attn_Wo"], f32)
    bo = np.asarray(inputs["L0_attn_bo"], f32)
    rel = np.asarray(inputs["L0_attn_rel_emb"], f32)
    sh = {}
    sh["wqT"] = np.ascontiguousarray(
        Wq.reshape(8, 128, 4, 2, 128).transpose(2, 1, 0, 3, 4))
    sh["wkT"] = np.ascontiguousarray(
        Wk.reshape(8, 128, 4, 2, 128).transpose(2, 1, 0, 3, 4))
    sh["wvP"] = np.ascontiguousarray(
        Wv.reshape(8, 128, 4, 256).transpose(2, 1, 0, 3))
    sh["woP"] = np.ascontiguousarray(Wo.reshape(16, 64, 8, 128).transpose(1, 0, 2, 3))
    sh["boH"] = bo.reshape(2, 512)
    sh["relH"] = rel.reshape(1, 17)
    p = np.arange(128)[:, None]
    c = np.arange(144)[None, :]
    sh["GdH"] = rel[np.clip(c - p, 0, 16)].astype(f32)
    sh["U01dH"] = (c[:, 0:128] >= p).astype(f32)
    pp = np.arange(64)[:, None]
    cc = np.arange(16)[None, :]
    sh["GpH"] = rel[np.clip(cc + 64 - pp, 0, 16)].astype(f32)
    sh["r16H"] = np.broadcast_to(rel[16:17], (128, 1)).astype(f32).copy()
    c2 = np.arange(128)[None, :]
    sh["UcH"] = (c2 >= p).astype(f32)
    sh["UwH"] = (c2 <= p).astype(f32)
    hs = np.zeros((128, 2, 4), f32)
    for pr in range(128):
        for r in range(2):
            hs[pr, r, 2 * r + pr // 64] = 1.0
    sh["HSH"] = hs
    sel4 = np.zeros((4, 128), f32)
    sel4[0, 0:64] = 1; sel4[1, 0:64] = 1; sel4[2, 64:128] = 1; sel4[3, 64:128] = 1
    sh["SEL4H"] = sel4
    csq, ssq, csk, ssk = make_trig()
    sh["CSqH"] = csq.astype(f32); sh["SSqH"] = ssq.astype(f32)
    sh["CSkH"] = csk.astype(f32); sh["SSkH"] = ssk.astype(f32)

    wqkv = np.asarray(inputs["L0_ffn_wqkv"], f32)
    wol = np.asarray(inputs["L0_ffn_wo"], f32)
    w1 = np.asarray(inputs["L0_ffn_w1"], f32)
    w2 = np.asarray(inputs["L0_ffn_w2"], f32)
    b1 = np.asarray(inputs["L0_ffn_b1"], f32)
    b2 = np.asarray(inputs["L0_ffn_b2"], f32)
    lg = np.asarray(inputs["L0_ffn_ln_g"], f32)
    lb = np.asarray(inputs["L0_ffn_ln_b"], f32)
    qs = np.asarray(inputs["L0_ffn_q_scale"], f32)
    ks = np.asarray(inputs["L0_ffn_k_scale"], f32)
    perm = ((np.arange(128) % 64 + 32) % 64) + 64 * (np.arange(128) // 64)
    for l in (0, 1):
        wq3 = wqkv[l].reshape(8, 128, 3, 2, 256)
        sh[f"wqkvP{l}"] = np.ascontiguousarray(wq3.transpose(2, 3, 1, 0, 4))
        sh[f"wolP{l}"] = np.ascontiguousarray(wol[l].reshape(8, 64, 8, 128).transpose(1, 0, 2, 3))
        w1p = np.zeros((22, 128, 8, 2, 128), f32)
        wa_ = w1[l][:, 0:2730]
        wg_ = w1[l][:, 2730:5460]
        for k in range(22):
            awk = min(128, 2730 - 128 * k)
            w1p[k, :, :, 0, 0:awk] = wa_[:, 128 * k:128 * k + awk].reshape(8, 128, awk).transpose(1, 0, 2)
            w1p[k, :, :, 1, 0:awk] = wg_[:, 128 * k:128 * k + awk].reshape(8, 128, awk).transpose(1, 0, 2)
        sh[f"w1P{l}"] = np.ascontiguousarray(w1p)
        w2p = np.zeros((22 * 128, 1024), f32)
        w2p[0:2730] = w2[l]
        sh[f"w2P{l}"] = np.ascontiguousarray(
            w2p.reshape(22, 128, 8, 128).transpose(2, 1, 0, 3))
        b1a = np.zeros((2, 1408), f32)
        b1a.reshape(-1)[0:2730] = b1[l, 0:2730]
        sh[f"b1aH{l}"] = b1a
        b1g = np.zeros((128, 22), f32)
        for k in range(22):
            seg = b1[l, 2730 + 128 * k: 2730 + min(128 * (k + 1), 2730)]
            b1g[0:len(seg), k] = seg
        sh[f"b1gH{l}"] = b1g
        sh[f"b2H{l}"] = b2[l].reshape(2, 512)
        sh[f"lgH{l}"] = np.ascontiguousarray(lg[l].reshape(8, 128).T)
        sh[f"lbH{l}"] = np.ascontiguousarray(lb[l].reshape(8, 128).T)
        sh[f"qsH{l}"] = np.tile(qs[l], 2).reshape(128, 1).copy()
        sh[f"qspH{l}"] = np.tile(qs[l], 2)[perm].reshape(128, 1).copy()
        sh[f"ksH{l}"] = np.tile(ks[l], 2).reshape(128, 1).copy()
        sh[f"kspH{l}"] = np.tile(ks[l], 2)[perm].reshape(128, 1).copy()
    cores = []
    for b in range(8):
        d = dict(sh)
        d["xT"] = np.ascontiguousarray(x[b].T)
        cores.append(d)
    return cores


class G:
    pass


def psr(i, half):
    return f"PS{i}{'ab'[half]}"


def build_program(cfg):
    nc = bass.Bass("TRN2", target_bir_lowering=False)
    g = G()
    g.cfg = cfg

    def dram(name, shape, dtype=F32, kind="ExternalInput"):
        setattr(g, name, nc.dram_tensor(name, shape, dtype, kind=kind).ap())

    dram("xT", [DM, S])
    dram("wqT", [4, 128, 8, 2, 128]); dram("wkT", [4, 128, 8, 2, 128])
    dram("wvP", [4, 128, 8, 256]); dram("woP", [64, 16, 8, 128])
    dram("boH", [2, 512]); dram("relH", [1, 17])
    dram("GdH", [128, 144]); dram("U01dH", [128, 128]); dram("GpH", [64, 16])
    dram("r16H", [128, 1]); dram("UcH", [128, 128]); dram("UwH", [128, 128])
    dram("HSH", [128, 2, 4]); dram("SEL4H", [4, 128])
    dram("CSqH", [128, S]); dram("SSqH", [128, S])
    dram("CSkH", [128, S]); dram("SSkH", [128, S])
    for l in (0, 1):
        dram(f"wqkvP{l}", [3, 2, 128, 8, 256]); dram(f"wolP{l}", [64, 8, 8, 128])
        dram(f"w1P{l}", [22, 128, 8, 2, 128]); dram(f"w2P{l}", [8, 128, 22, 128])
        dram(f"b1aH{l}", [2, 1408]); dram(f"b1gH{l}", [128, 22]); dram(f"b2H{l}", [2, 512])
        dram(f"lgH{l}", [128, 8]); dram(f"lbH{l}", [128, 8])
        dram(f"qsH{l}", [128, 1]); dram(f"qspH{l}", [128, 1])
        dram(f"ksH{l}", [128, 1]); dram(f"kspH{l}", [128, 1])
    dram("x1spill", [128, 8 * S], kind="Internal")
    dram("outT", [DM, S], kind="ExternalOutput")
    g.dump_stage = cfg.get("dump_stage")
    if g.dump_stage:
        dram("dumpT", [128, 8 * (S + 1)], kind="ExternalOutput")
        dram("dumpHF", [128, 8 * (S + 1)], BF16, kind="ExternalOutput")
    if cfg.get("dump_qk"):
        dram("dumpQ", [128, 2, S], BF16, kind="ExternalOutput")
        dram("dumpK", [128, 2, S], BF16, kind="ExternalOutput")
        dram("dumpV", [128, 16, 260], BF16, kind="ExternalOutput")

    st = ExitStack()
    en = st.enter_context

    def sb(name, shape, dtype):
        setattr(g, name, en(nc.sbuf_tensor(name, shape, dtype)))

    sb("XT", [128, 8, S + 1], F32)
    sb("HF", [128, 8, S + 1], BF16)
    sb("SQT", [128, S], BF16)
    sb("RS", [128, 4, 512], F32)
    sb("SMALL", [128, 16, 32], F32)
    sb("SMB", [128, 16, 128], BF16)
    sb("TBL", [128, 208], F32)
    sb("WBQ", [128, 6144], BF16)
    sb("WB3", [64, 4, 8, 128], BF16)
    sb("POOL", [128, 57472], U8)
    g.ps = [en(nc.psum_tensor(f"PS{i}", [128, 1024], F32)) for i in range(4)]
    g.bank = lambda i, half: g.ps[i][:, 512 * half:512 * half + 512]
    g.RSF = g.RS.ap().rearrange("p a b -> p (a b)")

    p = Prog(nc)
    g.p = p
    build_main(nc, p, g, cfg)
    p.finalize(st)
    st.close()
    return nc


WR = {"A": (0, 1536), "B": (1536, 3072), "C": (3072, 4608), "D": (4608, 6144)}


def wbq(g, lo, hi):
    return g.WBQ[:, lo:hi]


def wres(lo, hi):
    out = []
    for nm, (a, b) in WR.items():
        if lo < b and hi > a:
            out.append("WBQ" + nm)
    return out


def pool_attn(g, local):
    v = G()
    P = g.POOL
    def bf(off, n):
        return P[:, off:off + 2 * n].bitcast(BF16)
    v.QB = bf(0, 4096).rearrange("p (m t) -> p m t", m=2)
    v.KB = bf(8192, 4096).rearrange("p (m t) -> p m t", m=2)
    if local:
        v.PERM = bf(16384, 2048)
        v.VB = bf(20480, 4160).rearrange("p (k j) -> p k j", k=16)
        v.CTX = bf(28800, 2048).rearrange("p (h t) -> p h t", h=4)[0:64]
        v.AT = bf(32896, 2048)
        v.AT2 = bf(36992, 2048)
        v.CS1 = P[:, 41088:41088 + 8192].bitcast(F32)
        v.CS2 = P[:, 49280:49280 + 8192].bitcast(F32)
        v.RHL = v.PERM
        v.RHL_res = "PERM"
    else:
        v.VB = bf(16384, 4160).rearrange("p (k j) -> p k j", k=16)
        v.CTX = bf(24704, 2048).rearrange("p (h t) -> p h t", h=4)[0:64]
        v.AT = bf(28800, 2048)
        v.AT2 = bf(32896, 2048)
        v.RHL = bf(36992, 2048)
        v.RHL_res = "DRHL"
    return v


def pool_ffn(g):
    v = G()
    v.MC = g.POOL[:, 0:45056].bitcast(BF16).rearrange("p (k t) -> p k t", k=22)
    v.ST = g.POOL[:, 45056:45056 + 8192].bitcast(BF16)
    return v


def ld_tables(p, g):
    p.memset(g.SMB[:, 0, :], 1.0, writes=["SMB0"], tag="ones")
    p.memset(g.SMB[:, 10:14, :], 1.0, writes=["SMBw"], tag="ones512")
    p.dma("sync", g.TBL[:, 0:144], g.GdH, (), ["TBL0"], tag="Gd")
    p.dma("sync", g.TBL[0:64, 144:160], g.GpH, (), ["TBL1"], tag="Gp")
    p.dma("sync", g.TBL[0:1, 160:177], g.relH, (), ["TBL2"], tag="rel")
    p.dma("sync", g.TBL[:, 177:178], g.r16H, (), ["TBL3"], tag="r16")
    p.dma("gpsimd", g.SMB[:, 1, :], g.UcH, (), ["SMB1"], tag="Uc")
    p.dma("gpsimd", g.SMB[:, 2, :], g.UwH, (), ["SMB2"], tag="Uw")
    p.dma("gpsimd", g.SMB[:, 3, :], g.U01dH, (), ["SMB3"], tag="U01d")
    p.dma("gpsimd", g.SMB[:, 4, 0:8].rearrange("p (a b) -> p a b", a=2), g.HSH,
          (), ["SMB4"], tag="HS")
    p.ts(g.SMALL[:, 0, 0:1], g.TBL[:, 177:178], 0.125, ALU.mult,
         reads=["TBL3"], writes=["SMa0"], tag="r16s")
    p.red(g.SMALL[0:1, 1, 0:1], g.TBL[0:1, 160:177], ALU.max, reads=["TBL2"],
          writes=["SMa1"], absv=True, tag="relmax")
    p.copy(g.SMB[0:1, 6, 0:1], g.SMALL[0:1, 1, 0:1], reads=["SMa1"], writes=["SMB6"],
           tag="relmaxb")
    p.mm(g.bank(3, 0)[0:4, 0:1], g.SMB[0:1, 0, 0:4], g.SMB[0:1, 6, 0:1], True, True,
         reads=["SMB0", "SMB6"], writes=[psr(3, 0)], tag="relmm")
    p.copy(g.SMALL[0:4, 1, 1:2], g.bank(3, 0)[0:4, 0:1], reads=[psr(3, 0)],
           writes=["SMa1b"], tag="relmax4")


def stage_bias_hilo(p, g, srcH, dst_bf, dst_res, nseg, tag):
    rsb = g.RS.ap().rearrange("p a b -> p (a b)").bitcast(BF16)
    rsf = g.RSF
    RSALL = ["RS0", "RS1", "RS2", "RS3"]
    for s_ in range((nseg + 511) // 512):
        a, b = 512 * s_, min(512 * s_ + 512, nseg)
        w = b - a
        p.dma("sync", rsf[0:2, 0:w], srcH[0:2, a:b], RSALL, RSALL, tag=tag + "ld")
        p.copy(rsb[0:2, 1024:1024 + w], rsf[0:2, 0:w], reads=RSALL, writes=RSALL,
               tag=tag + "hi")
        p.copy(g.bank(3, 0)[0:2, 0:w], rsb[0:2, 1024:1024 + w], reads=RSALL,
               writes=[psr(3, 0)], tag=tag + "hf")
        p.tt(rsf[0:2, 0:w], rsf[0:2, 0:w], g.bank(3, 0)[0:2, 0:w], ALU.subtract,
             reads=RSALL + [psr(3, 0)], writes=RSALL, tag=tag + "lo")
        p.copy(rsb[0:2, 1536:1536 + w], rsf[0:2, 0:w], reads=RSALL, writes=RSALL,
               tag=tag + "lb")
        p.dma("sync", dst_bf[0:1, 0:2 * nseg].rearrange("p (r c) -> p r c", r=2)[:, :, a:b],
              rsb[0:2, 1024:1024 + w], RSALL, [dst_res], tag=tag + "chi")
        p.dma("sync", dst_bf[1:2, 0:2 * nseg].rearrange("p (r c) -> p r c", r=2)[:, :, a:b],
              rsb[0:2, 1536:1536 + w], RSALL, [dst_res], tag=tag + "clo")
    return dst_bf[0:2, 0:2 * nseg]


def bias_bounds(p, g, QB, KB, mode, tag):
    for which, buf, res in (("q", QB, "QB"), ("k", KB, "KB")):
        for m in range(2):
            p.tt(g.SQT[:, :], buf[:, m, :], buf[:, m, :], ALU.mult,
                 reads=[res], writes=["SQT"], tag=tag + "sq")
            for t4 in range(4):
                i, half = 2 + t4 // 2, t4 % 2
                p.mm(g.bank(i, half)[0:4, :],
                     g.SMB[:, 4, 0:8].rearrange("p (a b) -> p a b", a=2)[:, m, :],
                     g.SQT[:, 512 * t4:512 * t4 + 512], m == 0, m == 1,
                     reads=["SQT", "SMB4"], writes=[psr(i, half)], tag=tag + "n2")
        for t4 in range(4):
            i, half = 2 + t4 // 2, t4 % 2
            col = t4 if which == "q" else 4 + t4
            p.red(g.SMALL[0:4, 2, col:col + 1], g.bank(i, half)[0:4, :], ALU.max,
                  reads=[psr(i, half)], writes=[f"n2m{col}"], tag=tag + "n2m")
    p.red(g.SMALL[0:4, 2, 8:9], g.SMALL[0:4, 2, 4:8], ALU.max,
          reads=[f"n2m{c}" for c in (4, 5, 6, 7)], writes=["n2kg"], tag=tag + "kg")
    p.ts(g.SMALL[0:4, 3, 0:4], g.SMALL[0:4, 2, 0:4], g.SMALL[0:4, 2, 8:9], ALU.mult,
         reads=[f"n2m{c}" for c in (0, 1, 2, 3)] + ["n2kg"], writes=["bb"],
         tag=tag + "bb")
    p.act(g.SMALL[0:4, 3, 4:8], g.SMALL[0:4, 3, 0:4], AF.Ln, reads=["bb"],
          writes=["bb1"], tag=tag + "ln")
    p.act(g.SMALL[0:4, 3, 8:12], g.SMALL[0:4, 3, 4:8], AF.Exp, scale=0.5,
          reads=["bb1"], writes=["bb2"], tag=tag + "ex")
    if mode == "dense":
        p.ts(g.SMALL[0:4, 3, 12:16], g.SMALL[0:4, 3, 8:12], g.SMALL[0:4, 1, 1:2],
             ALU.mult, reads=["bb2", "SMa1b"], writes=["negb"], s2=-0.125,
             op1=ALU.mult, tag=tag + "nb")
    else:
        p.ts(g.SMALL[0:4, 3, 12:16], g.SMALL[0:4, 3, 8:12], -8.0, ALU.mult,
             reads=["bb2"], writes=["negb"], tag=tag + "nb")
    p.copy(g.SMB[0:4, 7, 0:4], g.SMALL[0:4, 3, 12:16], reads=["negb"],
           writes=["SMB7"], tag=tag + "nbb")
    p.dma("sync", g.SMB[0:1, 8, 0:16].rearrange("p (a b) -> p a b", a=4),
          g.SMB[0:4, 7, 0:4], ["SMB7"], ["SMB8"], tag=tag + "nbrow")
    p.mm(g.bank(3, 0)[:, 0:16], g.SMB[0:1, 0, :], g.SMB[0:1, 8, 0:16], True, True,
         reads=["SMB0", "SMB8"], writes=[psr(3, 0)], tag=tag + "bm")
    p.copy(g.SMALL[:, 4, 0:16], g.bank(3, 0)[:, 0:16], reads=[psr(3, 0)],
           writes=["BIAS"], tag=tag + "bev")


def attn_group(p, g, v, lay):
    nm = lay["tag"]
    dense = lay["dense"]
    nq0t = g.cfg.get("nq0t", 4)
    nh = g.cfg.get("nheads", 4)
    do_wo = g.cfg.get("do_wo", True)
    CTXB = [(1, 0), (1, 1), (3, 0)]
    queue = []

    def deadline(item):
        kind, gq, _ = item
        if kind == "norm":
            return gq + 2
        return (gq + 1) * nh + 2

    def do_flush(cur):
        while queue:
            if cur is not None and cur < min(deadline(it) for it in queue):
                break
            queue.pop(0)[2]()

    for q0t in range(nq0t):
        q0 = 512 * q0t
        chunks = lay["chunks"](q0t)
        nch = len(chunks)
        for hh in range(nh):
            m, base = hh // 2, 64 * (hh % 2)
            biasap = lay["bias_ap"](g, hh, q0t)
            gidx = q0t * nh + hh
            ci_bank = CTXB[gidx % 3]
            ctx_ps = g.bank(*ci_bank)[0:65, :]

            def emit_S(ci):
                kc, cmin, cmax, kind, tcol = chunks[ci]
                it = 0 if (ci // 2) % 2 == 0 else 2
                Sps = g.ps[it][:, 512 * (ci % 2):512 * (ci % 2) + 512]
                lhs = v.KB[base:base + 64, m, 128 * kc:128 * kc + 128]
                rhs = v.QB[base:base + 64, m, q0 + cmin:q0 + cmax]
                p.mm(Sps[:, cmin:cmax], lhs, rhs, True, True,
                     reads=["QB", "KB"], writes=[psr(it, ci % 2)], tag=nm + "S")

            def fixes(ci, aseg, ares, it):
                kc, cmin, cmax, kind, tcol = chunks[ci]
                Sps = g.ps[it][:, 512 * (ci % 2):512 * (ci % 2) + 512]
                sres = psr(it, ci % 2)
                if kind == "diag":
                    w = min(144, cmax - tcol)
                    p.tt(g.RS[:, 0, 0:w], Sps[:, tcol:tcol + w], g.TBL[:, 0:w],
                         ALU.mult, reads=[sres, "TBL0"], writes=["RS0"],
                         tag=nm + "bG")
                    p.act(aseg[:, tcol:tcol + w], g.RS[:, 0, 0:w], AF.Exp,
                          bias=biasap, scale=0.125, reads=["RS0", lay["bias_res"]],
                          writes=[ares], tag=nm + "bE")
                    wu = min(w, 128)
                    p.tt(aseg[:, tcol:tcol + wu], aseg[:, tcol:tcol + wu],
                         g.SMB[:, 3, 0:wu], ALU.mult, reads=[ares, "SMB3"],
                         writes=[ares], tag=nm + "bU")
                elif kind == "corner":
                    p.tt(g.RS[64:128, 0, 0:16], Sps[64:128, 0:16],
                         g.TBL[0:64, 144:160], ALU.mult,
                         reads=[sres, "TBL1"], writes=["RS0"], tag=nm + "cG")
                    p.act(aseg[64:128, 0:16], g.RS[64:128, 0, 0:16], AF.Exp,
                          bias=biasap[64:128, :], scale=0.125,
                          reads=["RS0", lay["bias_res"]], writes=[ares],
                          tag=nm + "cE")
                elif kind == "tri_causal":
                    p.tt(aseg[:, tcol:tcol + 128], aseg[:, tcol:tcol + 128],
                         g.SMB[:, 1, :], ALU.mult, reads=[ares, "SMB1"],
                         writes=[ares], tag=nm + "tc")
                elif kind == "tri_window":
                    p.tt(aseg[:, tcol:tcol + 128], aseg[:, tcol:tcol + 128],
                         g.SMB[:, 2, :], ALU.mult, reads=[ares, "SMB2"],
                         writes=[ares], tag=nm + "tw")

            def emit_av(bb):
                a0, a1 = 2 * bb, 2 * bb + 1
                ab2 = 1024 * (bb % 2)
                for ci in ([a0] if a1 >= nch else [a0, a1]):
                    kc, cmin, cmax, kind, tcol = chunks[ci]
                    p.mm(ctx_ps[:, cmin:cmax], v.VB[:, kc, 65 * hh:65 * hh + 65],
                         v.AT[:, ab2 + 512 * (ci % 2) + cmin:ab2 + 512 * (ci % 2) + cmax],
                         ci == 0, ci == nch - 1,
                         reads=[f"ATs{2 * (bb % 2) + (ci % 2)}", "VB"],
                         writes=[psr(*ci_bank)], tag=nm + "AV")

            npair = (nch + 1) // 2
            emit_S(0)
            if nch > 1:
                emit_S(1)
            for b in range(npair):
                c0 = 2 * b
                c1 = 2 * b + 1 if 2 * b + 1 < nch else None
                it = 0 if b % 2 == 0 else 2
                if 2 * b + 2 < nch:
                    emit_S(2 * b + 2)
                    if 2 * b + 3 < nch:
                        emit_S(2 * b + 3)
                if b == 0:
                    do_flush(gidx)
                cmin0 = chunks[c0][1]
                hi = 1024 if c1 is not None else 512
                abase = 1024 * (b % 2)
                ares0, ares1 = f"ATs{2 * (b % 2)}", f"ATs{2 * (b % 2) + 1}"
                p.act(v.AT[:, abase + cmin0:abase + hi],
                      g.ps[it][:, cmin0:hi], AF.Exp,
                      bias=biasap, scale=lay["scale"](g),
                      reads=[psr(it, 0), psr(it, 1), lay["bias_res"]] + lay["scale_res"],
                      writes=[ares0, ares1], tag=nm + "exp")
                fixes(c0, v.AT[:, abase:abase + 512], ares0, it)
                if c1 is not None:
                    fixes(c1, v.AT[:, abase + 512:abase + 1024], ares1, it)
                if b > 0:
                    emit_av(b - 1)
            emit_av(npair - 1)
            if g.cfg.get("skip_norm"):
                p.copy(v.CTX[0:64, hh, :], ctx_ps[0:64, :], reads=[psr(*ci_bank)],
                       writes=[f"CTXn{hh}"], tag=nm + "cp")
                continue
            p.copy(g.RS[64:65, 1, 0:512], ctx_ps[64:65, :], reads=[psr(*ci_bank)],
                   writes=["sumrow"], tag=nm + "sum")
            r = gidx % 2
            cS = 1024 * r
            p.ts(g.RS[64:65, 2, 0:512], g.RS[64:65, 1, 0:512], 1e-30, ALU.max,
                 reads=["sumrow"], writes=["RS2"], tag=nm + "smx")
            p.act(g.RS[64:65, 2, 0:512], g.RS[64:65, 2, 0:512], AF.Ln,
                  reads=["RS2"], writes=["RS2"], tag=nm + "rln")
            p.act(g.RS[64:65, 3, 0:512], g.RS[64:65, 2, 0:512], AF.Exp, scale=-1.0,
                  reads=["RS2"], writes=["RS3"], tag=nm + "rcp")
            p.copy(g.SQT[64:65, cS:cS + 512], g.RS[64:65, 3, 0:512],
                   reads=["RS3"], writes=[f"SQTr{r}"], tag=nm + "rhi")
            p.copy(g.RS[64:65, 2, 0:512], g.SQT[64:65, cS:cS + 512],
                   reads=[f"SQTr{r}"], writes=["RS2"], tag=nm + "rhf")
            p.tt(g.RS[64:65, 3, 0:512], g.RS[64:65, 3, 0:512],
                 g.RS[64:65, 2, 0:512], ALU.subtract,
                 reads=["RS3", "RS2"], writes=["RS3"], tag=nm + "rlo")
            p.copy(g.SQT[64:65, cS + 512:cS + 1024], g.RS[64:65, 3, 0:512],
                   reads=["RS3"], writes=[f"SQTr{r}"], tag=nm + "rlb")

            def make_norm(h2=hh, r=r, cS=cS, q0t=q0t):
                cb = CTXB[(q0t * nh + h2) % 3]

                def run():
                    p.mm(g.bank(3, 1)[0:64, :], g.SMB[64:65, 0, 0:64],
                         g.SQT[64:65, cS:cS + 512], True, False,
                         reads=[f"SQTr{r}", "SMB0"], writes=[psr(3, 1)],
                         tag=nm + "rm")
                    p.mm(g.bank(3, 1)[0:64, :], g.SMB[64:65, 0, 0:64],
                         g.SQT[64:65, cS + 512:cS + 1024], False, True,
                         reads=[f"SQTr{r}", "SMB0"], writes=[psr(3, 1)],
                         tag=nm + "rm")
                    p.copy(g.RS[0:64, 1, 0:512], g.bank(3, 1)[0:64, :],
                           reads=[psr(3, 1)], writes=["rme"], tag=nm + "rme")
                    p.tt(v.CTX[0:64, h2, :], g.bank(*cb)[0:64, :],
                         g.RS[0:64, 1, 0:512], ALU.mult,
                         reads=[psr(*cb), "rme"], writes=[f"CTXn{h2}"],
                         tag=nm + "cn")
                return run
            queue.append(("norm", gidx, make_norm()))
        if not do_wo:
            continue

        def make_wo(q0=q0, nm=nm, q0t=q0t):
            s_late = ((q0t + 1) * nh + 2) % 3
            banks = [(3, 1), CTXB[s_late]]

            def run():
                for d in range(8):
                    tb = banks[d % 2]
                    yb = g.bank(*tb)
                    yres = psr(*tb)
                    last_h = not (dense and lay["bo"])
                    for hh2 in range(nh):
                        p.mm(yb, lay["wo_ap"](d, hh2), v.CTX[0:64, hh2, :], hh2 == 0,
                             (hh2 == nh - 1) and last_h,
                             reads=[lay["wo_res"], f"CTXn{hh2}"],
                             writes=[yres], tag=nm + "wo")
                    if dense and lay["bo"]:
                        p.mm(yb, lay["bohl"][0:2, 128 * d:128 * d + 128],
                             g.SMB[0:2, 10:14, :].rearrange("p a b -> p (a b)"),
                             False, True,
                             reads=[lay["bohl_res"], "SMBw"], writes=[yres],
                             tag=nm + "bo")
                    p.tt(g.XT[:, d, 1 + q0:1 + q0 + 512], yb,
                         g.XT[:, d, 1 + q0:1 + q0 + 512], ALU.add,
                         reads=[yres, f"XT{d}"], writes=[f"XT{d}"], tag=nm + "res")
            return run

        queue.append(("wo", q0t, make_wo()))
    do_flush(None)


def dense_chunks(q0t):
    nch = 4 * q0t + 4
    plains, fixy = [], []
    for kc in range(nch):
        j = kc - (nch - 4)
        if j >= 0:
            fixy.append((kc, 128 * j, 512, "diag", 128 * j))
        elif kc == nch - 5:
            fixy.append((kc, 0, 512, "corner", 0))
        else:
            plains.append((kc, 0, 512, "plain", 0))
    return plains[:1] + fixy + plains[1:]


def local_chunks(q0t):
    out = []
    for jj in [0, -4, -3, -2, -1, 1, 2, 3]:
        kc = 4 * q0t + jj
        if kc < 0:
            continue
        cmin = max(0, 128 * jj)
        cmax = min(512, 128 * jj + 640)
        if jj <= -1:
            out.append((kc, cmin, cmax, "tri_window", cmax - 128))
        else:
            out.append((kc, cmin, cmax, "tri_causal", cmin))
    return out


def dense_phase(p, g, bohl_ap):
    v = pool_attn(g, local=False)
    p.memset(g.ps[0][:, :], 0.0, writes=[psr(0, 0), psr(0, 1)], tag="ps0z")
    p.memset(g.ps[2][:, :], 0.0, writes=[psr(2, 0), psr(2, 1)], tag="ps2z")
    parts = g.cfg.get("dense_parts", {"proj", "bias", "attn"})
    for grp in range(g.cfg.get("dense_groups", 4)):
        wq = wbq(g, 0, 2048).rearrange("p (c m j) -> p c m j", c=8, m=2)
        wk = wbq(g, 2048, 4096).rearrange("p (c m j) -> p c m j", c=8, m=2)
        wv = wbq(g, 4096, 6144).rearrange("p (c j) -> p c j", c=8)
        p.dma("gpsimd", wq, g.wqT[grp], wres(0, 2048), wres(0, 2048), tag="ldwq")
        p.dma("gpsimd", wk, g.wkT[grp], wres(2048, 4096), wres(2048, 4096), tag="ldwk")
        p.dma("gpsimd", wv, g.wvP[grp], wres(4096, 6144), wres(4096, 6144), tag="ldwv")
        p.dma("gpsimd", g.WB3[:, :, :, :], g.woP[:, 4 * grp:4 * grp + 4, :, :],
              ["WB3"], ["WB3"], tag="ldwo")
        for m in range(2):
            for t4 in range(4):
                sl = slice(1 + 512 * t4, 1 + 512 * t4 + 512)
                bq, bk = g.bank(0, t4 % 2), g.bank(1, t4 % 2)
                for c in range(8):
                    p.mm(bq, wq[:, c, m, :], g.HF[:, c, sl], c == 0, c == 7,
                         reads=wres(0, 2048) + [f"HF{c}"], writes=[psr(0, t4 % 2)],
                         tag="qprj")
                    p.mm(bk, wk[:, c, m, :], g.HF[:, c, sl], c == 0, c == 7,
                         reads=wres(2048, 4096) + [f"HF{c}"], writes=[psr(1, t4 % 2)],
                         tag="kprj")
                p.copy(v.QB[:, m, 512 * t4:512 * t4 + 512], bq, reads=[psr(0, t4 % 2)],
                       writes=["QB"], engine="scalar", tag="qev")
                p.copy(v.KB[:, m, 512 * t4:512 * t4 + 512], bk, reads=[psr(1, t4 % 2)],
                       writes=["KB"], engine="scalar", tag="kev")
        p.memset(v.VB[:, :, :].rearrange("p k (h j) -> p k h j", h=4)[:, :, :, 64:65],
                 1.0, writes=["VB"], tag="vones")
        for tt in range(16):
            bv = g.bank(2, tt % 2)
            off = 1 + 128 * tt
            for c in range(8):
                p.mm(bv[:, 0:256], g.HF[:, c, off:off + 128], wv[:, c, :],
                     c == 0, c == 7, reads=wres(4096, 6144) + [f"HF{c}"],
                     writes=[psr(2, tt % 2)], tag="vprj")
            p.copy(v.VB[:, tt, :].rearrange("p (h j) -> p h j", h=4)[:, :, 0:64],
                   bv[:, 0:256].rearrange("p (h j) -> p h j", h=4), reads=[psr(2, tt % 2)],
                   writes=["VB"], engine="scalar", tag="vev")
        if "bias" not in parts:
            continue
        bias_bounds(p, g, v.QB, v.KB, "dense", f"db{grp}")
        if "attn" not in parts:
            continue
        lay = dict(tag=f"dA{grp}", dense=True, scale=lambda g: g.SMALL[:, 0, 0:1],
                   scale_res=["SMa0"],
                   bias_ap=lambda g, hh, q0t: g.SMALL[:, 4, 4 * hh + q0t:4 * hh + q0t + 1],
                   bias_res="BIAS", chunks=dense_chunks,
                   wo_ap=lambda d, hh: g.WB3[0:64, hh, d, :], wo_res="WB3",
                   bo=(grp == 0), bohl=bohl_ap, bohl_res="MCst")
        attn_group(p, g, v, lay)


def ln_compute(p, g, shifted, stage_bf, stage_res, tag):
    RSALL = ["RS0", "RS1", "RS2", "RS3"]
    rsb = g.RS.ap().rearrange("p a b -> p (a b)").bitcast(BF16)
    for d in range(8):
        p.copy(g.HF[:, d, 1:S + 1], g.XT[:, d, 1:S + 1], reads=[f"XT{d}"],
               writes=[f"HF{d}"], tag=tag + "hf")
    for d in range(8):
        off = 1 if (not shifted or d < 4) else 0
        rd = [f"HF{d}", "HFpad"] if shifted else [f"HF{d}"]
        sqb, sqr = (g.SQT[:, :], "SQT") if d % 2 == 0 else (rsb[:, 0:2048], "SQB")
        wl = [sqr] if d != 1 else [sqr] + RSALL + ["rme", "sumrow", "RS2", "RS3"]
        p.tt(sqb, g.HF[:, d, off:off + S], g.HF[:, d, off:off + S],
             ALU.mult, reads=rd, writes=wl, tag=tag + "sq")
        for t4 in range(4):
            p.mm(g.bank(t4 // 2, t4 % 2)[0:1, :], g.SMB[:, 0, 0:1],
                 g.HF[:, d, off + 512 * t4:off + 512 * t4 + 512], d == 0, d == 7,
                 reads=rd + ["SMB0"], writes=[psr(t4 // 2, t4 % 2)], tag=tag + "sum")
            p.mm(g.bank(2 + t4 // 2, t4 % 2)[0:1, :], g.SMB[:, 0, 0:1],
                 sqb[:, 512 * t4:512 * t4 + 512], d == 0, d == 7,
                 reads=[sqr, "SMB0"], writes=[psr(2 + t4 // 2, t4 % 2)],
                 tag=tag + "sqs")
    r0 = g.RSF[0:1, :]
    for h in range(2):
        for j in range(2):
            t4 = 2 * h + j
            p.copy(r0[:, 512 * j:512 * j + 512], g.bank(t4 // 2, t4 % 2)[0:1, :],
                   reads=[psr(t4 // 2, t4 % 2)], writes=RSALL + ["SQB"],
                   tag=tag + "se")
            p.copy(r0[:, 1024 + 512 * j:1024 + 512 * j + 512],
                   g.bank(2 + t4 // 2, t4 % 2)[0:1, :],
                   reads=[psr(2 + t4 // 2, t4 % 2)], writes=RSALL + ["SQB"],
                   tag=tag + "qe")
        mu = r0[:, 0:1024]
        ms = r0[:, 1024:2048]
        p.ts(mu, mu, 1.0 / DM, ALU.mult, reads=RSALL, writes=RSALL, tag=tag + "mu")
        p.ts(ms, ms, 1.0 / DM, ALU.mult, reads=RSALL, writes=RSALL, tag=tag + "ms")
        for j in range(2):
            tb = g.bank(0, 0)[0:1, :]
            p.tt(tb, mu[:, 512 * j:512 * j + 512], mu[:, 512 * j:512 * j + 512],
                 ALU.mult, reads=RSALL, writes=[psr(0, 0)], tag=tag + "m2")
            p.tt(ms[:, 512 * j:512 * j + 512], ms[:, 512 * j:512 * j + 512], tb,
                 ALU.subtract, reads=RSALL + [psr(0, 0)], writes=RSALL,
                 tag=tag + "var")
        p.ts(ms, ms, 1e-5, ALU.add, reads=RSALL, writes=RSALL, tag=tag + "eps")
        p.act(ms, ms, AF.Ln, reads=RSALL, writes=RSALL, tag=tag + "ln")
        p.act(ms, ms, AF.Exp, scale=-0.5, reads=RSALL, writes=RSALL, tag=tag + "rs")
        p.tt(mu, mu, ms, ALU.mult, reads=RSALL, writes=RSALL, tag=tag + "b0")
        p.ts(mu, mu, -1.0, ALU.mult, reads=RSALL, writes=RSALL, tag=tag + "b1")
        hc = slice(1024 * h, 1024 * h + 1024)
        p.copy(stage_bf[0:1, 0:1024], ms, reads=RSALL, writes=[stage_res],
               tag=tag + "ahi")
        p.dma("sync", g.SQT[32:33, hc], stage_bf[0:1, 0:1024], [stage_res],
              ["SQT"], tag=tag + "ahm")
        for j in range(2):
            tb = g.bank(0, 0)[0:1, :]
            p.copy(tb, stage_bf[0:1, 512 * j:512 * j + 512], reads=[stage_res],
                   writes=[psr(0, 0)], tag=tag + "ahf")
            p.tt(ms[:, 512 * j:512 * j + 512], ms[:, 512 * j:512 * j + 512], tb,
                 ALU.subtract, reads=RSALL + [psr(0, 0)], writes=RSALL,
                 tag=tag + "alo")
        p.copy(stage_bf[0:1, 0:1024], ms, reads=RSALL, writes=[stage_res],
               tag=tag + "alb")
        p.dma("sync", g.SQT[33:34, hc], stage_bf[0:1, 0:1024], [stage_res],
              ["SQT"], tag=tag + "alm")
        p.copy(g.SQT[0:1, hc], mu, reads=RSALL, writes=["SQT"], tag=tag + "bhi")
        for j in range(2):
            tb = g.bank(0, 0)[0:1, :]
            p.copy(tb, g.SQT[0:1, 1024 * h + 512 * j:1024 * h + 512 * j + 512],
                   reads=["SQT"], writes=[psr(0, 0)], tag=tag + "bhf")
            p.tt(mu[:, 512 * j:512 * j + 512], mu[:, 512 * j:512 * j + 512], tb,
                 ALU.subtract, reads=RSALL + [psr(0, 0)], writes=RSALL,
                 tag=tag + "blo")
        p.copy(stage_bf[0:1, 0:1024], mu, reads=RSALL, writes=[stage_res],
               tag=tag + "blb")
        p.dma("sync", g.SQT[1:2, hc], stage_bf[0:1, 0:1024], [stage_res],
              ["SQT"], tag=tag + "blm")
    for t4 in range(4):
        p.mm(g.bank(t4 // 2, t4 % 2), g.SMB[32:34, 0, :],
             g.SQT[32:34, 512 * t4:512 * t4 + 512], True, True,
             reads=["SQT", "SMB0"], writes=[psr(t4 // 2, t4 % 2)], tag=tag + "Am")
        p.mm(g.bank(2 + t4 // 2, t4 % 2), g.SMB[0:2, 0, :],
             g.SQT[0:2, 512 * t4:512 * t4 + 512], True, True,
             reads=["SQT", "SMB0"], writes=[psr(2 + t4 // 2, t4 % 2)], tag=tag + "Bm")


def AB_aps(g):
    Am = [g.bank(t4 // 2, t4 % 2) for t4 in range(4)]
    Bm = [g.bank(2 + t4 // 2, t4 % 2) for t4 in range(4)]
    Ar = [psr(t4 // 2, t4 % 2) for t4 in range(4)]
    Br = [psr(2 + t4 // 2, t4 % 2) for t4 in range(4)]
    return Am, Bm, Ar, Br


def outer_ln(p, g, spill, stage_bf, stage_res, tag):
    ln_compute(p, g, False, stage_bf[0:1, 0:2048], stage_res, tag)
    Am, Bm, Ar, Br = AB_aps(g)
    for d in range(8):
        for t4 in range(4):
            sl = slice(1 + 512 * t4, 1 + 512 * t4 + 512)
            p.tt(g.XT[:, d, sl], g.XT[:, d, sl], Am[t4], ALU.mult,
                 reads=[f"XT{d}", Ar[t4]], writes=[f"XT{d}"], tag=tag + "ap1")
            p.tt(g.XT[:, d, sl], g.XT[:, d, sl], Bm[t4], ALU.add,
                 reads=[f"XT{d}", Br[t4]], writes=[f"XT{d}"], tag=tag + "ap2")
        p.copy(g.HF[:, d, 1:S + 1], g.XT[:, d, 1:S + 1], reads=[f"XT{d}"],
               writes=[f"HF{d}"], tag=tag + "hf2")
        if spill:
            p.dma("sync", g.x1spill[:, S * d:S * d + S], g.XT[:, d, 1:S + 1],
                  [f"XT{d}"], [f"spill{d}"], tag=tag + "sp")


def local_qk_prep(p, g, v, l, grp, side):
    buf = v.QB if side == "q" else v.KB
    res = "QB" if side == "q" else "KB"
    base_col = (0 if side == "q" else 512) + 256 * grp
    wr_lo, wr_hi = (0, 2048) if side == "q" else (2048, 4096)
    scol0 = getattr(g, f"{'qs' if side == 'q' else 'ks'}H{l}")
    p.dma("sync", g.SMALL[:, 5, 3:4], scol0, ["SMa5c"], ["SMa5c"], tag=f"{side}qs0")
    p.tt(g.SMALL[:, 5, 2:3], g.SMALL[:, 5, 3:4], g.SMALL[:, 5, 3:4], ALU.mult,
         reads=["SMa5c"], writes=["SMa5b"], tag=f"{side}qs2")
    p.ts(g.SMB[:, 15, 0:8].rearrange("p (a b) -> p a b", a=2),
         g.SMB[:, 4, 0:8].rearrange("p (a b) -> p a b", a=2),
         g.SMALL[:, 5, 2:3], ALU.mult, reads=["SMB4", "SMa5b"], writes=["SMB15"],
         tag=f"{side}hsq")
    wsl = wbq(g, wr_lo, wr_hi).rearrange("p (c j) -> p c j", c=8)
    p.dma("gpsimd", wsl, getattr(g, f"wqkvP{l}")[0 if side == "q" else 1, grp],
          wres(wr_lo, wr_hi), wres(wr_lo, wr_hi), tag=f"ldw{side}")
    for m in range(2):
        for t4 in range(4):
            b = g.bank(0, t4 % 2)
            for c in range(8):
                off = 1 if c < 4 else 0
                p.mm(b, wsl[:, c, 128 * m:128 * m + 128],
                     g.HF[:, c, off + 512 * t4:off + 512 * t4 + 512], c == 0, c == 7,
                     reads=wres(wr_lo, wr_hi) + [f"HF{c}", "HFpad"],
                     writes=[psr(0, t4 % 2)], tag=f"{side}prj")
            p.copy(buf[:, m, 512 * t4:512 * t4 + 512], b, reads=[psr(0, t4 % 2)],
                   writes=[res], engine="scalar", tag=f"{side}ev")
    for m in range(2):
        p.tt(g.SQT[:, :], buf[:, m, :], buf[:, m, :], ALU.mult, reads=[res],
             writes=["SQT"], tag=f"{side}sq")
        for t4 in range(4):
            i, half = 2 + t4 // 2, t4 % 2
            p.mm(g.bank(i, half)[0:4, :],
                 g.SMB[:, 4, 0:8].rearrange("p (a b) -> p a b", a=2)[:, m, :],
                 g.SQT[:, 512 * t4:512 * t4 + 512], m == 0, m == 1,
                 reads=["SQT", "SMB4"], writes=[psr(i, half)], tag=f"{side}n2")
            p.mm(g.bank(t4 // 2, t4 % 2)[0:4, :],
                 g.SMB[:, 15, 0:8].rearrange("p (a b) -> p a b", a=2)[:, m, :],
                 g.SQT[:, 512 * t4:512 * t4 + 512], m == 0, m == 1,
                 reads=["SQT", "SMB15"], writes=[psr(t4 // 2, t4 % 2)],
                 tag=f"{side}n2w")
    rsv = g.RSF[0:4, :]
    for t4 in range(4):
        i, half = 2 + t4 // 2, t4 % 2
        p.ts(g.RS[0:4, t4, 0:512], g.bank(i, half)[0:4, :], 1e-24, ALU.max,
             reads=[psr(i, half)], writes=[f"RS{t4}"], tag=f"{side}mx")
    p.act(rsv, rsv, AF.Ln, reads=["RS0", "RS1", "RS2", "RS3"],
          writes=["RS0", "RS1", "RS2", "RS3"], tag=f"{side}ln")
    p.act(rsv, rsv, AF.Exp, scale=-0.5, reads=["RS0", "RS1", "RS2", "RS3"],
          writes=["RS0", "RS1", "RS2", "RS3"], tag=f"{side}ex")
    p.tt(v.PERM[0:4, :], rsv, rsv, ALU.mult, reads=["RS0", "RS1", "RS2", "RS3"],
         writes=["PERM"], tag=f"{side}rn2")
    c0 = 0 if side == "q" else 8
    for t4 in range(4):
        bk = g.bank(t4 // 2, t4 % 2)
        p.tt(bk[0:4, :], bk[0:4, :], v.PERM[0:4, 512 * t4:512 * t4 + 512], ALU.mult,
             reads=[psr(t4 // 2, t4 % 2), "PERM"], writes=[psr(t4 // 2, t4 % 2)],
             tag=f"{side}n2t")
        p.red(g.SMALL[0:4, 8, c0 + t4:c0 + t4 + 1], bk[0:4, :], ALU.max,
              reads=[psr(t4 // 2, t4 % 2)], writes=[f"n2t{side}{t4}"],
              tag=f"{side}n2m")
    p.red(g.SMALL[0:4, 8, c0 + 4:c0 + 5], g.SMALL[0:4, 8, c0:c0 + 4], ALU.max,
          reads=[f"n2t{side}{t}" for t in range(4)], writes=["n2gq" if side == "q" else "n2gk"],
          tag=f"{side}n2g")
    p.copy(g.SQT[0:4, :], rsv, reads=["RS0", "RS1", "RS2", "RS3"], writes=["SQT"],
           tag=f"{side}hi")
    for t4 in range(4):
        p.copy(g.bank(2 + t4 // 2, t4 % 2)[0:4, :], g.SQT[0:4, 512 * t4:512 * t4 + 512],
               reads=["SQT"], writes=[psr(2 + t4 // 2, t4 % 2)], tag=f"{side}hf")
        p.tt(g.RS[0:4, t4, 0:512], g.RS[0:4, t4, 0:512],
             g.bank(2 + t4 // 2, t4 % 2)[0:4, :], ALU.subtract,
             reads=[f"RS{t4}", psr(2 + t4 // 2, t4 % 2)], writes=[f"RS{t4}"],
             tag=f"{side}lo")
    p.copy(v.PERM[0:4, :], rsv, reads=["RS0", "RS1", "RS2", "RS3"], writes=["PERM"],
           tag=f"{side}lb")
    csH = g.CSqH if side == "q" else g.CSkH
    ssH = g.SSqH if side == "q" else g.SSkH
    scol = getattr(g, f"{'qs' if side == 'q' else 'ks'}H{l}")
    spcol = getattr(g, f"{'qsp' if side == 'q' else 'ksp'}H{l}")
    p.dma("sync", v.CS1, csH, ["CS1"], ["CS1"], tag=f"{side}cs")
    p.dma("sync", v.CS2, ssH, ["CS2"], ["CS2"], tag=f"{side}ss")
    p.dma("sync", g.SMALL[:, 5, 0:1], scol, ["SMa5"], ["SMa5"], tag=f"{side}qs")
    p.dma("sync", g.SMALL[:, 5, 1:2], spcol, ["SMa5"], ["SMa5"], tag=f"{side}qsp")
    p.ts(v.CS1, v.CS1, g.SMALL[:, 5, 0:1], ALU.mult, reads=["CS1", "SMa5"],
         writes=["CS1"], tag=f"{side}cf")
    p.ts(v.CS2, v.CS2, g.SMALL[:, 5, 1:2], ALU.mult, reads=["CS2", "SMa5"],
         writes=["CS2"], tag=f"{side}sf")
    for m in range(2):
        rb = 32 * m
        p.dma("sync", v.AT2[rb + 0:rb + 1, :], g.SQT[2 * m:2 * m + 1, :], ["SQT"],
              ["AT1"], tag=f"{side}p1")
        p.dma("sync", v.AT2[rb + 1:rb + 2, :], v.PERM[2 * m:2 * m + 1, :], ["PERM"],
              ["AT1"], tag=f"{side}p2")
        p.dma("sync", v.AT2[rb + 2:rb + 3, :], g.SQT[2 * m + 1:2 * m + 2, :], ["SQT"],
              ["AT1"], tag=f"{side}p3")
        p.dma("sync", v.AT2[rb + 3:rb + 4, :], v.PERM[2 * m + 1:2 * m + 2, :], ["PERM"],
              ["AT1"], tag=f"{side}p4")
    for m in range(2):
        rb = 32 * m
        for blk, src in ((0, 32), (32, 0), (64, 96), (96, 64)):
            p.dma("sync", v.PERM[blk:blk + 32, :], buf[src:src + 32, m, :],
                  [res], ["PERM"], tag=f"{side}pm")
        p.tt(g.RSF, buf[:, m, :], v.CS1, ALU.mult, reads=[res, "CS1"],
             writes=["RS0", "RS1", "RS2", "RS3"], tag=f"{side}u")
        for half in range(2):
            p.tt(g.ps[half][:, :], v.PERM[:, 1024 * half:1024 * half + 1024],
                 v.CS2[:, 1024 * half:1024 * half + 1024], ALU.mult,
                 reads=["PERM", "CS2"], writes=[psr(half, 0), psr(half, 1)],
                 tag=f"{side}u2")
            p.tt(g.RSF[:, 1024 * half:1024 * half + 1024],
                 g.RSF[:, 1024 * half:1024 * half + 1024],
                 g.ps[half][:, :], ALU.add,
                 reads=["RS0", "RS1", "RS2", "RS3", psr(half, 0), psr(half, 1)],
                 writes=["RS0", "RS1", "RS2", "RS3"], tag=f"{side}s")
        for t4 in range(4):
            p.mm(g.bank(3, 1), g.SMB[0:4, 9, :] if rb == 0 else g.SMB[32:36, 9, :],
                 v.AT2[rb:rb + 4, 512 * t4:512 * t4 + 512],
                 True, True, reads=["AT1", "SMB9"], writes=[psr(3, 1)],
                 tag=f"{side}rm")
            p.tt(buf[:, m, 512 * t4:512 * t4 + 512],
                 g.RSF[:, 512 * t4:512 * t4 + 512], g.bank(3, 1), ALU.mult,
                 reads=["RS0", "RS1", "RS2", "RS3", psr(3, 1)], writes=[res],
                 tag=f"{side}qh")


def local_bias(p, g, tag):
    p.ts(g.SMALL[0:4, 8, 13:14], g.SMALL[0:4, 8, 4:5], g.SMALL[0:4, 8, 12:13],
         ALU.mult, reads=["n2gq", "n2gk"], writes=["lbb"], tag=tag + "p")
    p.act(g.SMALL[0:4, 8, 14:15], g.SMALL[0:4, 8, 13:14], AF.Ln, reads=["lbb"],
          writes=["lbb1"], tag=tag + "ln")
    p.act(g.SMALL[0:4, 8, 15:16], g.SMALL[0:4, 8, 14:15], AF.Exp, scale=0.5,
          reads=["lbb1"], writes=["lbb2"], tag=tag + "ex")
    p.ts(g.SMALL[0:4, 8, 16:17], g.SMALL[0:4, 8, 15:16], -16.0, ALU.mult,
         reads=["lbb2"], writes=["lnegb"], tag=tag + "nb")
    p.copy(g.SMB[0:4, 7, 0:1], g.SMALL[0:4, 8, 16:17], reads=["lnegb"],
           writes=["SMB7"], tag=tag + "nbb")
    p.dma("sync", g.SMB[0:1, 8, 0:4].rearrange("p (a b) -> p a b", a=4),
          g.SMB[0:4, 7, 0:1], ["SMB7"], ["SMB8"], tag=tag + "nbr")
    p.mm(g.bank(3, 0)[:, 0:4], g.SMB[0:1, 0, :], g.SMB[0:1, 8, 0:4], True, True,
         reads=["SMB0", "SMB8"], writes=[psr(3, 0)], tag=tag + "bm")
    p.copy(g.SMALL[:, 9, 0:4], g.bank(3, 0)[:, 0:4], reads=[psr(3, 0)],
           writes=["BIASL"], tag=tag + "bev")


def local_phase(p, g, l):
    v = pool_attn(g, local=True)
    p.memset(g.ps[0][:, :], 0.0, writes=[psr(0, 0), psr(0, 1)], tag="ps0z")
    p.memset(g.ps[2][:, :], 0.0, writes=[psr(2, 0), psr(2, 1)], tag="ps2z")
    if l > 0:
        for d in range(8):
            p.copy(g.HF[:, d, 1:S + 1], g.XT[:, d, 1:S + 1], reads=[f"XT{d}"],
                   writes=[f"HF{d}"], tag=f"lhf{l}")
    p.dma("gpsimd", g.SMB[0:4, 9, :], g.SEL4H, ["SMB9"], ["SMB9"], tag="sel0")
    p.dma("gpsimd", g.SMB[32:36, 9, :], g.SEL4H, ["SMB9"], ["SMB9"], tag="sel1")
    for grp in range(2):
        local_qk_prep(p, g, v, l, grp, "q")
        local_qk_prep(p, g, v, l, grp, "k")
        wv = wbq(g, 4096, 6144).rearrange("p (c j) -> p c j", c=8)
        p.dma("gpsimd", wv, getattr(g, f"wqkvP{l}")[2, grp],
              wres(4096, 6144), wres(4096, 6144), tag="ldwvl")
        p.memset(v.VB[:, :, :].rearrange("p k (h j) -> p k h j", h=4)[:, :, :, 64:65],
                 1.0, writes=["VB"], tag="vones")
        for tt in range(16):
            bv = g.bank(2, tt % 2)
            for c in range(8):
                off = 1 if c < 4 else 0
                p.mm(bv[:, 0:256], g.HF[:, c, off + 128 * tt:off + 128 * tt + 128],
                     wv[:, c, :], c == 0, c == 7,
                     reads=wres(4096, 6144) + [f"HF{c}", "HFpad"],
                     writes=[psr(2, tt % 2)], tag="vprjl")
            p.copy(v.VB[:, tt, :].rearrange("p (h j) -> p h j", h=4)[:, :, 0:64],
                   bv[:, 0:256].rearrange("p (h j) -> p h j", h=4),
                   reads=[psr(2, tt % 2)], writes=["VB"], engine="scalar", tag="vevl")
        p.dma("gpsimd", g.WB3[:, :, :, :], getattr(g, f"wolP{l}")[:, 4 * grp:4 * grp + 4, :, :],
              ["WB3"], ["WB3"], tag="ldwol")
        if g.cfg.get("dump_qk") == (l, grp):
            p.dma("sync", g.dumpQ, v.QB[:, :, :], ["QB"], [], tag="dq")
            p.dma("sync", g.dumpK, v.KB[:, :, :], ["KB"], [], tag="dk")
            p.dma("sync", g.dumpV, v.VB[:, :, :], ["VB"], [], tag="dv")
        local_bias(p, g, f"lb{l}{grp}")
        lay = dict(tag=f"lA{l}{grp}", dense=False, scale=lambda g: 8.0,
                   scale_res=[],
                   bias_ap=lambda g, hh, q0t: g.SMALL[:, 9, hh:hh + 1],
                   bias_res="BIASL", chunks=local_chunks,
                   wo_ap=lambda d, hh: g.WB3[0:64, hh, d, :], wo_res="WB3",
                   bo=False)
        attn_group(p, g, v, lay)


def ffn_phase(p, g, l):
    v = pool_ffn(g)
    p.dma("sync", g.SMALL[:, 6, 0:8], getattr(g, f"lgH{l}"), ["SMa6"], ["SMa6"],
          tag="lg")
    p.dma("sync", g.SMALL[:, 7, 0:8], getattr(g, f"lbH{l}"), ["SMa7"], ["SMa7"],
          tag="lb")
    p.dma("sync", g.SMALL[:, 8, 0:22], getattr(g, f"b1gH{l}"), ["SMa8"], ["SMa8"],
          tag="b1g")
    ln_compute(p, g, True, v.ST[0:1, 0:2048], "MCst", f"il{l}")
    Am, Bm, Ar, Br = AB_aps(g)
    for d in range(8):
        off = 1 if d < 4 else 0
        for t4 in range(4):
            slx = slice(off + 512 * t4, off + 512 * t4 + 512)
            slh = slice(1 + 512 * t4, 1 + 512 * t4 + 512)
            p.tt(g.RSF[:, 0:512], g.XT[:, d, slx], Am[t4], ALU.mult,
                 reads=[f"XT{d}", Ar[t4]], writes=["RS0"], tag="iap1")
            p.tt(g.RSF[:, 0:512], g.RSF[:, 0:512], Bm[t4], ALU.add,
                 reads=["RS0", Br[t4]], writes=["RS0"], tag="iap2")
            p.ts(g.HF[:, d, slh], g.RSF[:, 0:512], g.SMALL[:, 6, d:d + 1], ALU.mult,
                 reads=["RS0", "SMa6", "SMa7"], writes=[f"HF{d}"],
                 s2=g.SMALL[:, 7, d:d + 1], op1=ALU.add, tag="iap3")
    b1hl = stage_bias_hilo(p, g, getattr(g, f"b1aH{l}"), v.ST[:, 0:2816], "MCst",
                           1408, f"b1a{l}")
    b2hl = stage_bias_hilo(p, g, getattr(g, f"b2H{l}"), v.ST[:, 2816:3840], "MCst",
                           512, f"b2{l}")
    for th in range(2):
        tok = slice(1 + 1024 * th, 1 + 1024 * th + 1024)
        for k in range(22):
            aw = min(128, 2730 - 128 * k)
            lo = 0 if k % 2 == 0 else 3072
            wag = wbq(g, lo, lo + 2048).rearrange("p (c x j) -> p c x j", c=8, x=2)
            wa = wag[:, :, 0, :]
            wg = wag[:, :, 1, :]
            p.dma("gpsimd", wag, getattr(g, f"w1P{l}")[k],
                  wres(lo, lo + 2048), wres(lo, lo + 2048), tag="ldw1")
            ia, ig = (0, 2) if k % 2 == 0 else (1, 3)
            for c in range(8):
                for half in range(2):
                    hsl = slice(1 + 1024 * th + 512 * half, 1 + 1024 * th + 512 * half + 512)
                    p.mm(g.bank(ia, half)[0:aw, :], wa[:, c, 0:aw], g.HF[:, c, hsl],
                         c == 0, False, reads=wres(lo, lo + 1024) + [f"HF{c}"],
                         writes=[psr(ia, half)], tag="w1a")
                    p.mm(g.bank(ig, half)[0:aw, :], wg[:, c, 0:aw], g.HF[:, c, hsl],
                         c == 0, c == 7, reads=wres(lo + 1024, lo + 2048) + [f"HF{c}"],
                         writes=[psr(ig, half)], tag="w1g")
            r = 0 if k < 11 else 1
            for half in range(2):
                p.mm(g.bank(ia, half)[0:aw, :],
                     b1hl[0:2, 1408 * r + 128 * (k - 11 * r):1408 * r + 128 * (k - 11 * r) + aw],
                     g.SMB[0:2, 10:14, :].rearrange("p a b -> p (a b)"), False, True,
                     reads=["MCst", "SMBw"], writes=[psr(ia, half)], tag="b1mm")
            rseg = slice(0, 1024) if k % 2 == 0 else slice(1024, 2048)
            rres = ["RS0", "RS1"] if k % 2 == 0 else ["RS2", "RS3"]
            p.act(g.RSF[:, rseg][0:aw, :], g.ps[ig][0:aw, :], AF.Gelu,
                  bias=g.SMALL[0:aw, 8, k:k + 1], reads=[psr(ig, 0), psr(ig, 1), "SMa8"],
                  writes=rres, tag="gelu")
            p.tt(v.MC[0:aw, k, :], g.ps[ia][0:aw, :], g.RSF[0:aw, rseg], ALU.mult,
                 reads=[psr(ia, 0), psr(ia, 1)] + rres, writes=["MC"],
                 tag="mtt")
        for d in range(8):
            lo = 0 if d % 2 == 0 else 3072
            wd = wbq(g, lo, lo + 2816).rearrange("p (k j) -> p k j", k=22)
            p.dma("gpsimd", wd, getattr(g, f"w2P{l}")[d],
                  wres(lo, lo + 2816), wres(lo, lo + 2816), tag="ldw2")
            ybanks = (g.ps[2] if d % 2 == 0 else g.ps[3])
            yres = [psr(2, 0), psr(2, 1)] if d % 2 == 0 else [psr(3, 0), psr(3, 1)]
            for k in range(22):
                kw = min(128, 2730 - 128 * k)
                for half in range(2):
                    p.mm(ybanks[:, 512 * half:512 * half + 512], wd[0:kw, k, :],
                         v.MC[0:kw, k, 512 * half:512 * half + 512], k == 0, False,
                         reads=wres(lo, lo + 2816) + ["MC"], writes=yres, tag="w2")
            for half in range(2):
                p.mm(ybanks[:, 512 * half:512 * half + 512],
                     b2hl[0:2, 128 * d:128 * d + 128], g.SMB[0:2, 10:14, :].rearrange("p a b -> p (a b)"),
                     False, True, reads=["MCst", "SMBw"], writes=yres, tag="b2mm")
            p.tt(g.XT[:, d, tok], ybanks[:, :], g.XT[:, d, tok], ALU.add,
                 reads=yres + [f"XT{d}"], writes=[f"XT{d}"], tag="w2res")


def final_phase(p, g):
    v = pool_ffn(g)
    for d in range(8):
        p.dma("sync", g.RSF, g.x1spill[:, S * d:S * d + S],
              ["RS0", "RS1", "RS2", "RS3"], ["RS0", "RS1", "RS2", "RS3"],
              tag="unspill")
        p.tt(g.XT[:, d, 1:S + 1], g.XT[:, d, 1:S + 1], g.RSF, ALU.add,
             reads=[f"XT{d}", "RS0", "RS1", "RS2", "RS3"], writes=[f"XT{d}"],
             tag="addx1")
    ln_compute(p, g, False, v.ST[0:1, 0:2048], "MCst", "fl")
    Am, Bm, Ar, Br = AB_aps(g)
    for d in range(8):
        for t4 in range(4):
            sl = slice(1 + 512 * t4, 1 + 512 * t4 + 512)
            p.tt(g.XT[:, d, sl], g.XT[:, d, sl], Am[t4], ALU.mult,
                 reads=[f"XT{d}", Ar[t4]], writes=[f"XT{d}"], tag="fap1")
            p.tt(g.XT[:, d, sl], g.XT[:, d, sl], Bm[t4], ALU.add,
                 reads=[f"XT{d}", Br[t4]], writes=[f"XT{d}"], tag="fap2")
        p.dma("sync", g.outT[128 * d:128 * d + 128, :], g.XT[:, d, 1:S + 1],
              [f"XT{d}"], [f"out{d}"], tag="stout")


def dump_xt(p, g):
    p.dma("sync", g.dumpT, g.XT[:, :, :].rearrange("p a b -> p (a b)"),
          [f"XT{d}" for d in range(8)] + ["XTpad"], [], tag="dumpXT")
    p.dma("sync", g.dumpHF, g.HF[:, :, :].rearrange("p a b -> p (a b)"),
          [f"HF{d}" for d in range(8)] + ["HFpad"], [], tag="dumpHF")


def build_main(nc, p, g, cfg):
    stage = cfg.get("stages", "all")
    ds = cfg.get("dump_stage")
    p.memset(g.XT[:, :, 0:1], 0.0, writes=["XTpad"], tag="xtpad")
    p.memset(g.HF[:, :, 0:1], 0.0, writes=["HFpad"], tag="hfpad")
    p.dma("sync", g.XT[:, :, 1:S + 1], g.xT.rearrange("(a p) t -> p a t", p=128),
          (), [f"XT{d}" for d in range(8)], tag="ldx")
    ld_tables(p, g)
    for d in range(8):
        p.copy(g.HF[:, d, 1:S + 1], g.XT[:, d, 1:S + 1], reads=[f"XT{d}"],
               writes=[f"HF{d}"], tag="hf0")
    vff = pool_ffn(g)
    bo_bf = vff.ST[:, 0:1024]
    bohl = stage_bias_hilo(p, g, g.boH, bo_bf, "MCst", 512, "bo")
    want = lambda s: stage == "all" or s in stage
    if want("dense"):
        dense_phase(p, g, bohl)
        if ds == "x2":
            return dump_xt(p, g)
        p.fence()
        outer_ln(p, g, spill=True, stage_bf=vff.ST, stage_res="MCst", tag="ln1")
        if ds == "x1":
            return dump_xt(p, g)
    for l in (0, 1):
        if want(f"loc{l}"):
            p.fence()
            local_phase(p, g, l)
            if ds == f"xa{l}":
                return dump_xt(p, g)
        if want(f"ffn{l}"):
            p.fence()
            ffn_phase(p, g, l)
            if ds == f"xb{l}":
                return dump_xt(p, g)
    if want("final"):
        p.fence()
        final_phase(p, g)
    else:
        for d in range(8):
            p.dma("sync", g.outT[128 * d:128 * d + 128, :], g.XT[:, d, 1:S + 1],
                  [f"XT{d}"], [f"out{d}"], tag="stout0")


def kernel(**inputs):
    import os
    cfg = {}
    abl = os.environ.get("ABL", "")
    if abl:
        import json as _json
        cfg = _json.loads(abl)
        if "stages" in cfg:
            cfg["stages"] = set(cfg["stages"])
    cores = prepack(inputs)
    nc = build_program(cfg)
    res = run_bass_kernel_spmd(nc, cores, core_ids=list(range(8)))
    return np.stack([np.asarray(r["outT"], np.float32).T for r in res.results], axis=0)


# version 18 (final)
# speedup vs baseline: 1.0238x; 1.0091x over previous
import numpy as np
import concourse.bass as bass
import concourse.mybir as mybir
from concourse.bass_utils import run_bass_kernel_spmd
from contextlib import ExitStack

F32 = mybir.dt.float32
BF16 = mybir.dt.bfloat16
U8 = mybir.dt.uint8
AF = mybir.ActivationFunctionType
ALU = mybir.AluOpType
AX = mybir.AxisListType

S = 2048
DM = 1024
ENGINES = ("tensor", "vector", "scalar", "gpsimd", "sync")


DMA_WIN = 16


class Prog:
    def __init__(self, nc):
        self.nc = nc
        self.ops = []
        self.res = {}
        self.fence_deps = set()
        self.dmas = {}

    def op(self, engine, fn, reads=(), writes=(), dma=False, tag=""):
        deps = set(self.fence_deps)
        for r in reads:
            e = self.res.get(r)
            if e and e["w"] is not None:
                deps.add(e["w"])
            if e and r.startswith("PS"):
                for ri in e["r"]:
                    if self.ops[ri]["engine"] != engine:
                        deps.add(ri)
        for w in writes:
            e = self.res.get(w)
            if e:
                if e["w"] is not None:
                    deps.add(e["w"])
                deps.update(e["r"])
        i = len(self.ops)
        if dma:
            ql = self.dmas.setdefault(engine, [])
            if len(ql) >= DMA_WIN:
                deps.add(ql[-DMA_WIN])
            ql.append(i)
        pruned = set()
        for d in deps:
            dop = self.ops[d]
            if (engine == "tensor" and dop["engine"] == "tensor"
                    and not dma and not dop["dma"]):
                continue
            pruned.add(d)
        self.ops.append(dict(engine=engine, fn=fn, dma=dma, deps=pruned, tag=tag))
        for r in reads:
            self.res.setdefault(r, {"w": None, "r": []})["r"].append(i)
        for w in writes:
            self.res[w] = {"w": i, "r": []}
        return i

    def fence(self):
        tails = set()
        for e in self.res.values():
            if e["w"] is not None:
                tails.add(e["w"])
            tails.update(e["r"])
        self.fence_deps = tails

    def finalize(self, stack, last_engine="sync"):
        nc = self.nc
        n = len(self.ops)
        needed = set()
        for o in self.ops:
            needed.update(o["deps"])
        cnt = {}
        tok = [None] * n
        qidx = {}
        for i, o in enumerate(self.ops):
            if o["dma"]:
                j = qidx.get(o["engine"], 0)
                qidx[o["engine"]] = j + 1
                k = ("q", o["engine"], j % DMA_WIN)
                cnt[k] = cnt.get(k, 0) + 16
                tok[i] = (k, cnt[k])
            elif i in needed:
                k = ("e", o["engine"])
                cnt[k] = cnt.get(k, 0) + 1
                tok[i] = (k, cnt[k])
        issue_clock = [None] * n
        stream_clock = {e: {} for e in ENGINES}
        waits = [None] * n
        for i, o in enumerate(self.ops):
            sc = stream_clock[o["engine"]]
            need = {}
            for d in sorted(o["deps"]):
                k, v = tok[d]
                if sc.get(k, 0) < v and need.get(k, 0) < v:
                    need[k] = v
            waits[i] = need
            for d in sorted(o["deps"]):
                k, v = tok[d]
                if sc.get(k, 0) < v:
                    sc[k] = v
                ic = issue_clock[d]
                if ic:
                    for kk, vv in ic.items():
                        if sc.get(kk, 0) < vv:
                            sc[kk] = vv
            issue_clock[i] = dict(sc)
        sems = {k: stack.enter_context(nc.semaphore("_".join(str(x) for x in k)))
                for k in cnt}
        block = stack.enter_context(nc.Block())
        self.n_ops = n

        def emit_for(engine):
            def run(eng):
                for i, o in enumerate(self.ops):
                    if o["engine"] != engine:
                        continue
                    for k, v in sorted(waits[i].items()):
                        eng.wait_ge(sems[k], v)
                    instr = o["fn"](eng)
                    if tok[i] is not None:
                        k, v = tok[i]
                        instr.then_inc(sems[k], 16 if o["dma"] else 1)
                if engine == last_engine:
                    for k, v in sorted(cnt.items()):
                        eng.wait_ge(sems[k], v)
            return run

        for e in ENGINES:
            getattr(block, e)(emit_for(e))

    def mm(self, out, lhsT, rhs, start, stop, reads, writes, tag=""):
        def fn(eng, out=out, lhsT=lhsT, rhs=rhs, start=start, stop=stop):
            return eng.matmul(out, lhsT, rhs, start=start, stop=stop)
        return self.op("tensor", fn, reads, writes, tag=tag)

    def tt(self, out, a, b, alu, reads, writes, engine="vector", tag=""):
        def fn(eng, out=out, a=a, b=b, alu=alu):
            return eng.tensor_tensor(out, a, b, op=alu)
        return self.op(engine, fn, reads, writes, tag=tag)

    def ts(self, out, a, s1, op0, reads, writes, s2=None, op1=None, engine="vector", tag=""):
        def fn(eng, out=out, a=a, s1=s1, s2=s2, op0=op0, op1=op1):
            if s2 is None:
                return eng.tensor_scalar(out, a, s1, None, op0=op0)
            return eng.tensor_scalar(out, a, s1, s2, op0=op0, op1=op1)
        return self.op(engine, fn, reads, writes, tag=tag)

    def act(self, out, in_, func, reads, writes, bias=None, scale=None, accum=None, tag=""):
        def fn(eng, out=out, in_=in_, func=func, bias=bias, scale=scale, accum=accum):
            kw = {}
            if bias is not None:
                kw["bias"] = bias
            if scale is not None:
                kw["scale"] = scale
            if accum is not None:
                kw["accum_out"] = accum
            return eng.activation(out, in_, func, **kw)
        return self.op("scalar", fn, reads, writes, tag=tag)

    def copy(self, out, in_, reads, writes, engine="vector", tag=""):
        if engine == "scalar":
            def fn(eng, out=out, in_=in_):
                return eng.copy(out, in_)
        else:
            def fn(eng, out=out, in_=in_):
                return eng.tensor_copy(out, in_)
        return self.op(engine, fn, reads, writes, tag=tag)

    def red(self, out, in_, alu, reads, writes, absv=False, tag=""):
        def fn(eng, out=out, in_=in_, alu=alu, absv=absv):
            return eng.tensor_reduce(out, in_, axis=AX.X, op=alu,
                                     apply_absolute_value=absv or None)
        return self.op("vector", fn, reads, writes, tag=tag)

    def recip(self, out, in_, reads, writes, tag=""):
        def fn(eng, out=out, in_=in_):
            return eng.reciprocal(out, in_)
        return self.op("vector", fn, reads, writes, tag=tag)

    def memset(self, ap, val, writes, engine="vector", tag=""):
        def fn(eng, ap=ap, val=val):
            return eng.memset(ap, val)
        return self.op(engine, fn, (), writes, tag=tag)

    def dma(self, queue, out, in_, reads, writes, tag=""):
        def fn(eng, out=out, in_=in_):
            return eng.dma_start(out=out, in_=in_)
        return self.op(queue, fn, reads, writes, dma=True, tag=tag)


def make_trig():
    d = np.arange(128) % 64
    i = (d % 32).astype(np.float64)
    t = np.arange(S, dtype=np.float64)
    inv_freq = 1.0 / (10000.0 ** ((2 * i) / 64.0))
    freqs = inv_freq[:, None] * t[None, :]
    base = ((2 * i) + 0.4 * 64) / (1.4 * 64)
    power = (t - S // 2) / 256.0
    scale = base[:, None] ** power[None, :]
    sign = np.where(d < 32, -1.0, 1.0)[:, None]
    return (np.cos(freqs) * scale, np.sin(freqs) * scale * sign,
            np.cos(freqs) / scale, np.sin(freqs) / scale * sign)


def prepack(inputs):
    f32 = np.float32
    x = np.asarray(inputs["L0_attn_x"], f32)
    Wq = np.asarray(inputs["L0_attn_Wq"], f32)
    Wk = np.asarray(inputs["L0_attn_Wk"], f32)
    Wv = np.asarray(inputs["L0_attn_Wv"], f32)
    Wo = np.asarray(inputs["L0_attn_Wo"], f32)
    bo = np.asarray(inputs["L0_attn_bo"], f32)
    rel = np.asarray(inputs["L0_attn_rel_emb"], f32)
    sh = {}
    sh["wqT"] = np.ascontiguousarray(
        Wq.reshape(8, 128, 4, 2, 128).transpose(2, 1, 0, 3, 4))
    sh["wkT"] = np.ascontiguousarray(
        Wk.reshape(8, 128, 4, 2, 128).transpose(2, 1, 0, 3, 4))
    sh["wvP"] = np.ascontiguousarray(
        Wv.reshape(8, 128, 4, 256).transpose(2, 1, 0, 3))
    sh["woP"] = np.ascontiguousarray(Wo.reshape(16, 64, 8, 128).transpose(1, 0, 2, 3))
    sh["boH"] = bo.reshape(2, 512)
    sh["relH"] = rel.reshape(1, 17)
    p = np.arange(128)[:, None]
    c = np.arange(144)[None, :]
    sh["GdH"] = rel[np.clip(c - p, 0, 16)].astype(f32)
    sh["U01dH"] = (c[:, 0:128] >= p).astype(f32)
    pp = np.arange(64)[:, None]
    cc = np.arange(16)[None, :]
    sh["GpH"] = rel[np.clip(cc + 64 - pp, 0, 16)].astype(f32)
    sh["r16H"] = np.broadcast_to(rel[16:17], (128, 1)).astype(f32).copy()
    c2 = np.arange(128)[None, :]
    sh["UcH"] = (c2 >= p).astype(f32)
    sh["UwH"] = (c2 <= p).astype(f32)
    hs = np.zeros((128, 2, 4), f32)
    for pr in range(128):
        for r in range(2):
            hs[pr, r, 2 * r + pr // 64] = 1.0
    sh["HSH"] = hs
    sel4 = np.zeros((4, 128), f32)
    sel4[0, 0:64] = 1; sel4[1, 0:64] = 1; sel4[2, 64:128] = 1; sel4[3, 64:128] = 1
    sh["SEL4H"] = sel4
    csq, ssq, csk, ssk = make_trig()
    sh["CSqH"] = csq.astype(f32); sh["SSqH"] = ssq.astype(f32)
    sh["CSkH"] = csk.astype(f32); sh["SSkH"] = ssk.astype(f32)

    wqkv = np.asarray(inputs["L0_ffn_wqkv"], f32)
    wol = np.asarray(inputs["L0_ffn_wo"], f32)
    w1 = np.asarray(inputs["L0_ffn_w1"], f32)
    w2 = np.asarray(inputs["L0_ffn_w2"], f32)
    b1 = np.asarray(inputs["L0_ffn_b1"], f32)
    b2 = np.asarray(inputs["L0_ffn_b2"], f32)
    lg = np.asarray(inputs["L0_ffn_ln_g"], f32)
    lb = np.asarray(inputs["L0_ffn_ln_b"], f32)
    qs = np.asarray(inputs["L0_ffn_q_scale"], f32)
    ks = np.asarray(inputs["L0_ffn_k_scale"], f32)
    perm = ((np.arange(128) % 64 + 32) % 64) + 64 * (np.arange(128) // 64)
    for l in (0, 1):
        wq3 = wqkv[l].reshape(8, 128, 3, 2, 256)
        sh[f"wqkvP{l}"] = np.ascontiguousarray(wq3.transpose(2, 3, 1, 0, 4))
        sh[f"wolP{l}"] = np.ascontiguousarray(wol[l].reshape(8, 64, 8, 128).transpose(1, 0, 2, 3))
        w1p = np.zeros((22, 128, 8, 2, 128), f32)
        wa_ = w1[l][:, 0:2730]
        wg_ = w1[l][:, 2730:5460]
        for k in range(22):
            awk = min(128, 2730 - 128 * k)
            w1p[k, :, :, 0, 0:awk] = wa_[:, 128 * k:128 * k + awk].reshape(8, 128, awk).transpose(1, 0, 2)
            w1p[k, :, :, 1, 0:awk] = wg_[:, 128 * k:128 * k + awk].reshape(8, 128, awk).transpose(1, 0, 2)
        sh[f"w1P{l}"] = np.ascontiguousarray(w1p)
        w2p = np.zeros((22 * 128, 1024), f32)
        w2p[0:2730] = w2[l]
        sh[f"w2P{l}"] = np.ascontiguousarray(
            w2p.reshape(22, 128, 8, 128).transpose(2, 1, 0, 3))
        b1a = np.zeros((2, 1408), f32)
        b1a.reshape(-1)[0:2730] = b1[l, 0:2730]
        sh[f"b1aH{l}"] = b1a
        b1g = np.zeros((128, 22), f32)
        for k in range(22):
            seg = b1[l, 2730 + 128 * k: 2730 + min(128 * (k + 1), 2730)]
            b1g[0:len(seg), k] = seg
        sh[f"b1gH{l}"] = b1g
        sh[f"b2H{l}"] = b2[l].reshape(2, 512)
        sh[f"lgH{l}"] = np.ascontiguousarray(lg[l].reshape(8, 128).T)
        sh[f"lbH{l}"] = np.ascontiguousarray(lb[l].reshape(8, 128).T)
        sh[f"qsH{l}"] = np.tile(qs[l], 2).reshape(128, 1).copy()
        sh[f"qspH{l}"] = np.tile(qs[l], 2)[perm].reshape(128, 1).copy()
        sh[f"ksH{l}"] = np.tile(ks[l], 2).reshape(128, 1).copy()
        sh[f"kspH{l}"] = np.tile(ks[l], 2)[perm].reshape(128, 1).copy()
    cores = []
    for b in range(8):
        d = dict(sh)
        d["xT"] = np.ascontiguousarray(x[b].T)
        cores.append(d)
    return cores


class G:
    pass


def psr(i, half):
    return f"PS{i}{'ab'[half]}"


def build_program(cfg):
    nc = bass.Bass("TRN2", target_bir_lowering=False)
    g = G()
    g.cfg = cfg

    def dram(name, shape, dtype=F32, kind="ExternalInput"):
        setattr(g, name, nc.dram_tensor(name, shape, dtype, kind=kind).ap())

    dram("xT", [DM, S])
    dram("wqT", [4, 128, 8, 2, 128]); dram("wkT", [4, 128, 8, 2, 128])
    dram("wvP", [4, 128, 8, 256]); dram("woP", [64, 16, 8, 128])
    dram("boH", [2, 512]); dram("relH", [1, 17])
    dram("GdH", [128, 144]); dram("U01dH", [128, 128]); dram("GpH", [64, 16])
    dram("r16H", [128, 1]); dram("UcH", [128, 128]); dram("UwH", [128, 128])
    dram("HSH", [128, 2, 4]); dram("SEL4H", [4, 128])
    dram("CSqH", [128, S]); dram("SSqH", [128, S])
    dram("CSkH", [128, S]); dram("SSkH", [128, S])
    for l in (0, 1):
        dram(f"wqkvP{l}", [3, 2, 128, 8, 256]); dram(f"wolP{l}", [64, 8, 8, 128])
        dram(f"w1P{l}", [22, 128, 8, 2, 128]); dram(f"w2P{l}", [8, 128, 22, 128])
        dram(f"b1aH{l}", [2, 1408]); dram(f"b1gH{l}", [128, 22]); dram(f"b2H{l}", [2, 512])
        dram(f"lgH{l}", [128, 8]); dram(f"lbH{l}", [128, 8])
        dram(f"qsH{l}", [128, 1]); dram(f"qspH{l}", [128, 1])
        dram(f"ksH{l}", [128, 1]); dram(f"kspH{l}", [128, 1])
    dram("x1spill", [128, 8 * S], kind="Internal")
    dram("outT", [DM, S], kind="ExternalOutput")
    g.dump_stage = cfg.get("dump_stage")
    if g.dump_stage:
        dram("dumpT", [128, 8 * (S + 1)], kind="ExternalOutput")
        dram("dumpHF", [128, 8 * (S + 1)], BF16, kind="ExternalOutput")
    if cfg.get("dump_qk"):
        dram("dumpQ", [128, 2, S], BF16, kind="ExternalOutput")
        dram("dumpK", [128, 2, S], BF16, kind="ExternalOutput")
        dram("dumpV", [128, 16, 260], BF16, kind="ExternalOutput")

    st = ExitStack()
    en = st.enter_context

    def sb(name, shape, dtype):
        setattr(g, name, en(nc.sbuf_tensor(name, shape, dtype)))

    sb("XT", [128, 8, S + 1], F32)
    sb("HF", [128, 8, S + 1], BF16)
    sb("SQT", [128, S], BF16)
    sb("RS", [128, 4, 512], F32)
    sb("SMALL", [128, 16, 32], F32)
    sb("SMB", [128, 16, 128], BF16)
    sb("TBL", [128, 208], F32)
    sb("WBQ", [128, 6144], BF16)
    sb("WB3", [64, 4, 8, 128], BF16)
    sb("POOL", [128, 57472], U8)
    g.ps = [en(nc.psum_tensor(f"PS{i}", [128, 1024], F32)) for i in range(4)]
    g.bank = lambda i, half: g.ps[i][:, 512 * half:512 * half + 512]
    g.RSF = g.RS.ap().rearrange("p a b -> p (a b)")

    p = Prog(nc)
    g.p = p
    build_main(nc, p, g, cfg)
    p.finalize(st)
    st.close()
    return nc


WR = {"A": (0, 1536), "B": (1536, 3072), "C": (3072, 4608), "D": (4608, 6144)}


def wbq(g, lo, hi):
    return g.WBQ[:, lo:hi]


def wres(lo, hi):
    out = []
    for nm, (a, b) in WR.items():
        if lo < b and hi > a:
            out.append("WBQ" + nm)
    return out


def pool_attn(g, local):
    v = G()
    P = g.POOL
    def bf(off, n):
        return P[:, off:off + 2 * n].bitcast(BF16)
    v.QB = bf(0, 4096).rearrange("p (m t) -> p m t", m=2)
    v.KB = bf(8192, 4096).rearrange("p (m t) -> p m t", m=2)
    if local:
        v.PERM = bf(16384, 2048)
        v.VB = bf(20480, 4160).rearrange("p (k j) -> p k j", k=16)
        v.CTX = bf(28800, 2048).rearrange("p (h t) -> p h t", h=4)[0:64]
        v.AT = bf(32896, 2048)
        v.AT2 = bf(36992, 2048)
        v.CS1 = P[:, 41088:41088 + 8192].bitcast(F32)
        v.CS2 = P[:, 49280:49280 + 8192].bitcast(F32)
        v.RHL = v.PERM
        v.RHL_res = "PERM"
    else:
        v.VB = bf(16384, 4160).rearrange("p (k j) -> p k j", k=16)
        v.CTX = bf(24704, 2048).rearrange("p (h t) -> p h t", h=4)[0:64]
        v.AT = bf(28800, 2048)
        v.AT2 = bf(32896, 2048)
        v.RHL = bf(36992, 2048)
        v.RHL_res = "DRHL"
    return v


def pool_ffn(g):
    v = G()
    v.MC = g.POOL[:, 0:45056].bitcast(BF16).rearrange("p (k t) -> p k t", k=22)
    v.ST = g.POOL[:, 45056:45056 + 8192].bitcast(BF16)
    return v


def ld_tables(p, g):
    p.memset(g.SMB[:, 0, :], 1.0, writes=["SMB0"], tag="ones")
    p.memset(g.SMB[:, 10:14, :], 1.0, writes=["SMBw"], tag="ones512")
    p.dma("sync", g.TBL[:, 0:144], g.GdH, (), ["TBL0"], tag="Gd")
    p.dma("sync", g.TBL[0:64, 144:160], g.GpH, (), ["TBL1"], tag="Gp")
    p.dma("sync", g.TBL[0:1, 160:177], g.relH, (), ["TBL2"], tag="rel")
    p.dma("sync", g.TBL[:, 177:178], g.r16H, (), ["TBL3"], tag="r16")
    p.dma("gpsimd", g.SMB[:, 1, :], g.UcH, (), ["SMB1"], tag="Uc")
    p.dma("gpsimd", g.SMB[:, 2, :], g.UwH, (), ["SMB2"], tag="Uw")
    p.dma("gpsimd", g.SMB[:, 3, :], g.U01dH, (), ["SMB3"], tag="U01d")
    p.dma("gpsimd", g.SMB[:, 4, 0:8].rearrange("p (a b) -> p a b", a=2), g.HSH,
          (), ["SMB4"], tag="HS")
    p.ts(g.SMALL[:, 0, 0:1], g.TBL[:, 177:178], 0.125, ALU.mult,
         reads=["TBL3"], writes=["SMa0"], tag="r16s")
    p.red(g.SMALL[0:1, 1, 0:1], g.TBL[0:1, 160:177], ALU.max, reads=["TBL2"],
          writes=["SMa1"], absv=True, tag="relmax")
    p.copy(g.SMB[0:1, 6, 0:1], g.SMALL[0:1, 1, 0:1], reads=["SMa1"], writes=["SMB6"],
           tag="relmaxb")
    p.mm(g.bank(3, 0)[0:4, 0:1], g.SMB[0:1, 0, 0:4], g.SMB[0:1, 6, 0:1], True, True,
         reads=["SMB0", "SMB6"], writes=[psr(3, 0)], tag="relmm")
    p.copy(g.SMALL[0:4, 1, 1:2], g.bank(3, 0)[0:4, 0:1], reads=[psr(3, 0)],
           writes=["SMa1b"], tag="relmax4")


def stage_bias_hilo(p, g, srcH, dst_bf, dst_res, nseg, tag):
    rsb = g.RS.ap().rearrange("p a b -> p (a b)").bitcast(BF16)
    rsf = g.RSF
    RSALL = ["RS0", "RS1", "RS2", "RS3"]
    for s_ in range((nseg + 511) // 512):
        a, b = 512 * s_, min(512 * s_ + 512, nseg)
        w = b - a
        p.dma("sync", rsf[0:2, 0:w], srcH[0:2, a:b], RSALL, RSALL, tag=tag + "ld")
        p.copy(rsb[0:2, 1024:1024 + w], rsf[0:2, 0:w], reads=RSALL, writes=RSALL,
               tag=tag + "hi")
        p.copy(g.bank(3, 0)[0:2, 0:w], rsb[0:2, 1024:1024 + w], reads=RSALL,
               writes=[psr(3, 0)], tag=tag + "hf")
        p.tt(rsf[0:2, 0:w], rsf[0:2, 0:w], g.bank(3, 0)[0:2, 0:w], ALU.subtract,
             reads=RSALL + [psr(3, 0)], writes=RSALL, tag=tag + "lo")
        p.copy(rsb[0:2, 1536:1536 + w], rsf[0:2, 0:w], reads=RSALL, writes=RSALL,
               tag=tag + "lb")
        p.dma("sync", dst_bf[0:1, 0:2 * nseg].rearrange("p (r c) -> p r c", r=2)[:, :, a:b],
              rsb[0:2, 1024:1024 + w], RSALL, [dst_res], tag=tag + "chi")
        p.dma("sync", dst_bf[1:2, 0:2 * nseg].rearrange("p (r c) -> p r c", r=2)[:, :, a:b],
              rsb[0:2, 1536:1536 + w], RSALL, [dst_res], tag=tag + "clo")
    return dst_bf[0:2, 0:2 * nseg]


def bias_bounds(p, g, QB, KB, mode, tag):
    for which, buf, res in (("q", QB, "QB"), ("k", KB, "KB")):
        for m in range(2):
            p.tt(g.SQT[:, :], buf[:, m, :], buf[:, m, :], ALU.mult,
                 reads=[res], writes=["SQT"], tag=tag + "sq")
            for t4 in range(4):
                i, half = 2 + t4 // 2, t4 % 2
                p.mm(g.bank(i, half)[0:4, :],
                     g.SMB[:, 4, 0:8].rearrange("p (a b) -> p a b", a=2)[:, m, :],
                     g.SQT[:, 512 * t4:512 * t4 + 512], m == 0, m == 1,
                     reads=["SQT", "SMB4"], writes=[psr(i, half)], tag=tag + "n2")
        for t4 in range(4):
            i, half = 2 + t4 // 2, t4 % 2
            col = t4 if which == "q" else 4 + t4
            p.red(g.SMALL[0:4, 2, col:col + 1], g.bank(i, half)[0:4, :], ALU.max,
                  reads=[psr(i, half)], writes=[f"n2m{col}"], tag=tag + "n2m")
    p.red(g.SMALL[0:4, 2, 8:9], g.SMALL[0:4, 2, 4:8], ALU.max,
          reads=[f"n2m{c}" for c in (4, 5, 6, 7)], writes=["n2kg"], tag=tag + "kg")
    p.ts(g.SMALL[0:4, 3, 0:4], g.SMALL[0:4, 2, 0:4], g.SMALL[0:4, 2, 8:9], ALU.mult,
         reads=[f"n2m{c}" for c in (0, 1, 2, 3)] + ["n2kg"], writes=["bb"],
         tag=tag + "bb")
    p.act(g.SMALL[0:4, 3, 4:8], g.SMALL[0:4, 3, 0:4], AF.Ln, reads=["bb"],
          writes=["bb1"], tag=tag + "ln")
    p.act(g.SMALL[0:4, 3, 8:12], g.SMALL[0:4, 3, 4:8], AF.Exp, scale=0.5,
          reads=["bb1"], writes=["bb2"], tag=tag + "ex")
    if mode == "dense":
        p.ts(g.SMALL[0:4, 3, 12:16], g.SMALL[0:4, 3, 8:12], g.SMALL[0:4, 1, 1:2],
             ALU.mult, reads=["bb2", "SMa1b"], writes=["negb"], s2=-0.125,
             op1=ALU.mult, tag=tag + "nb")
    else:
        p.ts(g.SMALL[0:4, 3, 12:16], g.SMALL[0:4, 3, 8:12], -8.0, ALU.mult,
             reads=["bb2"], writes=["negb"], tag=tag + "nb")
    p.copy(g.SMB[0:4, 7, 0:4], g.SMALL[0:4, 3, 12:16], reads=["negb"],
           writes=["SMB7"], tag=tag + "nbb")
    p.dma("sync", g.SMB[0:1, 8, 0:16].rearrange("p (a b) -> p a b", a=4),
          g.SMB[0:4, 7, 0:4], ["SMB7"], ["SMB8"], tag=tag + "nbrow")
    p.mm(g.bank(3, 0)[:, 0:16], g.SMB[0:1, 0, :], g.SMB[0:1, 8, 0:16], True, True,
         reads=["SMB0", "SMB8"], writes=[psr(3, 0)], tag=tag + "bm")
    p.copy(g.SMALL[:, 4, 0:16], g.bank(3, 0)[:, 0:16], reads=[psr(3, 0)],
           writes=["BIAS"], tag=tag + "bev")


def attn_group(p, g, v, lay):
    nm = lay["tag"]
    dense = lay["dense"]
    nq0t = g.cfg.get("nq0t", 4)
    nh = g.cfg.get("nheads", 4)
    do_wo = g.cfg.get("do_wo", True)
    CTXB = [(1, 0), (1, 1), (3, 0)]
    queue = []

    def deadline(item):
        kind, gq, _ = item
        if kind == "norm":
            return gq + 2
        return (gq + 1) * nh + 2

    def do_flush(cur):
        while queue:
            if cur is not None and cur < min(deadline(it) for it in queue):
                break
            queue.pop(0)[2]()

    for q0t in range(nq0t):
        q0 = 512 * q0t
        chunks = lay["chunks"](q0t)
        nch = len(chunks)
        for hh in range(nh):
            m, base = hh // 2, 64 * (hh % 2)
            biasap = lay["bias_ap"](g, hh, q0t)
            gidx = q0t * nh + hh
            ci_bank = CTXB[gidx % 3]
            ctx_ps = g.bank(*ci_bank)[0:65, :]

            def emit_S(ci):
                kc, cmin, cmax, kind, tcol = chunks[ci]
                it = 0 if (ci // 2) % 2 == 0 else 2
                Sps = g.ps[it][:, 512 * (ci % 2):512 * (ci % 2) + 512]
                lhs = v.KB[base:base + 64, m, 128 * kc:128 * kc + 128]
                rhs = v.QB[base:base + 64, m, q0 + cmin:q0 + cmax]
                p.mm(Sps[:, cmin:cmax], lhs, rhs, True, True,
                     reads=["QB", "KB"], writes=[psr(it, ci % 2)], tag=nm + "S")

            def fixes(ci, aseg, ares, it):
                kc, cmin, cmax, kind, tcol = chunks[ci]
                Sps = g.ps[it][:, 512 * (ci % 2):512 * (ci % 2) + 512]
                sres = psr(it, ci % 2)
                if kind == "diag":
                    w = min(144, cmax - tcol)
                    p.tt(g.RS[:, 0, 0:w], Sps[:, tcol:tcol + w], g.TBL[:, 0:w],
                         ALU.mult, reads=[sres, "TBL0"], writes=["RS0"],
                         tag=nm + "bG")
                    p.act(aseg[:, tcol:tcol + w], g.RS[:, 0, 0:w], AF.Exp,
                          bias=biasap, scale=0.125, reads=["RS0", lay["bias_res"]],
                          writes=[ares], tag=nm + "bE")
                    wu = min(w, 128)
                    p.tt(aseg[:, tcol:tcol + wu], aseg[:, tcol:tcol + wu],
                         g.SMB[:, 3, 0:wu], ALU.mult, reads=[ares, "SMB3"],
                         writes=[ares], tag=nm + "bU")
                elif kind == "corner":
                    p.tt(g.RS[64:128, 0, 0:16], Sps[64:128, 0:16],
                         g.TBL[0:64, 144:160], ALU.mult,
                         reads=[sres, "TBL1"], writes=["RS0"], tag=nm + "cG")
                    p.act(aseg[64:128, 0:16], g.RS[64:128, 0, 0:16], AF.Exp,
                          bias=biasap[64:128, :], scale=0.125,
                          reads=["RS0", lay["bias_res"]], writes=[ares],
                          tag=nm + "cE")
                elif kind == "tri_causal":
                    p.tt(aseg[:, tcol:tcol + 128], aseg[:, tcol:tcol + 128],
                         g.SMB[:, 1, :], ALU.mult, reads=[ares, "SMB1"],
                         writes=[ares], tag=nm + "tc")
                elif kind == "tri_window":
                    p.tt(aseg[:, tcol:tcol + 128], aseg[:, tcol:tcol + 128],
                         g.SMB[:, 2, :], ALU.mult, reads=[ares, "SMB2"],
                         writes=[ares], tag=nm + "tw")

            def emit_av(bb):
                a0, a1 = 2 * bb, 2 * bb + 1
                ab2 = 1024 * (bb % 2)
                for ci in ([a0] if a1 >= nch else [a0, a1]):
                    kc, cmin, cmax, kind, tcol = chunks[ci]
                    p.mm(ctx_ps[:, cmin:cmax], v.VB[:, kc, 65 * hh:65 * hh + 65],
                         v.AT[:, ab2 + 512 * (ci % 2) + cmin:ab2 + 512 * (ci % 2) + cmax],
                         ci == 0, ci == nch - 1,
                         reads=[f"ATs{2 * (bb % 2) + (ci % 2)}", "VB"],
                         writes=[psr(*ci_bank)], tag=nm + "AV")

            npair = (nch + 1) // 2
            emit_S(0)
            if nch > 1:
                emit_S(1)
            for b in range(npair):
                c0 = 2 * b
                c1 = 2 * b + 1 if 2 * b + 1 < nch else None
                it = 0 if b % 2 == 0 else 2
                if 2 * b + 2 < nch:
                    emit_S(2 * b + 2)
                    if 2 * b + 3 < nch:
                        emit_S(2 * b + 3)
                if b == 0:
                    do_flush(gidx)
                cmin0 = chunks[c0][1]
                hi = 1024 if c1 is not None else 512
                abase = 1024 * (b % 2)
                ares0, ares1 = f"ATs{2 * (b % 2)}", f"ATs{2 * (b % 2) + 1}"
                p.act(v.AT[:, abase + cmin0:abase + hi],
                      g.ps[it][:, cmin0:hi], AF.Exp,
                      bias=biasap, scale=lay["scale"](g),
                      reads=[psr(it, 0), psr(it, 1), lay["bias_res"]] + lay["scale_res"],
                      writes=[ares0, ares1], tag=nm + "exp")
                fixes(c0, v.AT[:, abase:abase + 512], ares0, it)
                if c1 is not None:
                    fixes(c1, v.AT[:, abase + 512:abase + 1024], ares1, it)
                if b > 0:
                    emit_av(b - 1)
            emit_av(npair - 1)
            if g.cfg.get("skip_norm"):
                p.copy(v.CTX[0:64, hh, :], ctx_ps[0:64, :], reads=[psr(*ci_bank)],
                       writes=[f"CTXn{hh}"], tag=nm + "cp")
                continue
            p.copy(g.RS[64:65, 1, 0:512], ctx_ps[64:65, :], reads=[psr(*ci_bank)],
                   writes=["sumrow"], tag=nm + "sum")
            r = gidx % 2
            cS = 1024 * r
            p.ts(g.RS[64:65, 2, 0:512], g.RS[64:65, 1, 0:512], 1e-30, ALU.max,
                 reads=["sumrow"], writes=["RS2"], tag=nm + "smx")
            p.act(g.RS[64:65, 2, 0:512], g.RS[64:65, 2, 0:512], AF.Ln,
                  reads=["RS2"], writes=["RS2"], tag=nm + "rln")
            p.act(g.RS[64:65, 3, 0:512], g.RS[64:65, 2, 0:512], AF.Exp, scale=-1.0,
                  reads=["RS2"], writes=["RS3"], tag=nm + "rcp")
            p.copy(g.SQT[64:65, cS:cS + 512], g.RS[64:65, 3, 0:512],
                   reads=["RS3"], writes=[f"SQTr{r}"], tag=nm + "rhi")
            p.copy(g.RS[64:65, 2, 0:512], g.SQT[64:65, cS:cS + 512],
                   reads=[f"SQTr{r}"], writes=["RS2"], tag=nm + "rhf")
            p.tt(g.RS[64:65, 3, 0:512], g.RS[64:65, 3, 0:512],
                 g.RS[64:65, 2, 0:512], ALU.subtract,
                 reads=["RS3", "RS2"], writes=["RS3"], tag=nm + "rlo")
            p.copy(g.SQT[64:65, cS + 512:cS + 1024], g.RS[64:65, 3, 0:512],
                   reads=["RS3"], writes=[f"SQTr{r}"], tag=nm + "rlb")

            def make_norm(h2=hh, r=r, cS=cS, q0t=q0t):
                cb = CTXB[(q0t * nh + h2) % 3]

                def run():
                    p.mm(g.bank(3, 1)[0:64, :], g.SMB[64:65, 0, 0:64],
                         g.SQT[64:65, cS:cS + 512], True, False,
                         reads=[f"SQTr{r}", "SMB0"], writes=[psr(3, 1)],
                         tag=nm + "rm")
                    p.mm(g.bank(3, 1)[0:64, :], g.SMB[64:65, 0, 0:64],
                         g.SQT[64:65, cS + 512:cS + 1024], False, True,
                         reads=[f"SQTr{r}", "SMB0"], writes=[psr(3, 1)],
                         tag=nm + "rm")
                    p.copy(g.RS[0:64, 1, 0:512], g.bank(3, 1)[0:64, :],
                           reads=[psr(3, 1)], writes=["rme"], tag=nm + "rme")
                    p.tt(v.CTX[0:64, h2, :], g.bank(*cb)[0:64, :],
                         g.RS[0:64, 1, 0:512], ALU.mult,
                         reads=[psr(*cb), "rme"], writes=[f"CTXn{h2}"],
                         tag=nm + "cn")
                return run
            queue.append(("norm", gidx, make_norm()))
        if not do_wo:
            continue

        def make_wo(q0=q0, nm=nm, q0t=q0t):
            s_late = ((q0t + 1) * nh + 2) % 3
            banks = [(3, 1), CTXB[s_late]]

            def run():
                for d in range(8):
                    tb = banks[d % 2]
                    yb = g.bank(*tb)
                    yres = psr(*tb)
                    last_h = not (dense and lay["bo"])
                    for hh2 in range(nh):
                        p.mm(yb, lay["wo_ap"](d, hh2), v.CTX[0:64, hh2, :], hh2 == 0,
                             (hh2 == nh - 1) and last_h,
                             reads=[lay["wo_res"], f"CTXn{hh2}"],
                             writes=[yres], tag=nm + "wo")
                    if dense and lay["bo"]:
                        p.mm(yb, lay["bohl"][0:2, 128 * d:128 * d + 128],
                             g.SMB[0:2, 10:14, :].rearrange("p a b -> p (a b)"),
                             False, True,
                             reads=[lay["bohl_res"], "SMBw"], writes=[yres],
                             tag=nm + "bo")
                    p.tt(g.XT[:, d, 1 + q0:1 + q0 + 512], yb,
                         g.XT[:, d, 1 + q0:1 + q0 + 512], ALU.add,
                         reads=[yres, f"XT{d}"], writes=[f"XT{d}"], tag=nm + "res")
            return run

        queue.append(("wo", q0t, make_wo()))
    do_flush(None)


def dense_chunks(q0t):
    nch = 4 * q0t + 4
    plains, fixy = [], []
    for kc in range(nch):
        j = kc - (nch - 4)
        if j >= 0:
            fixy.append((kc, 128 * j, 512, "diag", 128 * j))
        elif kc == nch - 5:
            fixy.append((kc, 0, 512, "corner", 0))
        else:
            plains.append((kc, 0, 512, "plain", 0))
    return plains[:1] + fixy + plains[1:]


def local_chunks(q0t):
    out = []
    for jj in [0, -4, -3, -2, -1, 1, 2, 3]:
        kc = 4 * q0t + jj
        if kc < 0:
            continue
        cmin = max(0, 128 * jj)
        cmax = min(512, 128 * jj + 640)
        if jj <= -1:
            out.append((kc, cmin, cmax, "tri_window", cmax - 128))
        else:
            out.append((kc, cmin, cmax, "tri_causal", cmin))
    return out


def dense_phase(p, g, bohl_ap):
    v = pool_attn(g, local=False)
    p.memset(g.ps[0][:, :], 0.0, writes=[psr(0, 0), psr(0, 1)], tag="ps0z")
    p.memset(g.ps[2][:, :], 0.0, writes=[psr(2, 0), psr(2, 1)], tag="ps2z")
    parts = g.cfg.get("dense_parts", {"proj", "bias", "attn"})
    for grp in range(g.cfg.get("dense_groups", 4)):
        wq = wbq(g, 0, 2048).rearrange("p (c m j) -> p c m j", c=8, m=2)
        wk = wbq(g, 2048, 4096).rearrange("p (c m j) -> p c m j", c=8, m=2)
        wv = wbq(g, 4096, 6144).rearrange("p (c j) -> p c j", c=8)
        p.dma("gpsimd", wq, g.wqT[grp], wres(0, 2048), wres(0, 2048), tag="ldwq")
        p.dma("gpsimd", wk, g.wkT[grp], wres(2048, 4096), wres(2048, 4096), tag="ldwk")
        p.dma("gpsimd", wv, g.wvP[grp], wres(4096, 6144), wres(4096, 6144), tag="ldwv")
        p.dma("gpsimd", g.WB3[:, :, :, :], g.woP[:, 4 * grp:4 * grp + 4, :, :],
              ["WB3"], ["WB3"], tag="ldwo")
        for m in range(2):
            for t4 in range(4):
                sl = slice(1 + 512 * t4, 1 + 512 * t4 + 512)
                bq, bk = g.bank(0, t4 % 2), g.bank(1, t4 % 2)
                for c in range(8):
                    p.mm(bq, wq[:, c, m, :], g.HF[:, c, sl], c == 0, c == 7,
                         reads=wres(0, 2048) + [f"HF{c}"], writes=[psr(0, t4 % 2)],
                         tag="qprj")
                    p.mm(bk, wk[:, c, m, :], g.HF[:, c, sl], c == 0, c == 7,
                         reads=wres(2048, 4096) + [f"HF{c}"], writes=[psr(1, t4 % 2)],
                         tag="kprj")
                p.copy(v.QB[:, m, 512 * t4:512 * t4 + 512], bq, reads=[psr(0, t4 % 2)],
                       writes=["QB"], engine="scalar", tag="qev")
                p.copy(v.KB[:, m, 512 * t4:512 * t4 + 512], bk, reads=[psr(1, t4 % 2)],
                       writes=["KB"], engine="scalar", tag="kev")
        p.memset(v.VB[:, :, :].rearrange("p k (h j) -> p k h j", h=4)[:, :, :, 64:65],
                 1.0, writes=["VB"], tag="vones")
        for tt in range(16):
            bv = g.bank(2, tt % 2)
            off = 1 + 128 * tt
            for c in range(8):
                p.mm(bv[:, 0:256], g.HF[:, c, off:off + 128], wv[:, c, :],
                     c == 0, c == 7, reads=wres(4096, 6144) + [f"HF{c}"],
                     writes=[psr(2, tt % 2)], tag="vprj")
            p.copy(v.VB[:, tt, :].rearrange("p (h j) -> p h j", h=4)[:, :, 0:64],
                   bv[:, 0:256].rearrange("p (h j) -> p h j", h=4), reads=[psr(2, tt % 2)],
                   writes=["VB"], engine="scalar", tag="vev")
        if "bias" not in parts:
            continue
        bias_bounds(p, g, v.QB, v.KB, "dense", f"db{grp}")
        if "attn" not in parts:
            continue
        lay = dict(tag=f"dA{grp}", dense=True, scale=lambda g: g.SMALL[:, 0, 0:1],
                   scale_res=["SMa0"],
                   bias_ap=lambda g, hh, q0t: g.SMALL[:, 4, 4 * hh + q0t:4 * hh + q0t + 1],
                   bias_res="BIAS", chunks=dense_chunks,
                   wo_ap=lambda d, hh: g.WB3[0:64, hh, d, :], wo_res="WB3",
                   bo=(grp == 0), bohl=bohl_ap, bohl_res="MCst")
        attn_group(p, g, v, lay)


def ln_compute(p, g, shifted, stage_bf, stage_res, tag):
    RSALL = ["RS0", "RS1", "RS2", "RS3"]
    rsb = g.RS.ap().rearrange("p a b -> p (a b)").bitcast(BF16)
    for d in range(8):
        p.copy(g.HF[:, d, 1:S + 1], g.XT[:, d, 1:S + 1], reads=[f"XT{d}"],
               writes=[f"HF{d}"], tag=tag + "hf")
    for d in range(8):
        off = 1 if (not shifted or d < 4) else 0
        rd = [f"HF{d}", "HFpad"] if shifted else [f"HF{d}"]
        sqb, sqr = (g.SQT[:, :], "SQT") if d % 2 == 0 else (rsb[:, 0:2048], "SQB")
        wl = [sqr] if d != 1 else [sqr] + RSALL + ["rme", "sumrow", "RS2", "RS3"]
        p.tt(sqb, g.HF[:, d, off:off + S], g.HF[:, d, off:off + S],
             ALU.mult, reads=rd, writes=wl, tag=tag + "sq")
        for t4 in range(4):
            p.mm(g.bank(t4 // 2, t4 % 2)[0:1, :], g.SMB[:, 0, 0:1],
                 g.HF[:, d, off + 512 * t4:off + 512 * t4 + 512], d == 0, d == 7,
                 reads=rd + ["SMB0"], writes=[psr(t4 // 2, t4 % 2)], tag=tag + "sum")
            p.mm(g.bank(2 + t4 // 2, t4 % 2)[0:1, :], g.SMB[:, 0, 0:1],
                 sqb[:, 512 * t4:512 * t4 + 512], d == 0, d == 7,
                 reads=[sqr, "SMB0"], writes=[psr(2 + t4 // 2, t4 % 2)],
                 tag=tag + "sqs")
    r0 = g.RSF[0:1, :]
    for h in range(2):
        for j in range(2):
            t4 = 2 * h + j
            p.copy(r0[:, 512 * j:512 * j + 512], g.bank(t4 // 2, t4 % 2)[0:1, :],
                   reads=[psr(t4 // 2, t4 % 2)], writes=RSALL + ["SQB"],
                   tag=tag + "se")
            p.copy(r0[:, 1024 + 512 * j:1024 + 512 * j + 512],
                   g.bank(2 + t4 // 2, t4 % 2)[0:1, :],
                   reads=[psr(2 + t4 // 2, t4 % 2)], writes=RSALL + ["SQB"],
                   tag=tag + "qe")
        mu = r0[:, 0:1024]
        ms = r0[:, 1024:2048]
        p.ts(mu, mu, 1.0 / DM, ALU.mult, reads=RSALL, writes=RSALL, tag=tag + "mu")
        p.ts(ms, ms, 1.0 / DM, ALU.mult, reads=RSALL, writes=RSALL, tag=tag + "ms")
        for j in range(2):
            tb = g.bank(0, 0)[0:1, :]
            p.tt(tb, mu[:, 512 * j:512 * j + 512], mu[:, 512 * j:512 * j + 512],
                 ALU.mult, reads=RSALL, writes=[psr(0, 0)], tag=tag + "m2")
            p.tt(ms[:, 512 * j:512 * j + 512], ms[:, 512 * j:512 * j + 512], tb,
                 ALU.subtract, reads=RSALL + [psr(0, 0)], writes=RSALL,
                 tag=tag + "var")
        p.ts(ms, ms, 1e-5, ALU.add, reads=RSALL, writes=RSALL, tag=tag + "eps")
        p.act(ms, ms, AF.Ln, reads=RSALL, writes=RSALL, tag=tag + "ln")
        p.act(ms, ms, AF.Exp, scale=-0.5, reads=RSALL, writes=RSALL, tag=tag + "rs")
        p.tt(mu, mu, ms, ALU.mult, reads=RSALL, writes=RSALL, tag=tag + "b0")
        p.ts(mu, mu, -1.0, ALU.mult, reads=RSALL, writes=RSALL, tag=tag + "b1")
        hc = slice(1024 * h, 1024 * h + 1024)
        p.copy(stage_bf[0:1, 0:1024], ms, reads=RSALL, writes=[stage_res],
               tag=tag + "ahi")
        p.dma("sync", g.SQT[32:33, hc], stage_bf[0:1, 0:1024], [stage_res],
              ["SQT"], tag=tag + "ahm")
        for j in range(2):
            tb = g.bank(0, 0)[0:1, :]
            p.copy(tb, stage_bf[0:1, 512 * j:512 * j + 512], reads=[stage_res],
                   writes=[psr(0, 0)], tag=tag + "ahf")
            p.tt(ms[:, 512 * j:512 * j + 512], ms[:, 512 * j:512 * j + 512], tb,
                 ALU.subtract, reads=RSALL + [psr(0, 0)], writes=RSALL,
                 tag=tag + "alo")
        p.copy(stage_bf[0:1, 0:1024], ms, reads=RSALL, writes=[stage_res],
               tag=tag + "alb")
        p.dma("sync", g.SQT[33:34, hc], stage_bf[0:1, 0:1024], [stage_res],
              ["SQT"], tag=tag + "alm")
        p.copy(g.SQT[0:1, hc], mu, reads=RSALL, writes=["SQT"], tag=tag + "bhi")
        for j in range(2):
            tb = g.bank(0, 0)[0:1, :]
            p.copy(tb, g.SQT[0:1, 1024 * h + 512 * j:1024 * h + 512 * j + 512],
                   reads=["SQT"], writes=[psr(0, 0)], tag=tag + "bhf")
            p.tt(mu[:, 512 * j:512 * j + 512], mu[:, 512 * j:512 * j + 512], tb,
                 ALU.subtract, reads=RSALL + [psr(0, 0)], writes=RSALL,
                 tag=tag + "blo")
        p.copy(stage_bf[0:1, 0:1024], mu, reads=RSALL, writes=[stage_res],
               tag=tag + "blb")
        p.dma("sync", g.SQT[1:2, hc], stage_bf[0:1, 0:1024], [stage_res],
              ["SQT"], tag=tag + "blm")
    for t4 in range(4):
        p.mm(g.bank(t4 // 2, t4 % 2), g.SMB[32:34, 0, :],
             g.SQT[32:34, 512 * t4:512 * t4 + 512], True, True,
             reads=["SQT", "SMB0"], writes=[psr(t4 // 2, t4 % 2)], tag=tag + "Am")
        p.mm(g.bank(2 + t4 // 2, t4 % 2), g.SMB[0:2, 0, :],
             g.SQT[0:2, 512 * t4:512 * t4 + 512], True, True,
             reads=["SQT", "SMB0"], writes=[psr(2 + t4 // 2, t4 % 2)], tag=tag + "Bm")


def AB_aps(g):
    Am = [g.bank(t4 // 2, t4 % 2) for t4 in range(4)]
    Bm = [g.bank(2 + t4 // 2, t4 % 2) for t4 in range(4)]
    Ar = [psr(t4 // 2, t4 % 2) for t4 in range(4)]
    Br = [psr(2 + t4 // 2, t4 % 2) for t4 in range(4)]
    return Am, Bm, Ar, Br


def outer_ln(p, g, spill, stage_bf, stage_res, tag):
    ln_compute(p, g, False, stage_bf[0:1, 0:2048], stage_res, tag)
    Am, Bm, Ar, Br = AB_aps(g)
    for d in range(8):
        for t4 in range(4):
            sl = slice(1 + 512 * t4, 1 + 512 * t4 + 512)
            p.tt(g.XT[:, d, sl], g.XT[:, d, sl], Am[t4], ALU.mult,
                 reads=[f"XT{d}", Ar[t4]], writes=[f"XT{d}"], tag=tag + "ap1")
            p.tt(g.XT[:, d, sl], g.XT[:, d, sl], Bm[t4], ALU.add,
                 reads=[f"XT{d}", Br[t4]], writes=[f"XT{d}"], tag=tag + "ap2")
        p.copy(g.HF[:, d, 1:S + 1], g.XT[:, d, 1:S + 1], reads=[f"XT{d}"],
               writes=[f"HF{d}"], tag=tag + "hf2")
        if spill:
            p.dma("sync", g.x1spill[:, S * d:S * d + S], g.XT[:, d, 1:S + 1],
                  [f"XT{d}"], [f"spill{d}"], tag=tag + "sp")


def local_qk_prep(p, g, v, l, grp, side):
    buf = v.QB if side == "q" else v.KB
    res = "QB" if side == "q" else "KB"
    base_col = (0 if side == "q" else 512) + 256 * grp
    wr_lo, wr_hi = (0, 2048) if side == "q" else (2048, 4096)
    scol0 = getattr(g, f"{'qs' if side == 'q' else 'ks'}H{l}")
    p.dma("sync", g.SMALL[:, 5, 3:4], scol0, ["SMa5c"], ["SMa5c"], tag=f"{side}qs0")
    p.tt(g.SMALL[:, 5, 2:3], g.SMALL[:, 5, 3:4], g.SMALL[:, 5, 3:4], ALU.mult,
         reads=["SMa5c"], writes=["SMa5b"], tag=f"{side}qs2")
    p.ts(g.SMB[:, 15, 0:8].rearrange("p (a b) -> p a b", a=2),
         g.SMB[:, 4, 0:8].rearrange("p (a b) -> p a b", a=2),
         g.SMALL[:, 5, 2:3], ALU.mult, reads=["SMB4", "SMa5b"], writes=["SMB15"],
         tag=f"{side}hsq")
    wsl = wbq(g, wr_lo, wr_hi).rearrange("p (c j) -> p c j", c=8)
    p.dma("gpsimd", wsl, getattr(g, f"wqkvP{l}")[0 if side == "q" else 1, grp],
          wres(wr_lo, wr_hi), wres(wr_lo, wr_hi), tag=f"ldw{side}")
    for m in range(2):
        for t4 in range(4):
            b = g.bank(0, t4 % 2)
            for c in range(8):
                off = 1 if c < 4 else 0
                p.mm(b, wsl[:, c, 128 * m:128 * m + 128],
                     g.HF[:, c, off + 512 * t4:off + 512 * t4 + 512], c == 0, c == 7,
                     reads=wres(wr_lo, wr_hi) + [f"HF{c}", "HFpad"],
                     writes=[psr(0, t4 % 2)], tag=f"{side}prj")
            p.copy(buf[:, m, 512 * t4:512 * t4 + 512], b, reads=[psr(0, t4 % 2)],
                   writes=[res], engine="scalar", tag=f"{side}ev")
    for m in range(2):
        p.tt(g.SQT[:, :], buf[:, m, :], buf[:, m, :], ALU.mult, reads=[res],
             writes=["SQT"], tag=f"{side}sq")
        for t4 in range(4):
            i, half = 2 + t4 // 2, t4 % 2
            p.mm(g.bank(i, half)[0:4, :],
                 g.SMB[:, 4, 0:8].rearrange("p (a b) -> p a b", a=2)[:, m, :],
                 g.SQT[:, 512 * t4:512 * t4 + 512], m == 0, m == 1,
                 reads=["SQT", "SMB4"], writes=[psr(i, half)], tag=f"{side}n2")
            p.mm(g.bank(t4 // 2, t4 % 2)[0:4, :],
                 g.SMB[:, 15, 0:8].rearrange("p (a b) -> p a b", a=2)[:, m, :],
                 g.SQT[:, 512 * t4:512 * t4 + 512], m == 0, m == 1,
                 reads=["SQT", "SMB15"], writes=[psr(t4 // 2, t4 % 2)],
                 tag=f"{side}n2w")
    rsv = g.RSF[0:4, :]
    for t4 in range(4):
        i, half = 2 + t4 // 2, t4 % 2
        p.ts(g.RS[0:4, t4, 0:512], g.bank(i, half)[0:4, :], 1e-24, ALU.max,
             reads=[psr(i, half)], writes=[f"RS{t4}"], tag=f"{side}mx")
    p.act(rsv, rsv, AF.Ln, reads=["RS0", "RS1", "RS2", "RS3"],
          writes=["RS0", "RS1", "RS2", "RS3"], tag=f"{side}ln")
    p.act(rsv, rsv, AF.Exp, scale=-0.5, reads=["RS0", "RS1", "RS2", "RS3"],
          writes=["RS0", "RS1", "RS2", "RS3"], tag=f"{side}ex")
    p.tt(v.PERM[0:4, :], rsv, rsv, ALU.mult, reads=["RS0", "RS1", "RS2", "RS3"],
         writes=["PERM"], tag=f"{side}rn2")
    c0 = 0 if side == "q" else 8
    for t4 in range(4):
        bk = g.bank(t4 // 2, t4 % 2)
        p.tt(bk[0:4, :], bk[0:4, :], v.PERM[0:4, 512 * t4:512 * t4 + 512], ALU.mult,
             reads=[psr(t4 // 2, t4 % 2), "PERM"], writes=[psr(t4 // 2, t4 % 2)],
             tag=f"{side}n2t")
        p.red(g.SMALL[0:4, 8, c0 + t4:c0 + t4 + 1], bk[0:4, :], ALU.max,
              reads=[psr(t4 // 2, t4 % 2)], writes=[f"n2t{side}{t4}"],
              tag=f"{side}n2m")
    p.red(g.SMALL[0:4, 8, c0 + 4:c0 + 5], g.SMALL[0:4, 8, c0:c0 + 4], ALU.max,
          reads=[f"n2t{side}{t}" for t in range(4)], writes=["n2gq" if side == "q" else "n2gk"],
          tag=f"{side}n2g")
    p.copy(g.SQT[0:4, :], rsv, reads=["RS0", "RS1", "RS2", "RS3"], writes=["SQT"],
           tag=f"{side}hi")
    for t4 in range(4):
        p.copy(g.bank(2 + t4 // 2, t4 % 2)[0:4, :], g.SQT[0:4, 512 * t4:512 * t4 + 512],
               reads=["SQT"], writes=[psr(2 + t4 // 2, t4 % 2)], tag=f"{side}hf")
        p.tt(g.RS[0:4, t4, 0:512], g.RS[0:4, t4, 0:512],
             g.bank(2 + t4 // 2, t4 % 2)[0:4, :], ALU.subtract,
             reads=[f"RS{t4}", psr(2 + t4 // 2, t4 % 2)], writes=[f"RS{t4}"],
             tag=f"{side}lo")
    p.copy(v.PERM[0:4, :], rsv, reads=["RS0", "RS1", "RS2", "RS3"], writes=["PERM"],
           tag=f"{side}lb")
    csH = g.CSqH if side == "q" else g.CSkH
    ssH = g.SSqH if side == "q" else g.SSkH
    scol = getattr(g, f"{'qs' if side == 'q' else 'ks'}H{l}")
    spcol = getattr(g, f"{'qsp' if side == 'q' else 'ksp'}H{l}")
    p.dma("sync", v.CS1, csH, ["CS1"], ["CS1"], tag=f"{side}cs")
    p.dma("sync", v.CS2, ssH, ["CS2"], ["CS2"], tag=f"{side}ss")
    p.dma("sync", g.SMALL[:, 5, 0:1], scol, ["SMa5"], ["SMa5"], tag=f"{side}qs")
    p.dma("sync", g.SMALL[:, 5, 1:2], spcol, ["SMa5"], ["SMa5"], tag=f"{side}qsp")
    p.ts(v.CS1, v.CS1, g.SMALL[:, 5, 0:1], ALU.mult, reads=["CS1", "SMa5"],
         writes=["CS1"], tag=f"{side}cf")
    p.ts(v.CS2, v.CS2, g.SMALL[:, 5, 1:2], ALU.mult, reads=["CS2", "SMa5"],
         writes=["CS2"], tag=f"{side}sf")
    for m in range(2):
        rb = 32 * m
        p.dma("sync", v.AT2[rb + 0:rb + 1, :], g.SQT[2 * m:2 * m + 1, :], ["SQT"],
              ["AT1"], tag=f"{side}p1")
        p.dma("sync", v.AT2[rb + 1:rb + 2, :], v.PERM[2 * m:2 * m + 1, :], ["PERM"],
              ["AT1"], tag=f"{side}p2")
        p.dma("sync", v.AT2[rb + 2:rb + 3, :], g.SQT[2 * m + 1:2 * m + 2, :], ["SQT"],
              ["AT1"], tag=f"{side}p3")
        p.dma("sync", v.AT2[rb + 3:rb + 4, :], v.PERM[2 * m + 1:2 * m + 2, :], ["PERM"],
              ["AT1"], tag=f"{side}p4")
    for m in range(2):
        rb = 32 * m
        for blk, src in ((0, 32), (32, 0), (64, 96), (96, 64)):
            p.dma("sync", v.PERM[blk:blk + 32, :], buf[src:src + 32, m, :],
                  [res], ["PERM"], tag=f"{side}pm")
        p.tt(g.RSF, buf[:, m, :], v.CS1, ALU.mult, reads=[res, "CS1"],
             writes=["RS0", "RS1", "RS2", "RS3"], tag=f"{side}u")
        for half in range(2):
            p.tt(g.ps[half][:, :], v.PERM[:, 1024 * half:1024 * half + 1024],
                 v.CS2[:, 1024 * half:1024 * half + 1024], ALU.mult,
                 reads=["PERM", "CS2"], writes=[psr(half, 0), psr(half, 1)],
                 tag=f"{side}u2")
            p.tt(g.RSF[:, 1024 * half:1024 * half + 1024],
                 g.RSF[:, 1024 * half:1024 * half + 1024],
                 g.ps[half][:, :], ALU.add,
                 reads=["RS0", "RS1", "RS2", "RS3", psr(half, 0), psr(half, 1)],
                 writes=["RS0", "RS1", "RS2", "RS3"], tag=f"{side}s")
        for t4 in range(4):
            p.mm(g.bank(3, 1), g.SMB[0:4, 9, :] if rb == 0 else g.SMB[32:36, 9, :],
                 v.AT2[rb:rb + 4, 512 * t4:512 * t4 + 512],
                 True, True, reads=["AT1", "SMB9"], writes=[psr(3, 1)],
                 tag=f"{side}rm")
            p.tt(buf[:, m, 512 * t4:512 * t4 + 512],
                 g.RSF[:, 512 * t4:512 * t4 + 512], g.bank(3, 1), ALU.mult,
                 reads=["RS0", "RS1", "RS2", "RS3", psr(3, 1)], writes=[res],
                 tag=f"{side}qh")


def local_bias(p, g, tag):
    p.ts(g.SMALL[0:4, 8, 13:14], g.SMALL[0:4, 8, 4:5], g.SMALL[0:4, 8, 12:13],
         ALU.mult, reads=["n2gq", "n2gk"], writes=["lbb"], tag=tag + "p")
    p.act(g.SMALL[0:4, 8, 14:15], g.SMALL[0:4, 8, 13:14], AF.Ln, reads=["lbb"],
          writes=["lbb1"], tag=tag + "ln")
    p.act(g.SMALL[0:4, 8, 15:16], g.SMALL[0:4, 8, 14:15], AF.Exp, scale=0.5,
          reads=["lbb1"], writes=["lbb2"], tag=tag + "ex")
    p.ts(g.SMALL[0:4, 8, 16:17], g.SMALL[0:4, 8, 15:16], -16.0, ALU.mult,
         reads=["lbb2"], writes=["lnegb"], tag=tag + "nb")
    p.copy(g.SMB[0:4, 7, 0:1], g.SMALL[0:4, 8, 16:17], reads=["lnegb"],
           writes=["SMB7"], tag=tag + "nbb")
    p.dma("sync", g.SMB[0:1, 8, 0:4].rearrange("p (a b) -> p a b", a=4),
          g.SMB[0:4, 7, 0:1], ["SMB7"], ["SMB8"], tag=tag + "nbr")
    p.mm(g.bank(3, 0)[:, 0:4], g.SMB[0:1, 0, :], g.SMB[0:1, 8, 0:4], True, True,
         reads=["SMB0", "SMB8"], writes=[psr(3, 0)], tag=tag + "bm")
    p.copy(g.SMALL[:, 9, 0:4], g.bank(3, 0)[:, 0:4], reads=[psr(3, 0)],
           writes=["BIASL"], tag=tag + "bev")


def local_phase(p, g, l):
    v = pool_attn(g, local=True)
    p.memset(g.ps[0][:, :], 0.0, writes=[psr(0, 0), psr(0, 1)], tag="ps0z")
    p.memset(g.ps[2][:, :], 0.0, writes=[psr(2, 0), psr(2, 1)], tag="ps2z")
    if l > 0:
        for d in range(8):
            p.copy(g.HF[:, d, 1:S + 1], g.XT[:, d, 1:S + 1], reads=[f"XT{d}"],
                   writes=[f"HF{d}"], tag=f"lhf{l}")
    p.dma("gpsimd", g.SMB[0:4, 9, :], g.SEL4H, ["SMB9"], ["SMB9"], tag="sel0")
    p.dma("gpsimd", g.SMB[32:36, 9, :], g.SEL4H, ["SMB9"], ["SMB9"], tag="sel1")
    for grp in range(2):
        local_qk_prep(p, g, v, l, grp, "q")
        local_qk_prep(p, g, v, l, grp, "k")
        wv = wbq(g, 4096, 6144).rearrange("p (c j) -> p c j", c=8)
        p.dma("gpsimd", wv, getattr(g, f"wqkvP{l}")[2, grp],
              wres(4096, 6144), wres(4096, 6144), tag="ldwvl")
        p.memset(v.VB[:, :, :].rearrange("p k (h j) -> p k h j", h=4)[:, :, :, 64:65],
                 1.0, writes=["VB"], tag="vones")
        for tt in range(16):
            bv = g.bank(2, tt % 2)
            for c in range(8):
                off = 1 if c < 4 else 0
                p.mm(bv[:, 0:256], g.HF[:, c, off + 128 * tt:off + 128 * tt + 128],
                     wv[:, c, :], c == 0, c == 7,
                     reads=wres(4096, 6144) + [f"HF{c}", "HFpad"],
                     writes=[psr(2, tt % 2)], tag="vprjl")
            p.copy(v.VB[:, tt, :].rearrange("p (h j) -> p h j", h=4)[:, :, 0:64],
                   bv[:, 0:256].rearrange("p (h j) -> p h j", h=4),
                   reads=[psr(2, tt % 2)], writes=["VB"], engine="scalar", tag="vevl")
        p.dma("gpsimd", g.WB3[:, :, :, :], getattr(g, f"wolP{l}")[:, 4 * grp:4 * grp + 4, :, :],
              ["WB3"], ["WB3"], tag="ldwol")
        if g.cfg.get("dump_qk") == (l, grp):
            p.dma("sync", g.dumpQ, v.QB[:, :, :], ["QB"], [], tag="dq")
            p.dma("sync", g.dumpK, v.KB[:, :, :], ["KB"], [], tag="dk")
            p.dma("sync", g.dumpV, v.VB[:, :, :], ["VB"], [], tag="dv")
        local_bias(p, g, f"lb{l}{grp}")
        lay = dict(tag=f"lA{l}{grp}", dense=False, scale=lambda g: 8.0,
                   scale_res=[],
                   bias_ap=lambda g, hh, q0t: g.SMALL[:, 9, hh:hh + 1],
                   bias_res="BIASL", chunks=local_chunks,
                   wo_ap=lambda d, hh: g.WB3[0:64, hh, d, :], wo_res="WB3",
                   bo=False)
        attn_group(p, g, v, lay)


def ffn_phase(p, g, l):
    v = pool_ffn(g)
    p.dma("sync", g.SMALL[:, 6, 0:8], getattr(g, f"lgH{l}"), ["SMa6"], ["SMa6"],
          tag="lg")
    p.dma("sync", g.SMALL[:, 7, 0:8], getattr(g, f"lbH{l}"), ["SMa7"], ["SMa7"],
          tag="lb")
    p.dma("sync", g.SMALL[:, 8, 0:22], getattr(g, f"b1gH{l}"), ["SMa8"], ["SMa8"],
          tag="b1g")
    ln_compute(p, g, True, v.ST[0:1, 0:2048], "MCst", f"il{l}")
    Am, Bm, Ar, Br = AB_aps(g)
    for d in range(8):
        off = 1 if d < 4 else 0
        for t4 in range(4):
            slx = slice(off + 512 * t4, off + 512 * t4 + 512)
            slh = slice(1 + 512 * t4, 1 + 512 * t4 + 512)
            p.tt(g.RSF[:, 0:512], g.XT[:, d, slx], Am[t4], ALU.mult,
                 reads=[f"XT{d}", Ar[t4]], writes=["RS0"], tag="iap1")
            p.tt(g.RSF[:, 0:512], g.RSF[:, 0:512], Bm[t4], ALU.add,
                 reads=["RS0", Br[t4]], writes=["RS0"], tag="iap2")
            p.ts(g.HF[:, d, slh], g.RSF[:, 0:512], g.SMALL[:, 6, d:d + 1], ALU.mult,
                 reads=["RS0", "SMa6", "SMa7"], writes=[f"HF{d}"],
                 s2=g.SMALL[:, 7, d:d + 1], op1=ALU.add, tag="iap3")
    b1hl = stage_bias_hilo(p, g, getattr(g, f"b1aH{l}"), v.ST[:, 0:2816], "MCst",
                           1408, f"b1a{l}")
    b2hl = stage_bias_hilo(p, g, getattr(g, f"b2H{l}"), v.ST[:, 2816:3840], "MCst",
                           512, f"b2{l}")
    for th in range(2):
        tok = slice(1 + 1024 * th, 1 + 1024 * th + 1024)
        for k in range(22):
            aw = min(128, 2730 - 128 * k)
            lo = 0 if k % 2 == 0 else 3072
            wag = wbq(g, lo, lo + 2048).rearrange("p (c x j) -> p c x j", c=8, x=2)
            wa = wag[:, :, 0, :]
            wg = wag[:, :, 1, :]
            p.dma("gpsimd", wag, getattr(g, f"w1P{l}")[k],
                  wres(lo, lo + 2048), wres(lo, lo + 2048), tag="ldw1")
            ia, ig = (0, 2) if k % 2 == 0 else (1, 3)
            for c in range(8):
                for half in range(2):
                    hsl = slice(1 + 1024 * th + 512 * half, 1 + 1024 * th + 512 * half + 512)
                    p.mm(g.bank(ia, half)[0:aw, :], wa[:, c, 0:aw], g.HF[:, c, hsl],
                         c == 0, False, reads=wres(lo, lo + 1024) + [f"HF{c}"],
                         writes=[psr(ia, half)], tag="w1a")
                    p.mm(g.bank(ig, half)[0:aw, :], wg[:, c, 0:aw], g.HF[:, c, hsl],
                         c == 0, c == 7, reads=wres(lo + 1024, lo + 2048) + [f"HF{c}"],
                         writes=[psr(ig, half)], tag="w1g")
            r = 0 if k < 11 else 1
            for half in range(2):
                p.mm(g.bank(ia, half)[0:aw, :],
                     b1hl[0:2, 1408 * r + 128 * (k - 11 * r):1408 * r + 128 * (k - 11 * r) + aw],
                     g.SMB[0:2, 10:14, :].rearrange("p a b -> p (a b)"), False, True,
                     reads=["MCst", "SMBw"], writes=[psr(ia, half)], tag="b1mm")
            rseg = slice(0, 1024) if k % 2 == 0 else slice(1024, 2048)
            rres = ["RS0", "RS1"] if k % 2 == 0 else ["RS2", "RS3"]
            p.act(g.RSF[:, rseg][0:aw, :], g.ps[ig][0:aw, :], AF.Gelu,
                  bias=g.SMALL[0:aw, 8, k:k + 1], reads=[psr(ig, 0), psr(ig, 1), "SMa8"],
                  writes=rres, tag="gelu")
            p.tt(v.MC[0:aw, k, :], g.ps[ia][0:aw, :], g.RSF[0:aw, rseg], ALU.mult,
                 reads=[psr(ia, 0), psr(ia, 1)] + rres, writes=["MC"],
                 tag="mtt")
        for d in range(8):
            lo = 0 if d % 2 == 0 else 3072
            wd = wbq(g, lo, lo + 2816).rearrange("p (k j) -> p k j", k=22)
            p.dma("gpsimd", wd, getattr(g, f"w2P{l}")[d],
                  wres(lo, lo + 2816), wres(lo, lo + 2816), tag="ldw2")
            ybanks = (g.ps[2] if d % 2 == 0 else g.ps[3])
            yres = [psr(2, 0), psr(2, 1)] if d % 2 == 0 else [psr(3, 0), psr(3, 1)]
            for k in range(22):
                kw = min(128, 2730 - 128 * k)
                for half in range(2):
                    p.mm(ybanks[:, 512 * half:512 * half + 512], wd[0:kw, k, :],
                         v.MC[0:kw, k, 512 * half:512 * half + 512], k == 0, False,
                         reads=wres(lo, lo + 2816) + ["MC"], writes=yres, tag="w2")
            for half in range(2):
                p.mm(ybanks[:, 512 * half:512 * half + 512],
                     b2hl[0:2, 128 * d:128 * d + 128], g.SMB[0:2, 10:14, :].rearrange("p a b -> p (a b)"),
                     False, True, reads=["MCst", "SMBw"], writes=yres, tag="b2mm")
            p.tt(g.XT[:, d, tok], ybanks[:, :], g.XT[:, d, tok], ALU.add,
                 reads=yres + [f"XT{d}"], writes=[f"XT{d}"], tag="w2res")


def final_phase(p, g):
    v = pool_ffn(g)
    for d in range(8):
        p.dma("sync", g.RSF, g.x1spill[:, S * d:S * d + S],
              ["RS0", "RS1", "RS2", "RS3"], ["RS0", "RS1", "RS2", "RS3"],
              tag="unspill")
        p.tt(g.XT[:, d, 1:S + 1], g.XT[:, d, 1:S + 1], g.RSF, ALU.add,
             reads=[f"XT{d}", "RS0", "RS1", "RS2", "RS3"], writes=[f"XT{d}"],
             tag="addx1")
    ln_compute(p, g, False, v.ST[0:1, 0:2048], "MCst", "fl")
    Am, Bm, Ar, Br = AB_aps(g)
    for d in range(8):
        for t4 in range(4):
            sl = slice(1 + 512 * t4, 1 + 512 * t4 + 512)
            p.tt(g.XT[:, d, sl], g.XT[:, d, sl], Am[t4], ALU.mult,
                 reads=[f"XT{d}", Ar[t4]], writes=[f"XT{d}"], tag="fap1")
            p.tt(g.XT[:, d, sl], g.XT[:, d, sl], Bm[t4], ALU.add,
                 reads=[f"XT{d}", Br[t4]], writes=[f"XT{d}"], tag="fap2")
        p.dma("sync", g.outT[128 * d:128 * d + 128, :], g.XT[:, d, 1:S + 1],
              [f"XT{d}"], [f"out{d}"], tag="stout")


def dump_xt(p, g):
    p.dma("sync", g.dumpT, g.XT[:, :, :].rearrange("p a b -> p (a b)"),
          [f"XT{d}" for d in range(8)] + ["XTpad"], [], tag="dumpXT")
    p.dma("sync", g.dumpHF, g.HF[:, :, :].rearrange("p a b -> p (a b)"),
          [f"HF{d}" for d in range(8)] + ["HFpad"], [], tag="dumpHF")


def build_main(nc, p, g, cfg):
    stage = cfg.get("stages", "all")
    ds = cfg.get("dump_stage")
    p.memset(g.XT[:, :, 0:1], 0.0, writes=["XTpad"], tag="xtpad")
    p.memset(g.HF[:, :, 0:1], 0.0, writes=["HFpad"], tag="hfpad")
    p.dma("sync", g.XT[:, :, 1:S + 1], g.xT.rearrange("(a p) t -> p a t", p=128),
          (), [f"XT{d}" for d in range(8)], tag="ldx")
    ld_tables(p, g)
    for d in range(8):
        p.copy(g.HF[:, d, 1:S + 1], g.XT[:, d, 1:S + 1], reads=[f"XT{d}"],
               writes=[f"HF{d}"], tag="hf0")
    vff = pool_ffn(g)
    bo_bf = vff.ST[:, 0:1024]
    bohl = stage_bias_hilo(p, g, g.boH, bo_bf, "MCst", 512, "bo")
    want = lambda s: stage == "all" or s in stage
    if want("dense"):
        dense_phase(p, g, bohl)
        if ds == "x2":
            return dump_xt(p, g)
        p.fence()
        outer_ln(p, g, spill=True, stage_bf=vff.ST, stage_res="MCst", tag="ln1")
        if ds == "x1":
            return dump_xt(p, g)
    for l in (0, 1):
        if want(f"loc{l}"):
            p.fence()
            local_phase(p, g, l)
            if ds == f"xa{l}":
                return dump_xt(p, g)
        if want(f"ffn{l}"):
            p.fence()
            ffn_phase(p, g, l)
            if ds == f"xb{l}":
                return dump_xt(p, g)
    if want("final"):
        p.fence()
        final_phase(p, g)
    else:
        for d in range(8):
            p.dma("sync", g.outT[128 * d:128 * d + 128, :], g.XT[:, d, 1:S + 1],
                  [f"XT{d}"], [f"out{d}"], tag="stout0")


def kernel(**inputs):
    import os
    cfg = {}
    abl = os.environ.get("ABL", "")
    if abl:
        import json as _json
        cfg = _json.loads(abl)
        if "stages" in cfg:
            cfg["stages"] = set(cfg["stages"])
    cores = prepack(inputs)
    nc = build_program(cfg)
    res = run_bass_kernel_spmd(nc, cores, core_ids=list(range(8)))
    return np.stack([np.asarray(r["outT"], np.float32).T for r in res.results], axis=0)
